# version 08
# baseline (speedup 1.0000x reference)
import numpy as np
from contextlib import contextmanager
import concourse.bass as bass
import concourse.mybir as mybir
from concourse.bass_utils import run_bass_kernel_spmd
from contextlib import ExitStack

F32 = mybir.dt.float32
BF16 = mybir.dt.bfloat16
I32 = mybir.dt.int32
A = mybir.AluOpType
AF = mybir.ActivationFunctionType
AX = mybir.AxisListType

B, S, D = 2, 4096, 2048
NCORE = 8
CHUNK = 1024
FT = D // 128
H0, ET0 = 8192, 64
H1, H1P = 5461, 5504
ET1 = H1P // 128
EPS_LN = 1e-5
SHIFT = 16.0

ENGINES = ("tensor", "vector", "scalar", "gpsimd", "sync")


class _Sem:
    __slots__ = ("handle", "count", "snapshots", "name")

    def __init__(self, handle, name):
        self.handle = handle
        self.count = 0
        self.name = name
        self.snapshots = {}


class Dep:
    __slots__ = ("sem", "value")

    def __init__(self, sem, value):
        self.sem = sem
        self.value = value


class Prog:

    def __init__(self, nc, ctx):
        self.nc = nc
        self.ctx = ctx
        self.ops = {e: [] for e in ENGINES}
        self.esem = {e: _Sem(ctx.enter_context(nc.semaphore(f"es_{e}")), f"es_{e}")
                     for e in ENGINES}
        self.streams = {}
        self.cc_sem = _Sem(ctx.enter_context(nc.semaphore("cc")), "cc")
        self.clock = {e: {} for e in ENGINES}
        self.last_writer = {}
        self.readers = {}
        self.n_waits = 0
        self.n_bumped = 0

    def _stream(self, name):
        if name not in self.streams:
            h = self.ctx.enter_context(self.nc.semaphore(f"ds_{name}"))
            self.streams[name] = _Sem(h, f"ds_{name}")
        return self.streams[name]

    def _wait(self, eng, dep):
        sem, val = dep.sem, dep.value
        if self.clock[eng].get(sem.name, 0) >= val:
            return
        if sem.name.startswith("ds_") and val < sem.count:
            val = sem.count
            self.n_bumped += 1
        snap = sem.snapshots.get(val)
        c = self.clock[eng]
        c[sem.name] = max(c.get(sem.name, 0), val)
        if snap:
            for k, v in snap.items():
                if c.get(k, 0) < v:
                    c[k] = v
        h = sem.handle
        self.ops[eng].append(lambda E, h=h, val=val: E.wait_ge(h, val))
        self.n_waits += 1

    def _resolve(self, eng, reads, writes):
        for dl in getattr(self, "barrier_deps", ()):
            for d in dl:
                self._wait(eng, d)
        skip_own = (eng == "tensor")
        own = f"es_{eng}"
        for r in reads:
            lw = self.last_writer.get(r)
            if lw is not None and not (skip_own and lw.sem.name == own):
                self._wait(eng, lw)
        for w in writes:
            lw = self.last_writer.get(w)
            if lw is not None and not (skip_own and lw.sem.name == own):
                self._wait(eng, lw)
            for rd in self.readers.get(w, ()):
                if not (skip_own and rd.sem.name == own):
                    self._wait(eng, rd)

    def _commit(self, dep, reads, writes):
        for w in writes:
            self.last_writer[w] = dep
            self.readers[w] = []
        for r in reads:
            self.readers.setdefault(r, []).append(dep)

    def op(self, eng, fn, reads=(), writes=()):
        reads, writes = list(reads), list(writes)
        self._resolve(eng, reads, writes)
        sem = self.esem[eng]
        sem.count += 1
        val, h = sem.count, sem.handle

        def run(E, fn=fn, h=h):
            inst = fn(E)
            assert inst is not None, "op fn must return its last instruction"
            inst.then_inc(h, 1)

        self.ops[eng].append(run)
        dep = Dep(sem, val)
        snap = dict(self.clock[eng])
        snap[sem.name] = val
        sem.snapshots[val] = snap
        self._commit(dep, reads, writes)
        return dep

    def dma(self, eng, stream, out, in_, reads=(), writes=()):
        reads, writes = list(reads), list(writes)
        self._resolve(eng, reads, writes)
        s = self._stream(stream)
        s.count += 16
        val, h = s.count, s.handle

        def run(E, out=out, in_=in_, h=h):
            E.dma_start(out=out, in_=in_).then_inc(h, 16)

        self.ops[eng].append(run)
        dep = Dep(s, val)
        s.snapshots[val] = dict(self.clock[eng])
        self._commit(dep, reads, writes)
        return dep

    def idma(self, stream, out, in_, idx_ap, reads=(), writes=()):
        reads, writes = list(reads), list(writes)
        self._resolve("gpsimd", reads, writes)
        s = self._stream(stream)
        s.count += 16
        val, h = s.count, s.handle

        def run(E, out=out, in_=in_, idx_ap=idx_ap, h=h):
            E.indirect_dma_start(
                out=out, out_offset=None, in_=in_,
                in_offset=bass.IndirectOffsetOnAxis(ap=idx_ap, axis=0),
            ).then_inc(h, 16)

        self.ops["gpsimd"].append(run)
        dep = Dep(s, val)
        s.snapshots[val] = dict(self.clock["gpsimd"])
        self._commit(dep, reads, writes)
        return dep

    def collective(self, kind, op_, groups, ins, outs, reads=(), writes=()):
        reads, writes = list(reads), list(writes)
        self._resolve("gpsimd", reads, writes)
        s = self.cc_sem
        s.count += 1
        val, h = s.count, s.handle

        def run(E, kind=kind, op_=op_, rg=groups, ins=ins, outs=outs, h=h):
            E.collective_compute(kind, op_, replica_groups=rg, ins=ins,
                                 outs=outs).then_inc(h, 1)

        self.ops["gpsimd"].append(run)
        dep = Dep(s, val)
        s.snapshots[val] = dict(self.clock["gpsimd"])
        self._commit(dep, reads, writes)
        return dep

    def fence_all(self):
        deps = list(self.last_writer.values())
        for e in ENGINES:
            for d in deps:
                self._wait(e, d)

    def barrier(self):
        deps = list(self.last_writer.values())
        for keydeps in self.readers.values():
            deps.extend(keydeps)
        self.barrier_deps = [deps]

    def finalize(self):
        block = self.ctx.enter_context(self.nc.Block())
        for e in ENGINES:
            ops = self.ops[e]
            if not ops:
                continue

            def body(E, ops=ops):
                for f in ops:
                    f(E)

            getattr(block, e)(body)
        return block


def build_program(upto=4):
    nc = bass.Bass("TRN2", target_bir_lowering=False)
    ctx = ExitStack()
    p = Prog(nc, ctx)
    sb = ctx.enter_context

    def din(name, shape, dt=F32):
        return nc.dram_tensor(name, list(shape), dt, kind="ExternalInput").ap()

    def dint(name, shape, dt=F32):
        return nc.dram_tensor(name, list(shape), dt, kind="Internal").ap()

    x_in = din("x_in", (D, CHUNK))
    negslopes = din("negslopes", (128, 4))
    zsel = din("zsel", (128, 4))
    qshard = din("qshard", (128, 81918))
    qrcp = din("qrcp", (1, 16))
    qeps = din("qeps", (1, 16))
    gidx = din("gidx", (128, 24), I32)
    oidx = din("oidx", (128, 32), I32)
    wcat_st = din("wcat_st", (48, 128, 2048))
    bmix = din("bmix", (128, 64))
    dbias_in = din("dbias", (128, 16))
    wout_st = din("wout_st", (16, 128, 2048))
    wu_st = din("wu_st", (64, 128, 2048))
    wg0_st = din("wg0_st", (64, 128, 2048))
    wd_st = din("wd_st", (16, 128, 8192))
    wq_st = din("wq_st", (16, 128, 2048))
    wk_st = din("wk_st", (4, 128, 2048))
    wv_st = din("wv_st", (4, 128, 2048))
    wo_st = din("wo_st", (16, 128, 2048))
    w1_st = din("w1_st", (43, 128, 2048))
    w3_st = din("w3_st", (43, 128, 2048))
    w2_st = din("w2_st", (16, 128, 5504))
    x_out = nc.dram_tensor("x_out", [D, CHUNK], F32, kind="ExternalOutput").ap()

    xhbm = [dint("xhbm0", (D, CHUNK)), dint("xhbm1", (D, CHUNK))]
    hhbm = dint("hhbm", (H0, CHUNK), BF16)
    qdh = dint("qdh", (16, 128, 8192), BF16)
    ags_in = dint("ags_in", (1, 16))
    ags_out = dint("ags_out", (8, 16))
    agz_in = dint("agz_in", (128, 32))
    agz_out = dint("agz_out", (512, 32))
    agkv_in = dint("agkv_in", (1024, CHUNK), BF16)
    agkv_out = dint("agkv_out", (4096, CHUNK), BF16)
    agq_in = dint("agq_in", (2048, CHUNK), BF16)
    agq_out = dint("agq_out", (8192, CHUNK), BF16)
    ago_in = dint("ago_in", (512, S), BF16)
    ago_out = dint("ago_out", (2048, S), BF16)
    agd_in = dint("agd_in", (4, S))
    agd_out = dint("agd_out", (16, S))

    ALL8 = [list(range(8))]
    BGRP = [[0, 1, 2, 3], [4, 5, 6, 7]]

    @contextmanager
    def scope():
        sx = ExitStack()
        try:
            yield sx
        finally:
            sx.close()
            p.barrier()

    xln = sb(nc.sbuf_tensor("xln", [128, FT * CHUNK], BF16))
    biasc = sb(nc.sbuf_tensor("biasc", [128, 64], F32))
    bdec = sb(nc.sbuf_tensor("bdec", [128, 16], F32))
    dbc = sb(nc.sbuf_tensor("dbc", [128, 16], F32))
    slopc = sb(nc.sbuf_tensor("slopc", [128, 4], F32))
    zselc = sb(nc.sbuf_tensor("zselc", [128, 4], F32))
    onescol = sb(nc.sbuf_tensor("onescol", [128, 1], BF16))
    shiftc = sb(nc.sbuf_tensor("shiftc", [128, 1], F32))

    scol = sb(nc.sbuf_tensor("scol", [128, 16], F32))
    rscol = sb(nc.sbuf_tensor("rscol", [128, 16], F32))
    statrow = sb(nc.sbuf_tensor("statrow", [1, 4 * CHUNK], F32))
    rrep = sb(nc.sbuf_tensor("rrep", [128, CHUNK], F32))
    murep = sb(nc.sbuf_tensor("murep", [128, CHUNK], F32))
    gidxs = sb(nc.sbuf_tensor("gidxs", [128, 24], I32))
    lnx = sb(nc.sbuf_tensor("lnx", [128, 4 * CHUNK], F32))
    lnsq = [sb(nc.sbuf_tensor(f"lnsq{b}", [128, CHUNK], BF16)) for b in range(2)]
    lntmp = sb(nc.sbuf_tensor("lntmp", [128, CHUNK], F32))
    oidxs = sb(nc.sbuf_tensor("oidxs", [128, 32], I32))

    ones1 = sb(nc.sbuf_tensor("ones1", [1, 128], F32))
    p.op("gpsimd", lambda E: E.memset(ones1[:], 1.0), writes=["ones1"])
    p.op("gpsimd", lambda E: E.memset(onescol[:], 1.0), writes=["onescol"])
    p.op("gpsimd", lambda E: E.memset(shiftc[:], -SHIFT), writes=["shiftc"])

    p.dma("sync", "cst", out=biasc[:], in_=bmix, writes=["biasc"])
    p.dma("sync", "cst", out=slopc[:], in_=negslopes, writes=["slopc"])
    p.dma("sync", "cst", out=zselc[:], in_=zsel, writes=["zselc"])
    p.dma("sync", "cst", out=dbc[:], in_=dbias_in, writes=["dbc"])
    p.dma("sync", "cst", out=gidxs[:], in_=gidx, writes=["gidxs"])
    p.dma("sync", "cst", out=oidxs[:], in_=oidx, writes=["oidxs"])
    p.op("vector", lambda E: E.tensor_add(out=bdec[:], in0=biasc[:, 16:32], in1=dbc[:]),
         reads=["biasc", "dbc"], writes=["bdec"])

    QN = [16384, 16384, 16384, 10922, 10922, 10922]
    QCNT = [float(H0 * D)] * 3 + [float(H1 * D)] * 3

    def phase_qscale():
        with scope() as sx:
            qs = sx.enter_context(nc.sbuf_tensor("q_s", [128, 16384], F32))
            part = sx.enter_context(nc.sbuf_tensor("q_part", [128, 16], F32))
            part6 = sx.enter_context(nc.sbuf_tensor("q_p6", [128, 8], F32))
            ssum = sx.enter_context(nc.sbuf_tensor("q_ssum", [1, 16], F32))
            ag8 = sx.enter_context(nc.sbuf_tensor("q_ag8", [8, 16], F32))
            o8 = sx.enter_context(nc.sbuf_tensor("q_o8", [128, 1], F32))
            qr = sx.enter_context(nc.sbuf_tensor("q_qr", [1, 16], F32))
            qe = sx.enter_context(nc.sbuf_tensor("q_qe", [1, 16], F32))
            srow = sx.enter_context(nc.sbuf_tensor("q_srow", [1, 16], F32))
            rs16 = sx.enter_context(nc.sbuf_tensor("q_rs16", [1, 16], F32))
            pp = sx.enter_context(nc.psum_tensor("q_pp", [128, 512], F32))
            p.op("gpsimd", lambda E: E.memset(ssum[:], 0.0), writes=["q_ssum"])
            p.op("gpsimd", lambda E: E.memset(o8[:], 1.0), writes=["q_o8"])
            p.dma("sync", "cst", out=qr[:], in_=qrcp, writes=["q_qr"])
            p.dma("sync", "cst", out=qe[:], in_=qeps, writes=["q_qe"])
            off = 0
            j = 0
            ranges = []
            for m, ncols in enumerate(QN):
                a0 = j
                done = 0
                while done < ncols:
                    n = min(8192, ncols - done)
                    lo = (j % 2) * 8192
                    p.dma("sync", f"q{j % 2}", out=qs[:, lo:lo + n],
                          in_=qshard[:, off:off + n], writes=[f"q_s:{j % 2}"])
                    p.op("vector",
                         lambda E, n=n, j=j, lo=lo: E.reduce_sum(
                             out=part[:, j:j + 1], in_=qs[:, lo:lo + n], axis=AX.X,
                             apply_absolute_value=True),
                         reads=[f"q_s:{j % 2}"], writes=[f"q_part:{j}"])
                    off += n
                    done += n
                    j += 1
                ranges.append((a0, j))
            for m, (a0, a1) in enumerate(ranges):
                p.op("vector",
                     lambda E, m=m, a0=a0, a1=a1: E.reduce_sum(
                         out=part6[:, m:m + 1], in_=part[:, a0:a1], axis=AX.X),
                     reads=[f"q_part:{jj}" for jj in range(a0, a1)], writes=["q_p6"])
            p.op("tensor",
                 lambda E: E.matmul(pp[0:1, 0:6], o8[:, 0:1], part6[:, 0:6],
                                    start=True, stop=True),
                 reads=["q_p6", "q_o8"], writes=["q_pp"])
            p.op("vector", lambda E: E.tensor_copy(out=ssum[0:1, 0:6], in_=pp[0:1, 0:6]),
                 reads=["q_pp"], writes=["q_ssum"])
            p.dma("gpsimd", "agb", out=ags_in, in_=ssum[:], reads=["q_ssum"],
                  writes=["ags_in_d"])
            p.collective("AllGather", A.bypass, ALL8, [ags_in], [ags_out],
                         reads=["ags_in_d"], writes=["ags_out_d"])
            p.dma("gpsimd", "agb", out=ag8[:], in_=ags_out, reads=["ags_out_d"],
                  writes=["q_ag8"])
            p.op("tensor",
                 lambda E: E.matmul(pp[0:1, 16:32], o8[0:8, 0:1], ag8[:, :],
                                    start=True, stop=True),
                 reads=["q_ag8", "q_o8"], writes=["q_pp"])
            p.op("vector", lambda E: E.tensor_mul(out=srow[:], in0=pp[0:1, 16:32], in1=qr[:]),
                 reads=["q_pp", "q_qr"], writes=["q_srow"])
            p.op("vector", lambda E: E.tensor_add(out=srow[:], in0=srow[:], in1=qe[:]),
                 reads=["q_srow", "q_qe"], writes=["q_srow"])
            p.op("vector", lambda E: E.reciprocal(out=rs16[:], in_=srow[:]),
                 reads=["q_srow"], writes=["q_rs16"])
            p.op("tensor",
                 lambda E: E.matmul(pp[:, 32:48], ones1[0:1, :], srow[0:1, :],
                                    start=True, stop=True),
                 reads=["q_srow", "ones1"], writes=["q_pp"])
            p.op("scalar", lambda E: E.copy(out=scol[:, :], in_=pp[:, 32:48]),
                 reads=["q_pp"], writes=["scol"])
            p.op("tensor",
                 lambda E: E.matmul(pp[:, 48:64], ones1[0:1, :], rs16[0:1, :],
                                    start=True, stop=True),
                 reads=["q_rs16", "ones1"], writes=["q_pp"])
            p.op("scalar", lambda E: E.copy(out=rscol[:, :], in_=pp[:, 48:64]),
                 reads=["q_pp"], writes=["rscol"])

    def replicate_row(ps_rep, ps_key, row_ap, row_key, dst, dst_key, n):
        for h0 in range(0, n, 512):
            w = min(512, n - h0)
            p.op("tensor",
                 lambda E, h0=h0, w=w: E.matmul(
                     ps_rep[:, 0:w], ones1[0:1, :], row_ap[0:1, h0:h0 + w],
                     start=True, stop=True),
                 reads=[row_key, "ones1"], writes=[ps_key])
            p.op("scalar",
                 lambda E, h0=h0, w=w: E.copy(out=dst[:, h0:h0 + w], in_=ps_rep[:, 0:w]),
                 reads=[ps_key], writes=[dst_key])

    def cast_chunk(dst, dst_key, src, src_keys, cols):
        p.op("scalar",
             lambda E: E.activation(out=dst[:, 0:cols], in_=src[:, 0:cols], func=AF.Copy),
             reads=src_keys, writes=[dst_key])

    def quant_chunk(dst, dst_key, src, src_keys, cols, mq, tmp, tmp_key):
        p.op("vector",
             lambda E: E.tensor_scalar(out=dst[:, 0:cols], in0=src[:, 0:cols],
                                       scalar1=rscol[:, mq:mq + 1], scalar2=0.5,
                                       op0=A.mult, op1=A.is_gt),
             reads=list(src_keys) + ["rscol"], writes=[dst_key])
        p.op("vector",
             lambda E: E.tensor_scalar(out=tmp[:, 0:cols], in0=src[:, 0:cols],
                                       scalar1=rscol[:, mq:mq + 1], scalar2=-0.5,
                                       op0=A.mult, op1=A.is_lt),
             reads=list(src_keys) + ["rscol"], writes=[tmp_key])
        p.op("vector",
             lambda E: E.tensor_sub(out=dst[:, 0:cols], in0=dst[:, 0:cols],
                                    in1=tmp[:, 0:cols]),
             reads=[dst_key, tmp_key], writes=[dst_key])

    lncnt = [0]

    def layernorm(x_src, xsrc_key, sx):
        i = lncnt[0]
        lncnt[0] += 1
        sq, tmp = lnsq, lntmp
        with scope() as px:
            psx = [px.enter_context(nc.psum_tensor(f"ln{i}_px{h}", [128, 512], F32))
                   for h in range(2)]
            psq = [px.enter_context(nc.psum_tensor(f"ln{i}_pq{h}", [128, 512], F32))
                   for h in range(2)]
            psr = px.enter_context(nc.psum_tensor(f"ln{i}_pr", [128, 512], F32))
            for f in range(FT):
                s4 = f % 4
                p.dma("scalar", f"lnx{f % 3}", out=lnx[:, s4 * CHUNK:(s4 + 1) * CHUNK],
                      in_=x_src[f * 128:(f + 1) * 128, :],
                      reads=([f"{xsrc_key}:{f}"] if xsrc_key else []),
                      writes=[f"lnx:{s4}"])
                p.op("vector",
                     lambda E, f=f, s4=s4: E.tensor_copy(out=xln[:, f * CHUNK:(f + 1) * CHUNK],
                                                         in_=lnx[:, s4 * CHUNK:(s4 + 1) * CHUNK]),
                     reads=[f"lnx:{s4}"], writes=[f"xln:{f}"])
                b = f % 2
                p.op("scalar",
                     lambda E, f=f, b=b: E.activation(
                         out=sq[b][:], in_=xln[:, f * CHUNK:(f + 1) * CHUNK],
                         func=AF.Square),
                     reads=[f"xln:{f}"], writes=[f"lnsq:{b}"])
                def statg(E, f=f, b=b):
                    last = None
                    for h in range(2):
                        E.matmul(psx[h][0:1, :], onescol[:, 0:1],
                                 xln[:, f * CHUNK + h * 512: f * CHUNK + (h + 1) * 512],
                                 start=(f == 0), stop=(f == FT - 1))
                        last = E.matmul(psq[h][0:1, :], onescol[:, 0:1],
                                        sq[b][:, h * 512:(h + 1) * 512],
                                        start=(f == 0), stop=(f == FT - 1))
                    return last
                p.op("tensor", statg,
                     reads=[f"xln:{f}", f"lnsq:{b}", "onescol"],
                     writes=["ln_px0", "ln_px1", "ln_pq0", "ln_pq1"])
            mu = statrow[:, 0:CHUNK]
            e2 = statrow[:, CHUNK:2 * CHUNK]
            veps = statrow[:, 2 * CHUNK:3 * CHUNK]
            sd = statrow[:, 3 * CHUNK:4 * CHUNK]
            rr = statrow[:, 2 * CHUNK:3 * CHUNK]
            mr = statrow[:, 3 * CHUNK:4 * CHUNK]
            for h in range(2):
                p.op("vector",
                     lambda E, h=h: E.tensor_scalar(
                         out=mu[:, h * 512:(h + 1) * 512], in0=psx[h][0:1, :],
                         scalar1=1.0 / D, scalar2=None, op0=A.mult),
                     reads=[f"ln_px{h}"], writes=["ln_mu"])
                p.op("vector",
                     lambda E, h=h: E.tensor_scalar(
                         out=e2[:, h * 512:(h + 1) * 512], in0=psq[h][0:1, :],
                         scalar1=1.0 / D, scalar2=EPS_LN, op0=A.mult, op1=A.add),
                     reads=[f"ln_pq{h}"], writes=["ln_e2"])
            p.op("vector", lambda E: E.tensor_mul(out=veps[:], in0=mu[:], in1=mu[:]),
                 reads=["ln_mu"], writes=["ln_veps"])
            p.op("vector", lambda E: E.tensor_sub(out=veps[:], in0=e2[:], in1=veps[:]),
                 reads=["ln_e2", "ln_veps"], writes=["ln_veps"])
            p.op("scalar", lambda E: E.activation(out=sd[:], in_=veps[:], func=AF.Sqrt),
                 reads=["ln_veps"], writes=["ln_sd"])
            p.op("vector", lambda E: E.reciprocal(out=rr[:], in_=sd[:]),
                 reads=["ln_sd"], writes=["ln_rr", "ln_veps"])
            p.op("vector", lambda E: E.tensor_mul(out=mr[:], in0=mu[:], in1=rr[:]),
                 reads=["ln_mu", "ln_rr"], writes=["ln_mr", "ln_sd"])
            replicate_row(psr, "ln_psr", rr, "ln_rr", rrep, "rrep", CHUNK)
            replicate_row(psr, "ln_psr", mr, "ln_mr", murep, "murep", CHUNK)
            for f in range(FT):
                s4 = f % 4
                p.dma("scalar", f"lnx{f % 3}", out=lnx[:, s4 * CHUNK:(s4 + 1) * CHUNK],
                      in_=x_src[f * 128:(f + 1) * 128, :],
                      reads=([f"{xsrc_key}:{f}"] if xsrc_key else []),
                      writes=[f"lnx:{s4}"])
                p.op("vector",
                     lambda E, f=f, s4=s4: E.tensor_mul(
                         out=tmp[:], in0=lnx[:, s4 * CHUNK:(s4 + 1) * CHUNK], in1=rrep[:]),
                     reads=[f"lnx:{s4}", "rrep"], writes=["ln_tmp"])
                p.op("vector",
                     lambda E, f=f: E.tensor_sub(
                         out=xln[:, f * CHUNK:(f + 1) * CHUNK], in0=tmp[:], in1=murep[:]),
                     reads=["ln_tmp", "murep"], writes=[f"xln:{f}"])

    prcnt = [0]

    def proj_residual(sx, w_st, n_e, n_d, mov_key, mov_buf, xold_src, xold_key,
                      xnew_dst, xnew_key, bias_off=None, quant_m=None):
        i = prcnt[0]
        prcnt[0] += 1
        cc = n_d * 128
        wf = [sx.enter_context(nc.sbuf_tensor(f"pr{i}_wf{b}", [128, cc], F32)) for b in range(2)]
        wb = [sx.enter_context(nc.sbuf_tensor(f"pr{i}_wb{b}", [128, cc], BF16)) for b in range(2)]
        xo = [sx.enter_context(nc.sbuf_tensor(f"pr{i}_xo{b}", [128, CHUNK], F32)) for b in range(2)]
        xn = [sx.enter_context(nc.sbuf_tensor(f"pr{i}_xn{b}", [128, CHUNK], F32)) for b in range(2)]
        with scope() as px:
            pse = [px.enter_context(nc.psum_tensor(f"pr{i}_ps{b}", [128, CHUNK], F32))
                   for b in range(2)]
            assert quant_m is None

            def prep(e):
                b = e % 2
                p.dma("sync", f"prw{b}", out=wf[b][:], in_=w_st[e], writes=[f"pr_wf{b}"])
                cast_chunk(wb[b], f"pr_wb{b}", wf[b], [f"pr_wf{b}"], cc)
                p.dma("scalar", f"prx{b}", out=xo[b][:],
                      in_=xold_src[e * 128:(e + 1) * 128, :],
                      reads=([f"{xold_key}:{e}"] if xold_key else []),
                      writes=[f"pr_xo{b}"])

            prep(0)
            for e in range(n_e):
                b = e % 2
                if e + 1 < n_e:
                    prep(e + 1)
                def mmgrp(E, b=b):
                    last = None
                    for d in range(n_d):
                        for h in range(2):
                            last = E.matmul(
                                pse[b][:, h * 512:(h + 1) * 512],
                                wb[b][:, d * 128:(d + 1) * 128],
                                mov_buf[:, d * CHUNK + h * 512: d * CHUNK + (h + 1) * 512],
                                start=(d == 0), stop=(d == n_d - 1))
                    return last
                p.op("tensor", mmgrp,
                     reads=[f"pr_wb{b}"] + [f"{mov_key}:{d}" for d in range(n_d)],
                     writes=[f"pr_ps{b}"])
                if quant_m is not None:
                    p.op("vector",
                         lambda E, b=b, m=quant_m: E.scalar_tensor_tensor(
                             out=xn[b][:], in0=pse[b][:], scalar=scol[:, m:m + 1],
                             in1=xo[b][:], op0=A.mult, op1=A.add),
                         reads=[f"pr_ps{b}", "scol", f"pr_xo{b}"], writes=[f"pr_xn{b}"])
                elif bias_off is not None:
                    p.op("vector",
                         lambda E, b=b, e=e: E.scalar_tensor_tensor(
                             out=xn[b][:], in0=pse[b][:],
                             scalar=biasc[:, bias_off + e:bias_off + e + 1],
                             in1=xo[b][:], op0=A.add, op1=A.add),
                         reads=[f"pr_ps{b}", "biasc", f"pr_xo{b}"], writes=[f"pr_xn{b}"])
                else:
                    p.op("vector",
                         lambda E, b=b: E.tensor_add(out=xn[b][:], in0=pse[b][:], in1=xo[b][:]),
                         reads=[f"pr_ps{b}", f"pr_xo{b}"], writes=[f"pr_xn{b}"])
                p.dma("scalar", f"pro{b}", out=xnew_dst[e * 128:(e + 1) * 128, :],
                      in_=xn[b][:], reads=[f"pr_xn{b}"], writes=[f"{xnew_key}:{e}"])

    def phase_l0_mixer():
        with scope() as sx:
            layernorm(x_in, None, sx)
        with scope() as sx:
            ubuf = sx.enter_context(nc.sbuf_tensor("m_u", [128, FT * CHUNK], BF16))
            vbuf = sx.enter_context(nc.sbuf_tensor("m_v", [128, FT * CHUNK], BF16))
            wf = [sx.enter_context(nc.sbuf_tensor(f"m_wf{j}", [128, 2048], F32)) for j in range(3)]
            wb = [sx.enter_context(nc.sbuf_tensor(f"m_wb{j}", [128, 2048], BF16)) for j in range(6)]
            td = [sx.enter_context(nc.sbuf_tensor(f"m_td{b}", [128, CHUNK], F32)) for b in range(1)]
            ti = sx.enter_context(nc.sbuf_tensor("m_ti", [128, CHUNK], F32))
            tb = sx.enter_context(nc.sbuf_tensor("m_tb", [128, CHUNK], F32))
            zacc = sx.enter_context(nc.sbuf_tensor("m_za", [128, 32], F32))

            with scope() as px:
                ps = {n: px.enter_context(nc.psum_tensor(f"m_ps_{n}", [128, CHUNK], F32))
                      for n in ("inp", "dec", "gate")}
                def mprep(f):
                    par = (f % 2) * 3
                    for j in range(3):
                        p.dma("sync", f"mw{j}", out=wf[j][:], in_=wcat_st[3 * f + j],
                              writes=[f"m_wf{j}"])
                        cast_chunk(wb[par + j], f"m_wb{par + j}", wf[j], [f"m_wf{j}"], 2048)

                mprep(0)
                for f in range(FT):
                    par = (f % 2) * 3
                    if f + 1 < FT:
                        mprep(f + 1)
                    for j, pr in enumerate(("inp", "dec", "gate")):
                        def mmg(E, j=j, pr=pr, par=par):
                            last = None
                            for d in range(FT):
                                for h in range(2):
                                    last = E.matmul(
                                        ps[pr][:, h * 512:(h + 1) * 512],
                                        wb[par + j][:, d * 128:(d + 1) * 128],
                                        xln[:, d * CHUNK + h * 512: d * CHUNK + (h + 1) * 512],
                                        start=(d == 0), stop=(d == FT - 1))
                            return last
                        p.op("tensor", mmg,
                             reads=[f"m_wb{par + j}"] + [f"xln:{d}" for d in range(FT)],
                             writes=[f"m_ps_{pr}"])
                    b = 0
                    p.op("scalar",
                         lambda E, f=f, b=b: E.activation(
                             out=td[b][:], in_=ps["dec"][:], func=AF.Sigmoid,
                             bias=bdec[:, f:f + 1]),
                         reads=["m_ps_dec", "bdec"], writes=[f"m_td{b}"])
                    p.op("scalar",
                         lambda E, f=f: E.activation(
                             out=ti[:], in_=ps["inp"][:], func=AF.Silu,
                             bias=biasc[:, f:f + 1]),
                         reads=["m_ps_inp", "biasc"], writes=["m_ti"])
                    p.op("scalar",
                         lambda E, f=f: E.activation(
                             out=vbuf[:, f * CHUNK:(f + 1) * CHUNK], in_=ps["gate"][:],
                             func=AF.Sigmoid, bias=biasc[:, 32 + f:33 + f]),
                         reads=["m_ps_gate", "biasc"], writes=[f"m_v:{f}"])
                    p.op("vector",
                         lambda E, b=b: E.tensor_scalar(out=tb[:], in0=td[b][:],
                                                        scalar1=-1.0, scalar2=1.0,
                                                        op0=A.mult, op1=A.add),
                         reads=[f"m_td{b}"], writes=["m_tb"])
                    p.op("vector", lambda E: E.tensor_mul(out=tb[:], in0=tb[:], in1=ti[:]),
                         reads=["m_tb", "m_ti"], writes=["m_tb"])
                    p.op("vector",
                         lambda E, b=b: E.tensor_tensor_scan(
                             out=ti[:], data0=td[b][:], data1=tb[:], initial=0.0,
                             op0=A.mult, op1=A.add),
                         reads=[f"m_td{b}", "m_tb"], writes=["m_ti"])
                    p.op("vector",
                         lambda E, b=b: E.tensor_tensor_scan(
                             out=tb[:], data0=td[b][:], data1=td[b][:], initial=1.0,
                             op0=A.mult, op1=A.bypass),
                         reads=[f"m_td{b}"], writes=["m_tb"])
                    p.op("vector",
                         lambda E, f=f: E.tensor_mul(
                             out=ubuf[:, f * CHUNK:(f + 1) * CHUNK],
                             in0=vbuf[:, f * CHUNK:(f + 1) * CHUNK], in1=ti[:]),
                         reads=[f"m_v:{f}", "m_ti"], writes=[f"m_u:{f}"])
                    p.op("vector",
                         lambda E, f=f: E.tensor_mul(
                             out=vbuf[:, f * CHUNK:(f + 1) * CHUNK],
                             in0=vbuf[:, f * CHUNK:(f + 1) * CHUNK], in1=tb[:]),
                         reads=[f"m_v:{f}", "m_tb"], writes=[f"m_v:{f}"])
                    p.op("vector",
                         lambda E, f=f: E.tensor_copy(out=zacc[:, f:f + 1],
                                                      in_=ti[:, CHUNK - 1:CHUNK]),
                         reads=["m_ti"], writes=[f"m_za:{f}"])
                    p.op("vector",
                         lambda E, f=f: E.tensor_copy(out=zacc[:, 16 + f:17 + f],
                                                      in_=tb[:, CHUNK - 1:CHUNK]),
                         reads=["m_tb"], writes=[f"m_za:{16 + f}"])
            p.dma("gpsimd", "agb", out=agz_in, in_=zacc[:],
                  reads=[f"m_za:{j}" for j in range(32)], writes=["agz_in_d"])
            p.collective("AllGather", A.bypass, BGRP, [agz_in], [agz_out],
                         reads=["agz_in_d"], writes=["agz_out_d"])
            zg = sx.enter_context(nc.sbuf_tensor("m_zg", [128, 128], F32))
            for j in range(4):
                p.dma("gpsimd", "agb", out=zg[:, 32 * j:32 * (j + 1)],
                      in_=agz_out[128 * j:128 * (j + 1), :], reads=["agz_out_d"],
                      writes=[f"m_zg:{j}"])
            u = [sx.enter_context(nc.sbuf_tensor(f"m_uu{j}", [128, 16], F32))
                 for j in range(4)]
            zin = sx.enter_context(nc.sbuf_tensor("m_zin", [128, 16], F32))
            p.op("gpsimd", lambda E: E.memset(u[0][:], 0.0), writes=["m_uu0"])
            p.op("vector", lambda E: E.tensor_copy(out=u[1][:], in_=zg[:, 0:16]),
                 reads=["m_zg:0"], writes=["m_uu1"])
            for j in (2, 3):
                p.op("vector",
                     lambda E, j=j: E.tensor_mul(
                         out=u[j][:], in0=zg[:, 32 * (j - 1) + 16:32 * (j - 1) + 32],
                         in1=u[j - 1][:]),
                     reads=[f"m_zg:{j - 1}", f"m_uu{j - 1}"], writes=[f"m_uu{j}"])
                p.op("vector",
                     lambda E, j=j: E.tensor_add(
                         out=u[j][:], in0=u[j][:], in1=zg[:, 32 * (j - 1):32 * (j - 1) + 16]),
                     reads=[f"m_uu{j}", f"m_zg:{j - 1}"], writes=[f"m_uu{j}"])
            p.op("gpsimd", lambda E: E.memset(zin[:], 0.0), writes=["m_zin"])
            for j in range(4):
                p.op("vector",
                     lambda E, j=j: E.scalar_tensor_tensor(
                         out=zin[:], in0=u[j][:], scalar=zselc[:, j:j + 1], in1=zin[:],
                         op0=A.mult, op1=A.add),
                     reads=[f"m_uu{j}", "zselc", "m_zin"], writes=["m_zin"])
            for f in range(FT):
                p.op("vector",
                     lambda E, f=f: E.scalar_tensor_tensor(
                         out=xln[:, f * CHUNK:(f + 1) * CHUNK],
                         in0=vbuf[:, f * CHUNK:(f + 1) * CHUNK],
                         scalar=zin[:, f:f + 1],
                         in1=ubuf[:, f * CHUNK:(f + 1) * CHUNK],
                         op0=A.mult, op1=A.add),
                     reads=[f"m_v:{f}", "m_zin", f"m_u:{f}"], writes=[f"xln:{f}"])
        with scope() as sx:
            proj_residual(sx, wout_st, 16, FT, "xln", xln, x_in, None,
                          xhbm[0], "xh0", bias_off=48)

    def phase_ffn(xold_src, xold_key, xnew_dst, xnew_key, wu, wg, wdn, n_et,
                  mu_, mg_, md_, final=False):
        with scope() as sx:
            layernorm(xold_src, xold_key, sx)
        with scope() as sx:
            wfu = [sx.enter_context(nc.sbuf_tensor(f"f{prcnt[0]}_wfu{b}", [128, 2048], F32)) for b in range(3)]
            wfg = [sx.enter_context(nc.sbuf_tensor(f"f{prcnt[0]}_wfg{b}", [128, 2048], F32)) for b in range(3)]
            qu = [sx.enter_context(nc.sbuf_tensor(f"f{prcnt[0]}_qu{b}", [128, 2048], BF16)) for b in range(3)]
            qg = [sx.enter_context(nc.sbuf_tensor(f"f{prcnt[0]}_qg{b}", [128, 2048], BF16)) for b in range(3)]
            tsil = [sx.enter_context(nc.sbuf_tensor(f"f{prcnt[0]}_ts{b}", [128, CHUNK], F32)) for b in range(2)]
            hb = [sx.enter_context(nc.sbuf_tensor(f"f{prcnt[0]}_hb{b}", [128, CHUNK], BF16)) for b in range(2)]
            qm = [sx.enter_context(nc.sbuf_tensor(f"f{prcnt[0]}_qm{b}", [128, 2048], BF16)) for b in range(4)]
            dwf = [sx.enter_context(nc.sbuf_tensor(f"f{prcnt[0]}_dwf{b}", [128, 2048], F32)) for b in range(1)]
            dq = [sx.enter_context(nc.sbuf_tensor(f"f{prcnt[0]}_dq{b}", [128, 2048], BF16)) for b in range(2)]
            dqm = [sx.enter_context(nc.sbuf_tensor(f"f{prcnt[0]}_dqm{b}", [128, 2048], BF16)) for b in range(1)]
            cc = n_et * 128
            nq = (cc + 2047) // 2048
            dj_total = 16 * nq
            djq = 0
            with scope() as px:
                psu = [px.enter_context(nc.psum_tensor(f"f{prcnt[0]}_pu{b}", [128, CHUNK], F32)) for b in range(2)]
                psg = [px.enter_context(nc.psum_tensor(f"f{prcnt[0]}_pg{b}", [128, CHUNK], F32)) for b in range(2)]

                def down_prep_unit():
                    nonlocal djq
                    if djq >= dj_total:
                        return
                    j, qq = djq // nq, djq % nq
                    c0 = qq * 2048
                    c1 = min(cc, c0 + 2048)
                    bb = 0
                    p.dma("sync", f"dpw{bb}", out=dwf[bb][:, 0:c1 - c0],
                          in_=wdn[j][:, c0:c1], writes=[f"f_dwf{bb}"])
                    quant_chunk(dq[bb], f"f_dq{bb}", dwf[bb], [f"f_dwf{bb}"],
                                c1 - c0, md_, dqm[0], "f_dqm0")
                    p.dma("scalar", f"dpo{bb}", out=qdh[j][:, c0:c1],
                          in_=dq[bb][:, 0:c1 - c0], reads=[f"f_dq{bb}"],
                          writes=[f"qdh:{j}"])
                    djq += 1

                def fprep(e):
                    b3 = e % 3
                    b = e % 2
                    p.dma("sync", f"fu{b3}", out=wfu[b3][:], in_=wu[e], writes=[f"f_wfu{b3}"])
                    p.dma("sync", f"fg{b3}", out=wfg[b3][:], in_=wg[e], writes=[f"f_wfg{b3}"])
                    quant_chunk(qu[b3], f"f_qu{b3}", wfu[b3], [f"f_wfu{b3}"], 2048, mu_,
                                qm[b], f"f_qm{b}")
                    quant_chunk(qg[b3], f"f_qg{b3}", wfg[b3], [f"f_wfg{b3}"], 2048, mg_,
                                qm[2 + b], f"f_qm{2 + b}")

                fprep(0)
                for e in range(n_et):
                    b3 = e % 3
                    b = e % 2
                    if e + 1 < n_et:
                        fprep(e + 1)
                    while djq * n_et < (e + 1) * dj_total:
                        down_prep_unit()
                    def uq(E, b=b, b3=b3):
                        last = None
                        for d in range(FT):
                            for h in range(2):
                                last = E.matmul(
                                    psu[b][:, h * 512:(h + 1) * 512],
                                    qu[b3][:, d * 128:(d + 1) * 128],
                                    xln[:, d * CHUNK + h * 512: d * CHUNK + (h + 1) * 512],
                                    start=(d == 0), stop=(d == FT - 1))
                        return last
                    p.op("tensor", uq,
                         reads=[f"f_qu{b3}"] + [f"xln:{d}" for d in range(FT)],
                         writes=[f"f_pu{b}"])
                    def gq(E, b=b, b3=b3):
                        last = None
                        for d in range(FT):
                            for h in range(2):
                                last = E.matmul(
                                    psg[b][:, h * 512:(h + 1) * 512],
                                    qg[b3][:, d * 128:(d + 1) * 128],
                                    xln[:, d * CHUNK + h * 512: d * CHUNK + (h + 1) * 512],
                                    start=(d == 0), stop=(d == FT - 1))
                        return last
                    p.op("tensor", gq,
                         reads=[f"f_qg{b3}"] + [f"xln:{d}" for d in range(FT)],
                         writes=[f"f_pg{b}"])
                    p.op("scalar",
                         lambda E, b=b, m=mu_: E.activation(
                             out=tsil[b][:], in_=psu[b][:], func=AF.Silu,
                             scale=scol[:, m:m + 1]),
                         reads=[f"f_pu{b}", "scol"], writes=[f"f_ts{b}"])
                    p.op("vector",
                         lambda E, b=b, m=mg_: E.scalar_tensor_tensor(
                             out=hb[b][:], in0=psg[b][:], scalar=scol[:, m:m + 1],
                             in1=tsil[b][:], op0=A.mult, op1=A.mult),
                         reads=[f"f_pg{b}", "scol", f"f_ts{b}"], writes=[f"f_hb{b}"])
                    p.dma("scalar", f"fh{b}", out=hhbm[e * 128:(e + 1) * 128, :],
                          in_=hb[b][:], reads=[f"f_hb{b}"], writes=[f"hh:{e}"])
                while djq < dj_total:
                    down_prep_unit()
        i = prcnt[0]
        prcnt[0] += 1
        cc = n_et * 128
        nq = (cc + 2047) // 2048
        with scope() as sx:
            hsb = sx.enter_context(nc.sbuf_tensor(f"fd{i}_h", [128, n_et * 512], BF16))
            qd = [sx.enter_context(nc.sbuf_tensor(f"fd{i}_qd{b}", [128, cc], BF16)) for b in range(3)]
            xo = [sx.enter_context(nc.sbuf_tensor(f"fd{i}_xo{b}", [128, 512], F32)) for b in range(2)]
            xn = [sx.enter_context(nc.sbuf_tensor(f"fd{i}_xn{b}", [128, 512], F32)) for b in range(2)]
            with scope() as px:
                psd = [px.enter_context(nc.psum_tensor(f"fd{i}_ps{b}", [128, 512], F32))
                       for b in range(2)]
                for th in range(2):
                    for e in range(n_et):
                        p.dma("scalar", f"fhr{e % 3}",
                              out=hsb[:, e * 512:(e + 1) * 512],
                              in_=hhbm[e * 128:(e + 1) * 128, th * 512:(th + 1) * 512],
                              reads=[f"hh:{e}"], writes=[f"fd_h:{e}"])
                    def dprep(e, th=th):
                        b3 = e % 3
                        b = e % 2
                        p.dma("sync", f"fdw{b3}", out=qd[b3][:, 0:cc],
                              in_=qdh[e][:, 0:cc], reads=[f"qdh:{e}"],
                              writes=[f"fd_qd{b3}"])
                        p.dma("scalar", f"fdx{b}", out=xo[b][:],
                              in_=xold_src[e * 128:(e + 1) * 128, th * 512:(th + 1) * 512],
                              reads=([f"{xold_key}:{e}"] if xold_key else []),
                              writes=[f"fd_xo{b}"])

                    dprep(0)
                    for e in range(16):
                        b3 = e % 3
                        b = e % 2
                        if e + 1 < 16:
                            dprep(e + 1)
                        def dmg(E, b=b, b3=b3):
                            last = None
                            for d in range(n_et):
                                last = E.matmul(
                                    psd[b][:, :], qd[b3][:, d * 128:(d + 1) * 128],
                                    hsb[:, d * 512:(d + 1) * 512],
                                    start=(d == 0), stop=(d == n_et - 1))
                            return last
                        p.op("tensor", dmg,
                             reads=[f"fd_qd{b3}"] + [f"fd_h:{d}" for d in range(n_et)],
                             writes=[f"fd_ps{b}"])
                        p.op("vector",
                             lambda E, b=b, m=md_: E.scalar_tensor_tensor(
                                 out=xn[b][:], in0=psd[b][:], scalar=scol[:, m:m + 1],
                                 in1=xo[b][:], op0=A.mult, op1=A.add),
                             reads=[f"fd_ps{b}", "scol", f"fd_xo{b}"], writes=[f"fd_xn{b}"])
                        p.dma("scalar", f"fdo{b}",
                              out=xnew_dst[e * 128:(e + 1) * 128, th * 512:(th + 1) * 512],
                              in_=xn[b][:], reads=[f"fd_xn{b}"],
                              writes=[f"{xnew_key}:{e}"])

    def phase_attn(xold_src, xold_key, xnew_dst, xnew_key):
        with scope() as sx:
            layernorm(xold_src, xold_key, sx)
        with scope() as sx:
            wf = [sx.enter_context(nc.sbuf_tensor(f"a_wf{b}", [128, 2048], F32)) for b in range(2)]
            wb = [sx.enter_context(nc.sbuf_tensor(f"a_wb{b}", [128, 2048], BF16)) for b in range(2)]
            qt = [sx.enter_context(nc.sbuf_tensor(f"a_qt{b}", [128, CHUNK], BF16)) for b in range(2)]
            with scope() as px:
                psq = [px.enter_context(nc.psum_tensor(f"a_pq{b}", [128, CHUNK], F32)) for b in range(2)]
                for (w_st, n_e, dst, row0, scale) in (
                        (wk_st, 4, agkv_in, 0, 1.0),
                        (wv_st, 4, agkv_in, 512, 1.0),
                        (wq_st, 16, agq_in, 0, 1.0 / float(np.sqrt(128.0)))):
                    if w_st is wq_st:
                        p.collective("AllGather", A.bypass, BGRP, [agkv_in], [agkv_out],
                                     reads=["agkv_in_d"], writes=["agkv_out_d"])
                    def aprep(e, w_st=w_st):
                        b = e % 2
                        p.dma("sync", f"aw{b}", out=wf[b][:], in_=w_st[e], writes=[f"a_wf{b}"])
                        cast_chunk(wb[b], f"a_wb{b}", wf[b], [f"a_wf{b}"], 2048)

                    aprep(0)
                    for e in range(n_e):
                        b = e % 2
                        if e + 1 < n_e:
                            aprep(e + 1)
                        def qmg(E, b=b):
                            last = None
                            for d in range(FT):
                                for h in range(2):
                                    last = E.matmul(
                                        psq[b][:, h * 512:(h + 1) * 512],
                                        wb[b][:, d * 128:(d + 1) * 128],
                                        xln[:, d * CHUNK + h * 512: d * CHUNK + (h + 1) * 512],
                                        start=(d == 0), stop=(d == FT - 1))
                            return last
                        p.op("tensor", qmg,
                             reads=[f"a_wb{b}"] + [f"xln:{d}" for d in range(FT)],
                             writes=[f"a_pq{b}"])
                        p.op("scalar",
                             lambda E, b=b, scale=scale: E.activation(
                                 out=qt[b][:], in_=psq[b][:], func=AF.Copy, scale=scale),
                             reads=[f"a_pq{b}"], writes=[f"a_qt{b}"])
                        p.dma("scalar", f"aq{b}",
                              out=dst[row0 + e * 128: row0 + (e + 1) * 128, :],
                              in_=qt[b][:], reads=[f"a_qt{b}"],
                              writes=["agkv_in_d" if dst is agkv_in else "agq_in_d"])
        p.collective("AllGather", A.bypass, BGRP, [agq_in], [agq_out],
                     reads=["agq_in_d"], writes=["agq_out_d"])
        with scope() as sx:
            qg = sx.enter_context(nc.sbuf_tensor("at_q", [128, 4 * S], BF16))
            kg = sx.enter_context(nc.sbuf_tensor("at_k", [128, S], BF16))
            vf = sx.enter_context(nc.sbuf_tensor("at_vf", [128, S], BF16))
            vt = sx.enter_context(nc.sbuf_tensor("at_vt", [128, S], BF16))
            idn = sx.enter_context(nc.sbuf_tensor("at_idn", [128, 128], BF16))
            mstr = sx.enter_context(nc.sbuf_tensor("at_m", [128, 32 * 512], BF16))
            dtil = [sx.enter_context(nc.sbuf_tensor(f"at_d{b}", [128, 512], F32)) for b in range(2)]
            stt_ = [sx.enter_context(nc.sbuf_tensor(f"at_s{b}", [128, 1024], F32)) for b in range(2)]
            ptt = [sx.enter_context(nc.sbuf_tensor(f"at_p{b}", [128, 1024], BF16)) for b in range(3)]
            ob = [sx.enter_context(nc.sbuf_tensor(f"at_ob{b}", [128, 512], BF16)) for b in range(2)]
            for j in range(4):
                p.idma("gq", out=kg[:, j * CHUNK:(j + 1) * CHUNK], in_=agkv_out[:, :],
                       idx_ap=gidxs[:, 16 + j:17 + j],
                       reads=["agkv_out_d", "gidxs"], writes=[f"at_k:{j}"])
                p.idma("gq", out=vf[:, j * CHUNK:(j + 1) * CHUNK], in_=agkv_out[:, :],
                       idx_ap=gidxs[:, 20 + j:21 + j],
                       reads=["agkv_out_d", "gidxs"], writes=[f"at_vf:{j}"])
            for r in range(4):
                for j in range(4):
                    p.idma("gq", out=qg[:, (r * 4 + j) * CHUNK:(r * 4 + j + 1) * CHUNK],
                           in_=agq_out[:, :], idx_ap=gidxs[:, r * 4 + j:r * 4 + j + 1],
                           reads=["agq_out_d", "gidxs"], writes=[f"at_q:{r}:{j}"])
            p.op("gpsimd", lambda E: E.memset(idn[:], 1.0), writes=["at_idn"])
            p.op("gpsimd",
                 lambda E: E.affine_select(out=idn[:], in_=idn[:], pattern=[[-1, 128]],
                                           compare_op=A.is_equal, fill=0.0, base=0,
                                           channel_multiplier=1),
                 reads=["at_idn"], writes=["at_idn"])
            with scope() as px:
                pst = [px.enter_context(nc.psum_tensor(f"at_pt{b}", [128, 128], BF16))
                       for b in range(2)]
                for kt in range(32):
                    b = kt % 2
                    p.op("tensor",
                         lambda E, kt=kt, b=b: E.transpose(
                             pst[b][:, :], vf[:, kt * 128:(kt + 1) * 128], idn[:]),
                         reads=[f"at_vf:{kt // 8}", "at_idn"], writes=[f"at_pt{b}"])
                    p.op("scalar",
                         lambda E, kt=kt, b=b: E.copy(
                             out=vt[:, kt * 128:(kt + 1) * 128], in_=pst[b][:, :]),
                         reads=[f"at_pt{b}"], writes=[f"at_vt:{kt}"])
            with scope() as px:
                pssc = px.enter_context(nc.psum_tensor("at_sc", [128, 2048], F32))
                pso = px.enter_context(nc.psum_tensor("at_o", [128, 512], F32))
                psd = px.enter_context(nc.psum_tensor("at_dn", [128, 512], F32))
                dnrow = sx.enter_context(nc.sbuf_tensor("at_dnr", [1, 512], F32))
                for qs in range(8):
                    nkt = 4 * qs + 4
                    for kt in range(nkt):
                        b = kt % 2
                        p.op("gpsimd",
                             lambda E, qs=qs, kt=kt, b=b: E.iota(
                                 dtil[b][:], pattern=[[1, 512]], base=512 * qs - 128 * kt,
                                 channel_multiplier=-1,
                                 allow_small_or_imprecise_dtypes=True),
                             writes=[f"at_d{b}"])
                        p.op("scalar",
                             lambda E, kt=kt, b=b: E.activation(
                                 out=mstr[:, kt * 512:(kt + 1) * 512], in_=dtil[b][:],
                                 func=AF.Sqrt),
                             reads=[f"at_d{b}"], writes=[f"at_m:{kt}"])
                    for r in range(4):
                        def qkpair(kp, r=r, qs=qs):
                            k0 = 2 * kp
                            s0 = (kp % 2) * 1024
                            bp = kp % 3
                            bs = kp % 2
                            def qkg(E, k0=k0, s0=s0, r=r, qs=qs):
                                E.matmul(pssc[:, s0:s0 + 512],
                                         kg[:, k0 * 128:(k0 + 1) * 128],
                                         qg[:, r * S + qs * 512: r * S + (qs + 1) * 512],
                                         start=True, stop=True)
                                return E.matmul(pssc[:, s0 + 512:s0 + 1024],
                                                kg[:, (k0 + 1) * 128:(k0 + 2) * 128],
                                                qg[:, r * S + qs * 512: r * S + (qs + 1) * 512],
                                                start=True, stop=True)
                            p.op("tensor", qkg,
                                 reads=[f"at_k:{k0 // 8}", f"at_k:{(k0 + 1) // 8}",
                                        f"at_q:{r}:{qs // 2}"],
                                 writes=[f"at_scp{bs}"])
                            p.op("vector",
                                 lambda E, k0=k0, s0=s0, r=r, bs=bs: E.scalar_tensor_tensor(
                                     out=stt_[bs][:], in0=mstr[:, k0 * 512:(k0 + 2) * 512],
                                     scalar=slopc[:, r:r + 1], in1=pssc[:, s0:s0 + 1024],
                                     op0=A.mult, op1=A.add),
                                 reads=[f"at_m:{k0}", f"at_m:{k0 + 1}", "slopc", f"at_scp{bs}"],
                                 writes=[f"at_sp{bs}"])
                            p.op("scalar",
                                 lambda E, bp=bp, bs=bs: E.activation(
                                     out=ptt[bp][:], in_=stt_[bs][:], func=AF.Exp,
                                     bias=shiftc[:, 0:1]),
                                 reads=[f"at_sp{bs}", "shiftc"], writes=[f"at_pp{bp}"])
                            for kk in (k0, k0 + 1):
                                if kk >= 4 * qs:
                                    p.op("gpsimd",
                                         lambda E, qs=qs, kk=kk, bp=bp, k0=k0: E.affine_select(
                                             out=ptt[bp][:, (kk - k0) * 512:(kk - k0 + 1) * 512],
                                             in_=ptt[bp][:, (kk - k0) * 512:(kk - k0 + 1) * 512],
                                             pattern=[[1, 512]], compare_op=A.is_ge,
                                             fill=0.0, base=512 * qs - 128 * kk,
                                             channel_multiplier=-1),
                                         reads=[f"at_pp{bp}"], writes=[f"at_pp{bp}"])

                        def avdenpair(kp, nkt=nkt):
                            k0 = 2 * kp
                            bp = kp % 3
                            def avg_(E, k0=k0, bp=bp, nkt=nkt):
                                last = None
                                for kk in (k0, k0 + 1):
                                    E.matmul(pso[:, :], vt[:, kk * 128:(kk + 1) * 128],
                                             ptt[bp][:, (kk - k0) * 512:(kk - k0 + 1) * 512],
                                             start=(kk == 0), stop=(kk == nkt - 1))
                                    last = E.matmul(psd[0:1, :], onescol[:, 0:1],
                                                    ptt[bp][:, (kk - k0) * 512:(kk - k0 + 1) * 512],
                                                    start=(kk == 0), stop=(kk == nkt - 1))
                                return last
                            p.op("tensor", avg_,
                                 reads=[f"at_vt:{k0}", f"at_vt:{k0 + 1}", f"at_pp{bp}",
                                        "onescol"],
                                 writes=["at_pso", "at_psd"])

                        npair = nkt // 2
                        for kp in range(npair + 2):
                            if kp < npair:
                                qkpair(kp)
                            if kp >= 2:
                                avdenpair(kp - 2)
                        bo = r % 2
                        p.op("vector",
                             lambda E: E.tensor_copy(out=dnrow[:], in_=psd[0:1, :]),
                             reads=["at_psd"], writes=["at_dnr"])
                        p.dma("scalar", "aod",
                              out=agd_in[r:r + 1, qs * 512:(qs + 1) * 512],
                              in_=dnrow[:], reads=["at_dnr"], writes=["agd_in_d"])
                        p.op("scalar",
                             lambda E, bo=bo: E.copy(out=ob[bo][:], in_=pso[:, :]),
                             reads=["at_pso"], writes=[f"at_ob{bo}"])
                        p.dma("scalar", f"ao{bo}",
                              out=ago_in[r * 128:(r + 1) * 128, qs * 512:(qs + 1) * 512],
                              in_=ob[bo][:], reads=[f"at_ob{bo}"], writes=["ago_in_d"])
        p.collective("AllGather", A.bypass, BGRP, [ago_in], [ago_out],
                     reads=["ago_in_d"], writes=["ago_out_d"])
        p.collective("AllGather", A.bypass, BGRP, [agd_in], [agd_out],
                     reads=["agd_in_d"], writes=["agd_out_d"])
        with scope() as sx:
            mo = sx.enter_context(nc.sbuf_tensor("ao_m", [128, FT * CHUNK], BF16))
            dng = [sx.enter_context(nc.sbuf_tensor(f"ao_dn{b}", [128, CHUNK], F32)) for b in range(2)]
            ago_v = ago_out.rearrange("a (b c) -> (a b) c", c=CHUNK)
            agd_v = agd_out.rearrange("a (b c) -> (a b) c", c=CHUNK)
            for f in range(FT):
                b = f % 2
                p.idma("go", out=mo[:, f * CHUNK:(f + 1) * CHUNK], in_=ago_v,
                       idx_ap=oidxs[:, f:f + 1],
                       reads=["ago_out_d", "oidxs"], writes=[f"ao_m:{f}"])
                p.idma("go", out=dng[b][:], in_=agd_v, idx_ap=oidxs[:, 16 + f:17 + f],
                       reads=["agd_out_d", "oidxs"], writes=[f"ao_dn{b}"])
                p.op("vector",
                     lambda E, b=b: E.reciprocal(out=dng[b][:], in_=dng[b][:]),
                     reads=[f"ao_dn{b}"], writes=[f"ao_dn{b}"])
                p.op("vector",
                     lambda E, f=f, b=b: E.tensor_mul(
                         out=mo[:, f * CHUNK:(f + 1) * CHUNK],
                         in0=mo[:, f * CHUNK:(f + 1) * CHUNK], in1=dng[b][:]),
                     reads=[f"ao_m:{f}", f"ao_dn{b}"], writes=[f"ao_m:{f}"])
            proj_residual(sx, wo_st, 16, FT, "ao_m", mo, xold_src, xold_key,
                          xnew_dst, xnew_key)

    if upto >= 1:
        phase_l0_mixer()
    phase_qscale()
    cur, cur_key = (xhbm[0], "xh0") if upto >= 1 else (x_in, None)
    if upto >= 2:
        phase_ffn(cur, cur_key, xhbm[1], "xh1", wu_st, wg0_st, wd_st, ET0, 0, 1, 2)
        cur, cur_key = xhbm[1], "xh1"
    if upto >= 3:
        phase_attn(cur, cur_key, xhbm[0], "xh0")
        cur, cur_key = xhbm[0], "xh0"
    if upto >= 4:
        phase_ffn(cur, cur_key, x_out, "xout", w1_st, w3_st, w2_st, ET1, 3, 4, 5)
    else:
        with scope() as sx:
            t = [sx.enter_context(nc.sbuf_tensor(f"cp{b}", [128, CHUNK], F32)) for b in range(2)]
            for f in range(FT):
                b = f % 2
                p.dma("scalar", f"cpi{b}", out=t[b][:], in_=cur[f * 128:(f + 1) * 128, :],
                      reads=([f"{cur_key}:{f}"] if cur_key else []), writes=[f"cpt{b}"])
                p.dma("scalar", f"cpo{b}", out=x_out[f * 128:(f + 1) * 128, :], in_=t[b][:],
                      reads=[f"cpt{b}"], writes=[f"xout:{f}"])
    p.fence_all()
    p.finalize()
    return nc, p


def _wst(w, n_d, n_e):
    return np.ascontiguousarray(
        w.reshape(n_d, 128, n_e, 128).transpose(2, 1, 0, 3).reshape(n_e, 128, n_d * 128),
        dtype=np.float32)


def _col16(v):
    return np.ascontiguousarray(np.asarray(v, np.float32).reshape(16, 128).T)


def _host_inputs(inputs):
    f32 = lambda k: np.asarray(inputs[k], np.float32)
    x = f32("L0_attn_x")
    win, wdec, wgat, wout = (f32("L0_attn_W_in"), f32("L0_attn_W_decay"),
                             f32("L0_attn_W_gate"), f32("L0_attn_W_out"))
    wu, wg0, wd = f32("L0_ffn_W_up"), f32("L0_ffn_W_gate"), f32("L0_ffn_W_down")
    wq, wk, wv, wo = (f32("L1_attn_Wq"), f32("L1_attn_Wk"), f32("L1_attn_Wv"),
                      f32("L1_attn_Wo"))
    w1, w3, w2 = f32("L1_ffn_w1"), f32("L1_ffn_w3"), f32("L1_ffn_w2")

    win_s, wdec_s, wgat_s = _wst(win, 16, 16), _wst(wdec, 16, 16), _wst(wgat, 16, 16)
    wcat = np.empty((48, 128, 2048), np.float32)
    for f in range(16):
        wcat[3 * f + 0] = win_s[f]
        wcat[3 * f + 1] = wdec_s[f]
        wcat[3 * f + 2] = wgat_s[f]
    bmix = np.concatenate([_col16(inputs["L0_attn_b_in"]),
                           _col16(inputs["L0_attn_b_decay"]),
                           _col16(inputs["L0_attn_b_gate"]),
                           _col16(inputs["L0_attn_b_out"])], axis=1)
    w1p = np.zeros((2048, H1P), np.float32); w1p[:, :H1] = w1.T
    w3p = np.zeros((2048, H1P), np.float32); w3p[:, :H1] = w3.T
    w2p = np.zeros((H1P, 2048), np.float32); w2p[:H1, :] = w2.T

    shared = dict(
        wcat_st=wcat, bmix=bmix, dbias=_col16(inputs["L0_attn_decay_bias"]),
        wout_st=_wst(wout, 16, 16),
        wu_st=_wst(wu.T, 16, 64), wg0_st=_wst(wg0.T, 16, 64), wd_st=_wst(wd.T, 64, 16),
        wq_st=_wst(wq.T, 16, 16), wk_st=_wst(wk.T, 16, 4), wv_st=_wst(wv.T, 16, 4),
        wo_st=_wst(wo.T, 16, 16),
        w1_st=_wst(w1p, 16, 43), w3_st=_wst(w3p, 16, 43), w2_st=_wst(w2p, 43, 16))

    qflats = [wu.ravel(), wg0.ravel(), wd.ravel(), w1.ravel(), w3.ravel(), w2.ravel()]
    qrcp = np.zeros((1, 16), np.float32)
    qrcp[0, :6] = [np.float32(1.0) / np.float32(q.size) for q in qflats]
    qeps = np.zeros((1, 16), np.float32)
    qeps[0, :3] = 1e-8
    shared["qrcp"] = qrcp
    shared["qeps"] = qeps

    slopes = (2.0 ** (-0.5 * np.arange(1, 17))).astype(np.float32)
    prng = np.arange(128, dtype=np.int32)
    in_maps = []
    for k in range(NCORE):
        b, c = k // 4, k % 4
        g = c
        xs = np.ascontiguousarray(x[b, c * CHUNK:(c + 1) * CHUNK, :].T, dtype=np.float32)
        ns = np.repeat(-slopes[4 * g:4 * g + 4][None, :], 128, axis=0)
        zs = np.zeros((128, 4), np.float32)
        zs[:, c] = 1.0
        gi = np.zeros((128, 24), np.int32)
        for r in range(4):
            for j in range(4):
                gi[:, r * 4 + j] = j * 2048 + 512 * g + 128 * r + prng
        for j in range(4):
            gi[:, 16 + j] = j * 1024 + 128 * g + prng
            gi[:, 20 + j] = j * 1024 + 512 + 128 * g + prng
        oi = np.zeros((128, 32), np.int32)
        for f in range(FT):
            oi[:, f] = (128 * f + prng) * 4 + c
            oi[:, 16 + f] = f * 4 + c
        qsh = np.concatenate(
            [q[k * (q.size // 8):(k + 1) * (q.size // 8)].reshape(128, -1) for q in qflats],
            axis=1).astype(np.float32)
        m = dict(shared)
        m.update(x_in=xs, negslopes=np.ascontiguousarray(ns), zsel=zs,
                 gidx=np.ascontiguousarray(gi), oidx=np.ascontiguousarray(oi),
                 qshard=np.ascontiguousarray(qsh))
        in_maps.append(m)
    return in_maps


def kernel(**inputs):
    in_maps = _host_inputs(inputs)
    nc, _ = build_program(upto=4)
    res = run_bass_kernel_spmd(nc, in_maps, core_ids=list(range(NCORE)))
    out = np.empty((B, S, D), np.float32)
    for k in range(NCORE):
        b, c = k // 4, k % 4
        out[b, c * CHUNK:(c + 1) * CHUNK, :] = res.results[k]["x_out"].T
    return out


# version 09
# speedup vs baseline: 1.0126x; 1.0126x over previous
import numpy as np
from contextlib import contextmanager
import concourse.bass as bass
import concourse.mybir as mybir
from concourse.bass_utils import run_bass_kernel_spmd
from contextlib import ExitStack

F32 = mybir.dt.float32
BF16 = mybir.dt.bfloat16
I32 = mybir.dt.int32
A = mybir.AluOpType
AF = mybir.ActivationFunctionType
AX = mybir.AxisListType

B, S, D = 2, 4096, 2048
NCORE = 8
CHUNK = 1024
FT = D // 128
H0, ET0 = 8192, 64
H1, H1P = 5461, 5504
ET1 = H1P // 128
EPS_LN = 1e-5
SHIFT = 16.0

ENGINES = ("tensor", "vector", "scalar", "gpsimd", "sync")


class _Sem:
    __slots__ = ("handle", "count", "snapshots", "name")

    def __init__(self, handle, name):
        self.handle = handle
        self.count = 0
        self.name = name
        self.snapshots = {}


class Dep:
    __slots__ = ("sem", "value")

    def __init__(self, sem, value):
        self.sem = sem
        self.value = value


class Prog:

    def __init__(self, nc, ctx):
        self.nc = nc
        self.ctx = ctx
        self.ops = {e: [] for e in ENGINES}
        self.esem = {e: _Sem(ctx.enter_context(nc.semaphore(f"es_{e}")), f"es_{e}")
                     for e in ENGINES}
        self.streams = {}
        self.cc_sem = _Sem(ctx.enter_context(nc.semaphore("cc")), "cc")
        self.clock = {e: {} for e in ENGINES}
        self.last_writer = {}
        self.readers = {}
        self.n_waits = 0
        self.n_bumped = 0

    def _stream(self, name):
        if name not in self.streams:
            h = self.ctx.enter_context(self.nc.semaphore(f"ds_{name}"))
            self.streams[name] = _Sem(h, f"ds_{name}")
        return self.streams[name]

    def _wait(self, eng, dep):
        sem, val = dep.sem, dep.value
        if self.clock[eng].get(sem.name, 0) >= val:
            return
        if sem.name.startswith("ds_") and val < sem.count:
            val = sem.count
            self.n_bumped += 1
        snap = sem.snapshots.get(val)
        c = self.clock[eng]
        c[sem.name] = max(c.get(sem.name, 0), val)
        if snap:
            for k, v in snap.items():
                if c.get(k, 0) < v:
                    c[k] = v
        h = sem.handle
        self.ops[eng].append(lambda E, h=h, val=val: E.wait_ge(h, val))
        self.n_waits += 1

    def _resolve(self, eng, reads, writes):
        for dl in getattr(self, "barrier_deps", ()):
            for d in dl:
                self._wait(eng, d)
        skip_own = (eng == "tensor")
        own = f"es_{eng}"
        for r in reads:
            lw = self.last_writer.get(r)
            if lw is not None and not (skip_own and lw.sem.name == own):
                self._wait(eng, lw)
        for w in writes:
            lw = self.last_writer.get(w)
            if lw is not None and not (skip_own and lw.sem.name == own):
                self._wait(eng, lw)
            for rd in self.readers.get(w, ()):
                if not (skip_own and rd.sem.name == own):
                    self._wait(eng, rd)

    def _commit(self, dep, reads, writes):
        for w in writes:
            self.last_writer[w] = dep
            self.readers[w] = []
        for r in reads:
            self.readers.setdefault(r, []).append(dep)

    def op(self, eng, fn, reads=(), writes=()):
        reads, writes = list(reads), list(writes)
        self._resolve(eng, reads, writes)
        sem = self.esem[eng]
        sem.count += 1
        val, h = sem.count, sem.handle

        def run(E, fn=fn, h=h):
            inst = fn(E)
            assert inst is not None, "op fn must return its last instruction"
            inst.then_inc(h, 1)

        self.ops[eng].append(run)
        dep = Dep(sem, val)
        snap = dict(self.clock[eng])
        snap[sem.name] = val
        sem.snapshots[val] = snap
        self._commit(dep, reads, writes)
        return dep

    def dma(self, eng, stream, out, in_, reads=(), writes=()):
        reads, writes = list(reads), list(writes)
        self._resolve(eng, reads, writes)
        s = self._stream(stream)
        s.count += 16
        val, h = s.count, s.handle

        def run(E, out=out, in_=in_, h=h):
            E.dma_start(out=out, in_=in_).then_inc(h, 16)

        self.ops[eng].append(run)
        dep = Dep(s, val)
        s.snapshots[val] = dict(self.clock[eng])
        self._commit(dep, reads, writes)
        return dep

    def idma(self, stream, out, in_, idx_ap, reads=(), writes=()):
        reads, writes = list(reads), list(writes)
        self._resolve("gpsimd", reads, writes)
        s = self._stream(stream)
        s.count += 16
        val, h = s.count, s.handle

        def run(E, out=out, in_=in_, idx_ap=idx_ap, h=h):
            E.indirect_dma_start(
                out=out, out_offset=None, in_=in_,
                in_offset=bass.IndirectOffsetOnAxis(ap=idx_ap, axis=0),
            ).then_inc(h, 16)

        self.ops["gpsimd"].append(run)
        dep = Dep(s, val)
        s.snapshots[val] = dict(self.clock["gpsimd"])
        self._commit(dep, reads, writes)
        return dep

    def collective(self, kind, op_, groups, ins, outs, reads=(), writes=()):
        reads, writes = list(reads), list(writes)
        self._resolve("gpsimd", reads, writes)
        s = self.cc_sem
        s.count += 1
        val, h = s.count, s.handle

        def run(E, kind=kind, op_=op_, rg=groups, ins=ins, outs=outs, h=h):
            E.collective_compute(kind, op_, replica_groups=rg, ins=ins,
                                 outs=outs).then_inc(h, 1)

        self.ops["gpsimd"].append(run)
        dep = Dep(s, val)
        s.snapshots[val] = dict(self.clock["gpsimd"])
        self._commit(dep, reads, writes)
        return dep

    def fence_all(self):
        deps = list(self.last_writer.values())
        for e in ENGINES:
            for d in deps:
                self._wait(e, d)

    def barrier(self):
        deps = list(self.last_writer.values())
        for keydeps in self.readers.values():
            deps.extend(keydeps)
        self.barrier_deps = [deps]

    def finalize(self):
        block = self.ctx.enter_context(self.nc.Block())
        for e in ENGINES:
            ops = self.ops[e]
            if not ops:
                continue

            def body(E, ops=ops):
                for f in ops:
                    f(E)

            getattr(block, e)(body)
        return block


def build_program(upto=4):
    nc = bass.Bass("TRN2", target_bir_lowering=False)
    ctx = ExitStack()
    p = Prog(nc, ctx)
    sb = ctx.enter_context

    def din(name, shape, dt=F32):
        return nc.dram_tensor(name, list(shape), dt, kind="ExternalInput").ap()

    def dint(name, shape, dt=F32):
        return nc.dram_tensor(name, list(shape), dt, kind="Internal").ap()

    x_in = din("x_in", (D, CHUNK))
    negslopes = din("negslopes", (128, 4))
    zsel = din("zsel", (128, 4))
    qshard = din("qshard", (128, 81918))
    qrcp = din("qrcp", (1, 16))
    qeps = din("qeps", (1, 16))
    gidx = din("gidx", (128, 24), I32)
    oidx = din("oidx", (128, 32), I32)
    wcat_st = din("wcat_st", (48, 128, 2048))
    bmix = din("bmix", (128, 64))
    dbias_in = din("dbias", (128, 16))
    wout_st = din("wout_st", (16, 128, 2048))
    wu_st = din("wu_st", (64, 128, 2048))
    wg0_st = din("wg0_st", (64, 128, 2048))
    wd_st = din("wd_st", (16, 128, 8192))
    wq_st = din("wq_st", (16, 128, 2048))
    wk_st = din("wk_st", (4, 128, 2048))
    wv_st = din("wv_st", (4, 128, 2048))
    wo_st = din("wo_st", (16, 128, 2048))
    w1_st = din("w1_st", (43, 128, 2048))
    w3_st = din("w3_st", (43, 128, 2048))
    w2_st = din("w2_st", (16, 128, 5504))
    x_out = nc.dram_tensor("x_out", [D, CHUNK], F32, kind="ExternalOutput").ap()

    xhbm = [dint("xhbm0", (D, CHUNK)), dint("xhbm1", (D, CHUNK))]
    hhbm = dint("hhbm", (H0, CHUNK), BF16)
    qdh = dint("qdh", (16, 128, 8192), BF16)
    ags_in = dint("ags_in", (1, 16))
    ags_out = dint("ags_out", (8, 16))
    agz_in = dint("agz_in", (128, 32))
    agz_out = dint("agz_out", (512, 32))
    agkv_in = dint("agkv_in", (1024, CHUNK), BF16)
    agkv_out = dint("agkv_out", (4096, CHUNK), BF16)
    agq_in = dint("agq_in", (2048, CHUNK), BF16)
    agq_out = dint("agq_out", (8192, CHUNK), BF16)
    ago_in = dint("ago_in", (512, S), BF16)
    ago_out = dint("ago_out", (2048, S), BF16)
    agd_in = dint("agd_in", (4, S))
    agd_out = dint("agd_out", (16, S))

    ALL8 = [list(range(8))]
    BGRP = [[0, 1, 2, 3], [4, 5, 6, 7]]

    @contextmanager
    def scope():
        sx = ExitStack()
        try:
            yield sx
        finally:
            sx.close()
            p.barrier()

    xln = sb(nc.sbuf_tensor("xln", [128, FT * CHUNK], BF16))
    biasc = sb(nc.sbuf_tensor("biasc", [128, 64], F32))
    bdec = sb(nc.sbuf_tensor("bdec", [128, 16], F32))
    dbc = sb(nc.sbuf_tensor("dbc", [128, 16], F32))
    slopc = sb(nc.sbuf_tensor("slopc", [128, 4], F32))
    zselc = sb(nc.sbuf_tensor("zselc", [128, 4], F32))
    onescol = sb(nc.sbuf_tensor("onescol", [128, 1], BF16))
    shiftc = sb(nc.sbuf_tensor("shiftc", [128, 1], F32))

    scol = sb(nc.sbuf_tensor("scol", [128, 16], F32))
    rscol = sb(nc.sbuf_tensor("rscol", [128, 16], F32))
    statrow = sb(nc.sbuf_tensor("statrow", [1, 4 * CHUNK], F32))
    rrep = sb(nc.sbuf_tensor("rrep", [128, CHUNK], F32))
    murep = sb(nc.sbuf_tensor("murep", [128, CHUNK], F32))
    gidxs = sb(nc.sbuf_tensor("gidxs", [128, 24], I32))
    lnx = sb(nc.sbuf_tensor("lnx", [128, 4 * CHUNK], F32))
    lnsq = [sb(nc.sbuf_tensor(f"lnsq{b}", [128, CHUNK], BF16)) for b in range(2)]
    lntmp = sb(nc.sbuf_tensor("lntmp", [128, CHUNK], F32))
    oidxs = sb(nc.sbuf_tensor("oidxs", [128, 32], I32))

    ones1 = sb(nc.sbuf_tensor("ones1", [1, 128], F32))
    p.op("gpsimd", lambda E: E.memset(ones1[:], 1.0), writes=["ones1"])
    p.op("gpsimd", lambda E: E.memset(onescol[:], 1.0), writes=["onescol"])
    p.op("gpsimd", lambda E: E.memset(shiftc[:], -SHIFT), writes=["shiftc"])

    p.dma("sync", "cst", out=biasc[:], in_=bmix, writes=["biasc"])
    p.dma("sync", "cst", out=slopc[:], in_=negslopes, writes=["slopc"])
    p.dma("sync", "cst", out=zselc[:], in_=zsel, writes=["zselc"])
    p.dma("sync", "cst", out=dbc[:], in_=dbias_in, writes=["dbc"])
    p.dma("sync", "cst", out=gidxs[:], in_=gidx, writes=["gidxs"])
    p.dma("sync", "cst", out=oidxs[:], in_=oidx, writes=["oidxs"])
    p.op("vector", lambda E: E.tensor_add(out=bdec[:], in0=biasc[:, 16:32], in1=dbc[:]),
         reads=["biasc", "dbc"], writes=["bdec"])

    QN = [16384, 16384, 16384, 10922, 10922, 10922]
    QCNT = [float(H0 * D)] * 3 + [float(H1 * D)] * 3

    def qscale_make(sx):
        if True:
            qs = sx.enter_context(nc.sbuf_tensor("q_s", [128, 16384], F32))
            part = sx.enter_context(nc.sbuf_tensor("q_part", [128, 16], F32))
            part6 = sx.enter_context(nc.sbuf_tensor("q_p6", [128, 8], F32))
            ssum = sx.enter_context(nc.sbuf_tensor("q_ssum", [1, 16], F32))
            ag8 = sx.enter_context(nc.sbuf_tensor("q_ag8", [8, 16], F32))
            o8 = sx.enter_context(nc.sbuf_tensor("q_o8", [128, 1], F32))
            qr = sx.enter_context(nc.sbuf_tensor("q_qr", [1, 16], F32))
            qe = sx.enter_context(nc.sbuf_tensor("q_qe", [1, 16], F32))
            srow = sx.enter_context(nc.sbuf_tensor("q_srow", [1, 16], F32))
            rs16 = sx.enter_context(nc.sbuf_tensor("q_rs16", [1, 16], F32))
            pp = sx.enter_context(nc.psum_tensor("q_pp", [128, 512], F32))
            p.op("gpsimd", lambda E: E.memset(ssum[:], 0.0), writes=["q_ssum"])
            p.op("gpsimd", lambda E: E.memset(o8[:], 1.0), writes=["q_o8"])
            p.dma("sync", "cst", out=qr[:], in_=qrcp, writes=["q_qr"])
            p.dma("sync", "cst", out=qe[:], in_=qeps, writes=["q_qe"])
            units = []
            off = 0
            j = 0
            ranges = []
            for m, ncols in enumerate(QN):
                a0 = j
                done = 0
                while done < ncols:
                    n = min(8192, ncols - done)
                    units.append((j, off, n))
                    off += n
                    done += n
                    j += 1
                ranges.append((a0, j))

            def unit(uix):
                j, off, n = units[uix]
                lo = (j % 2) * 8192
                p.dma("sync", f"q{j % 2}", out=qs[:, lo:lo + n],
                      in_=qshard[:, off:off + n], writes=[f"q_s:{j % 2}"])
                p.op("vector",
                     lambda E, n=n, j=j, lo=lo: E.reduce_sum(
                         out=part[:, j:j + 1], in_=qs[:, lo:lo + n], axis=AX.X,
                         apply_absolute_value=True),
                     reads=[f"q_s:{j % 2}"], writes=[f"q_part:{j}"])

        def finalize():
            for m, (a0, a1) in enumerate(ranges):
                p.op("vector",
                     lambda E, m=m, a0=a0, a1=a1: E.reduce_sum(
                         out=part6[:, m:m + 1], in_=part[:, a0:a1], axis=AX.X),
                     reads=[f"q_part:{jj}" for jj in range(a0, a1)], writes=["q_p6"])
            p.op("tensor",
                 lambda E: E.matmul(pp[0:1, 0:6], o8[:, 0:1], part6[:, 0:6],
                                    start=True, stop=True),
                 reads=["q_p6", "q_o8"], writes=["q_pp"])
            p.op("vector", lambda E: E.tensor_copy(out=ssum[0:1, 0:6], in_=pp[0:1, 0:6]),
                 reads=["q_pp"], writes=["q_ssum"])
            p.dma("gpsimd", "agb", out=ags_in, in_=ssum[:], reads=["q_ssum"],
                  writes=["ags_in_d"])
            p.collective("AllGather", A.bypass, ALL8, [ags_in], [ags_out],
                         reads=["ags_in_d"], writes=["ags_out_d"])
            p.dma("gpsimd", "agb", out=ag8[:], in_=ags_out, reads=["ags_out_d"],
                  writes=["q_ag8"])
            p.op("tensor",
                 lambda E: E.matmul(pp[0:1, 16:32], o8[0:8, 0:1], ag8[:, :],
                                    start=True, stop=True),
                 reads=["q_ag8", "q_o8"], writes=["q_pp"])
            p.op("vector", lambda E: E.tensor_mul(out=srow[:], in0=pp[0:1, 16:32], in1=qr[:]),
                 reads=["q_pp", "q_qr"], writes=["q_srow"])
            p.op("vector", lambda E: E.tensor_add(out=srow[:], in0=srow[:], in1=qe[:]),
                 reads=["q_srow", "q_qe"], writes=["q_srow"])
            p.op("vector", lambda E: E.reciprocal(out=rs16[:], in_=srow[:]),
                 reads=["q_srow"], writes=["q_rs16"])
            p.op("tensor",
                 lambda E: E.matmul(pp[:, 32:48], ones1[0:1, :], srow[0:1, :],
                                    start=True, stop=True),
                 reads=["q_srow", "ones1"], writes=["q_pp"])
            p.op("scalar", lambda E: E.copy(out=scol[:, :], in_=pp[:, 32:48]),
                 reads=["q_pp"], writes=["scol"])
            p.op("tensor",
                 lambda E: E.matmul(pp[:, 48:64], ones1[0:1, :], rs16[0:1, :],
                                    start=True, stop=True),
                 reads=["q_rs16", "ones1"], writes=["q_pp"])
            p.op("scalar", lambda E: E.copy(out=rscol[:, :], in_=pp[:, 48:64]),
                 reads=["q_pp"], writes=["rscol"])
        return len(units), unit, finalize

    def replicate_row(ps_rep, ps_key, row_ap, row_key, dst, dst_key, n):
        for h0 in range(0, n, 512):
            w = min(512, n - h0)
            p.op("tensor",
                 lambda E, h0=h0, w=w: E.matmul(
                     ps_rep[:, 0:w], ones1[0:1, :], row_ap[0:1, h0:h0 + w],
                     start=True, stop=True),
                 reads=[row_key, "ones1"], writes=[ps_key])
            p.op("scalar",
                 lambda E, h0=h0, w=w: E.copy(out=dst[:, h0:h0 + w], in_=ps_rep[:, 0:w]),
                 reads=[ps_key], writes=[dst_key])

    def cast_chunk(dst, dst_key, src, src_keys, cols):
        p.op("scalar",
             lambda E: E.activation(out=dst[:, 0:cols], in_=src[:, 0:cols], func=AF.Copy),
             reads=src_keys, writes=[dst_key])

    def quant_chunk(dst, dst_key, src, src_keys, cols, mq, tmp, tmp_key):
        p.op("vector",
             lambda E: E.tensor_scalar(out=dst[:, 0:cols], in0=src[:, 0:cols],
                                       scalar1=rscol[:, mq:mq + 1], scalar2=0.5,
                                       op0=A.mult, op1=A.is_gt),
             reads=list(src_keys) + ["rscol"], writes=[dst_key])
        p.op("vector",
             lambda E: E.tensor_scalar(out=tmp[:, 0:cols], in0=src[:, 0:cols],
                                       scalar1=rscol[:, mq:mq + 1], scalar2=-0.5,
                                       op0=A.mult, op1=A.is_lt),
             reads=list(src_keys) + ["rscol"], writes=[tmp_key])
        p.op("vector",
             lambda E: E.tensor_sub(out=dst[:, 0:cols], in0=dst[:, 0:cols],
                                    in1=tmp[:, 0:cols]),
             reads=[dst_key, tmp_key], writes=[dst_key])

    lncnt = [0]

    def layernorm(x_src, xsrc_key, sx):
        i = lncnt[0]
        lncnt[0] += 1
        sq, tmp = lnsq, lntmp
        with scope() as px:
            psx = [px.enter_context(nc.psum_tensor(f"ln{i}_px{h}", [128, 512], F32))
                   for h in range(2)]
            psq = [px.enter_context(nc.psum_tensor(f"ln{i}_pq{h}", [128, 512], F32))
                   for h in range(2)]
            psr = px.enter_context(nc.psum_tensor(f"ln{i}_pr", [128, 512], F32))
            for f in range(FT):
                s4 = f % 4
                p.dma("scalar", f"lnx{f % 3}", out=lnx[:, s4 * CHUNK:(s4 + 1) * CHUNK],
                      in_=x_src[f * 128:(f + 1) * 128, :],
                      reads=([f"{xsrc_key}:{f}"] if xsrc_key else []),
                      writes=[f"lnx:{s4}"])
                p.op("vector",
                     lambda E, f=f, s4=s4: E.tensor_copy(out=xln[:, f * CHUNK:(f + 1) * CHUNK],
                                                         in_=lnx[:, s4 * CHUNK:(s4 + 1) * CHUNK]),
                     reads=[f"lnx:{s4}"], writes=[f"xln:{f}"])
                b = f % 2
                p.op("scalar",
                     lambda E, f=f, b=b: E.activation(
                         out=sq[b][:], in_=xln[:, f * CHUNK:(f + 1) * CHUNK],
                         func=AF.Square),
                     reads=[f"xln:{f}"], writes=[f"lnsq:{b}"])
                def statg(E, f=f, b=b):
                    last = None
                    for h in range(2):
                        E.matmul(psx[h][0:1, :], onescol[:, 0:1],
                                 xln[:, f * CHUNK + h * 512: f * CHUNK + (h + 1) * 512],
                                 start=(f == 0), stop=(f == FT - 1))
                        last = E.matmul(psq[h][0:1, :], onescol[:, 0:1],
                                        sq[b][:, h * 512:(h + 1) * 512],
                                        start=(f == 0), stop=(f == FT - 1))
                    return last
                p.op("tensor", statg,
                     reads=[f"xln:{f}", f"lnsq:{b}", "onescol"],
                     writes=["ln_px0", "ln_px1", "ln_pq0", "ln_pq1"])
            mu = statrow[:, 0:CHUNK]
            e2 = statrow[:, CHUNK:2 * CHUNK]
            veps = statrow[:, 2 * CHUNK:3 * CHUNK]
            sd = statrow[:, 3 * CHUNK:4 * CHUNK]
            rr = statrow[:, 2 * CHUNK:3 * CHUNK]
            mr = statrow[:, 3 * CHUNK:4 * CHUNK]
            for h in range(2):
                p.op("vector",
                     lambda E, h=h: E.tensor_scalar(
                         out=mu[:, h * 512:(h + 1) * 512], in0=psx[h][0:1, :],
                         scalar1=1.0 / D, scalar2=None, op0=A.mult),
                     reads=[f"ln_px{h}"], writes=["ln_mu"])
                p.op("vector",
                     lambda E, h=h: E.tensor_scalar(
                         out=e2[:, h * 512:(h + 1) * 512], in0=psq[h][0:1, :],
                         scalar1=1.0 / D, scalar2=EPS_LN, op0=A.mult, op1=A.add),
                     reads=[f"ln_pq{h}"], writes=["ln_e2"])
            p.op("vector", lambda E: E.tensor_mul(out=veps[:], in0=mu[:], in1=mu[:]),
                 reads=["ln_mu"], writes=["ln_veps"])
            p.op("vector", lambda E: E.tensor_sub(out=veps[:], in0=e2[:], in1=veps[:]),
                 reads=["ln_e2", "ln_veps"], writes=["ln_veps"])
            p.op("scalar", lambda E: E.activation(out=sd[:], in_=veps[:], func=AF.Sqrt),
                 reads=["ln_veps"], writes=["ln_sd"])
            p.op("vector", lambda E: E.reciprocal(out=rr[:], in_=sd[:]),
                 reads=["ln_sd"], writes=["ln_rr", "ln_veps"])
            p.op("vector", lambda E: E.tensor_mul(out=mr[:], in0=mu[:], in1=rr[:]),
                 reads=["ln_mu", "ln_rr"], writes=["ln_mr", "ln_sd"])
            replicate_row(psr, "ln_psr", rr, "ln_rr", rrep, "rrep", CHUNK)
            replicate_row(psr, "ln_psr", mr, "ln_mr", murep, "murep", CHUNK)
            for f in range(FT):
                s4 = f % 4
                p.dma("scalar", f"lnx{f % 3}", out=lnx[:, s4 * CHUNK:(s4 + 1) * CHUNK],
                      in_=x_src[f * 128:(f + 1) * 128, :],
                      reads=([f"{xsrc_key}:{f}"] if xsrc_key else []),
                      writes=[f"lnx:{s4}"])
                p.op("vector",
                     lambda E, f=f, s4=s4: E.tensor_mul(
                         out=tmp[:], in0=lnx[:, s4 * CHUNK:(s4 + 1) * CHUNK], in1=rrep[:]),
                     reads=[f"lnx:{s4}", "rrep"], writes=["ln_tmp"])
                p.op("vector",
                     lambda E, f=f: E.tensor_sub(
                         out=xln[:, f * CHUNK:(f + 1) * CHUNK], in0=tmp[:], in1=murep[:]),
                     reads=["ln_tmp", "murep"], writes=[f"xln:{f}"])

    prcnt = [0]

    def proj_residual(sx, w_st, n_e, n_d, mov_key, mov_buf, xold_src, xold_key,
                      xnew_dst, xnew_key, bias_off=None, quant_m=None, filler=None):
        i = prcnt[0]
        prcnt[0] += 1
        cc = n_d * 128
        wf = [sx.enter_context(nc.sbuf_tensor(f"pr{i}_wf{b}", [128, cc], F32)) for b in range(2)]
        wb = [sx.enter_context(nc.sbuf_tensor(f"pr{i}_wb{b}", [128, cc], BF16)) for b in range(2)]
        xo = [sx.enter_context(nc.sbuf_tensor(f"pr{i}_xo{b}", [128, CHUNK], F32)) for b in range(2)]
        xn = [sx.enter_context(nc.sbuf_tensor(f"pr{i}_xn{b}", [128, CHUNK], F32)) for b in range(2)]
        with scope() as px:
            pse = [px.enter_context(nc.psum_tensor(f"pr{i}_ps{b}", [128, CHUNK], F32))
                   for b in range(2)]
            assert quant_m is None

            def prep(e):
                b = e % 2
                p.dma("sync", f"prw{b}", out=wf[b][:], in_=w_st[e], writes=[f"pr_wf{b}"])
                cast_chunk(wb[b], f"pr_wb{b}", wf[b], [f"pr_wf{b}"], cc)
                p.dma("scalar", f"prx{b}", out=xo[b][:],
                      in_=xold_src[e * 128:(e + 1) * 128, :],
                      reads=([f"{xold_key}:{e}"] if xold_key else []),
                      writes=[f"pr_xo{b}"])

            prep(0)
            for e in range(n_e):
                b = e % 2
                if e + 1 < n_e:
                    prep(e + 1)
                if filler is not None:
                    filler(e)
                def mmgrp(E, b=b):
                    last = None
                    for d in range(n_d):
                        for h in range(2):
                            last = E.matmul(
                                pse[b][:, h * 512:(h + 1) * 512],
                                wb[b][:, d * 128:(d + 1) * 128],
                                mov_buf[:, d * CHUNK + h * 512: d * CHUNK + (h + 1) * 512],
                                start=(d == 0), stop=(d == n_d - 1))
                    return last
                p.op("tensor", mmgrp,
                     reads=[f"pr_wb{b}"] + [f"{mov_key}:{d}" for d in range(n_d)],
                     writes=[f"pr_ps{b}"])
                if quant_m is not None:
                    p.op("vector",
                         lambda E, b=b, m=quant_m: E.scalar_tensor_tensor(
                             out=xn[b][:], in0=pse[b][:], scalar=scol[:, m:m + 1],
                             in1=xo[b][:], op0=A.mult, op1=A.add),
                         reads=[f"pr_ps{b}", "scol", f"pr_xo{b}"], writes=[f"pr_xn{b}"])
                elif bias_off is not None:
                    p.op("vector",
                         lambda E, b=b, e=e: E.scalar_tensor_tensor(
                             out=xn[b][:], in0=pse[b][:],
                             scalar=biasc[:, bias_off + e:bias_off + e + 1],
                             in1=xo[b][:], op0=A.add, op1=A.add),
                         reads=[f"pr_ps{b}", "biasc", f"pr_xo{b}"], writes=[f"pr_xn{b}"])
                else:
                    p.op("vector",
                         lambda E, b=b: E.tensor_add(out=xn[b][:], in0=pse[b][:], in1=xo[b][:]),
                         reads=[f"pr_ps{b}", f"pr_xo{b}"], writes=[f"pr_xn{b}"])
                p.dma("scalar", f"pro{b}", out=xnew_dst[e * 128:(e + 1) * 128, :],
                      in_=xn[b][:], reads=[f"pr_xn{b}"], writes=[f"{xnew_key}:{e}"])

    def phase_l0_mixer():
        with scope() as sx:
            layernorm(x_in, None, sx)
        with scope() as sx:
            ubuf = sx.enter_context(nc.sbuf_tensor("m_u", [128, FT * CHUNK], BF16))
            vbuf = sx.enter_context(nc.sbuf_tensor("m_v", [128, FT * CHUNK], BF16))
            wf = [sx.enter_context(nc.sbuf_tensor(f"m_wf{j}", [128, 2048], F32)) for j in range(3)]
            wb = [sx.enter_context(nc.sbuf_tensor(f"m_wb{j}", [128, 2048], BF16)) for j in range(6)]
            td = [sx.enter_context(nc.sbuf_tensor(f"m_td{b}", [128, CHUNK], F32)) for b in range(1)]
            ti = sx.enter_context(nc.sbuf_tensor("m_ti", [128, CHUNK], F32))
            tb = sx.enter_context(nc.sbuf_tensor("m_tb", [128, CHUNK], F32))
            zacc = sx.enter_context(nc.sbuf_tensor("m_za", [128, 32], F32))

            with scope() as px:
                ps = {n: px.enter_context(nc.psum_tensor(f"m_ps_{n}", [128, CHUNK], F32))
                      for n in ("inp", "dec", "gate")}
                def mprep(f):
                    par = (f % 2) * 3
                    for j in range(3):
                        p.dma("sync", f"mw{j}", out=wf[j][:], in_=wcat_st[3 * f + j],
                              writes=[f"m_wf{j}"])
                        cast_chunk(wb[par + j], f"m_wb{par + j}", wf[j], [f"m_wf{j}"], 2048)

                mprep(0)
                for f in range(FT):
                    par = (f % 2) * 3
                    if f + 1 < FT:
                        mprep(f + 1)
                    for j, pr in enumerate(("inp", "dec", "gate")):
                        def mmg(E, j=j, pr=pr, par=par):
                            last = None
                            for d in range(FT):
                                for h in range(2):
                                    last = E.matmul(
                                        ps[pr][:, h * 512:(h + 1) * 512],
                                        wb[par + j][:, d * 128:(d + 1) * 128],
                                        xln[:, d * CHUNK + h * 512: d * CHUNK + (h + 1) * 512],
                                        start=(d == 0), stop=(d == FT - 1))
                            return last
                        p.op("tensor", mmg,
                             reads=[f"m_wb{par + j}"] + [f"xln:{d}" for d in range(FT)],
                             writes=[f"m_ps_{pr}"])
                    b = 0
                    p.op("scalar",
                         lambda E, f=f, b=b: E.activation(
                             out=td[b][:], in_=ps["dec"][:], func=AF.Sigmoid,
                             bias=bdec[:, f:f + 1]),
                         reads=["m_ps_dec", "bdec"], writes=[f"m_td{b}"])
                    p.op("scalar",
                         lambda E, f=f: E.activation(
                             out=ti[:], in_=ps["inp"][:], func=AF.Silu,
                             bias=biasc[:, f:f + 1]),
                         reads=["m_ps_inp", "biasc"], writes=["m_ti"])
                    p.op("scalar",
                         lambda E, f=f: E.activation(
                             out=vbuf[:, f * CHUNK:(f + 1) * CHUNK], in_=ps["gate"][:],
                             func=AF.Sigmoid, bias=biasc[:, 32 + f:33 + f]),
                         reads=["m_ps_gate", "biasc"], writes=[f"m_v:{f}"])
                    p.op("vector",
                         lambda E, b=b: E.tensor_scalar(out=tb[:], in0=td[b][:],
                                                        scalar1=-1.0, scalar2=1.0,
                                                        op0=A.mult, op1=A.add),
                         reads=[f"m_td{b}"], writes=["m_tb"])
                    p.op("vector", lambda E: E.tensor_mul(out=tb[:], in0=tb[:], in1=ti[:]),
                         reads=["m_tb", "m_ti"], writes=["m_tb"])
                    p.op("vector",
                         lambda E, b=b: E.tensor_tensor_scan(
                             out=ti[:], data0=td[b][:], data1=tb[:], initial=0.0,
                             op0=A.mult, op1=A.add),
                         reads=[f"m_td{b}", "m_tb"], writes=["m_ti"])
                    p.op("vector",
                         lambda E, b=b: E.tensor_tensor_scan(
                             out=tb[:], data0=td[b][:], data1=td[b][:], initial=1.0,
                             op0=A.mult, op1=A.bypass),
                         reads=[f"m_td{b}"], writes=["m_tb"])
                    p.op("vector",
                         lambda E, f=f: E.tensor_mul(
                             out=ubuf[:, f * CHUNK:(f + 1) * CHUNK],
                             in0=vbuf[:, f * CHUNK:(f + 1) * CHUNK], in1=ti[:]),
                         reads=[f"m_v:{f}", "m_ti"], writes=[f"m_u:{f}"])
                    p.op("vector",
                         lambda E, f=f: E.tensor_mul(
                             out=vbuf[:, f * CHUNK:(f + 1) * CHUNK],
                             in0=vbuf[:, f * CHUNK:(f + 1) * CHUNK], in1=tb[:]),
                         reads=[f"m_v:{f}", "m_tb"], writes=[f"m_v:{f}"])
                    p.op("vector",
                         lambda E, f=f: E.tensor_copy(out=zacc[:, f:f + 1],
                                                      in_=ti[:, CHUNK - 1:CHUNK]),
                         reads=["m_ti"], writes=[f"m_za:{f}"])
                    p.op("vector",
                         lambda E, f=f: E.tensor_copy(out=zacc[:, 16 + f:17 + f],
                                                      in_=tb[:, CHUNK - 1:CHUNK]),
                         reads=["m_tb"], writes=[f"m_za:{16 + f}"])
            p.dma("gpsimd", "agb", out=agz_in, in_=zacc[:],
                  reads=[f"m_za:{j}" for j in range(32)], writes=["agz_in_d"])
            p.collective("AllGather", A.bypass, BGRP, [agz_in], [agz_out],
                         reads=["agz_in_d"], writes=["agz_out_d"])
            zg = sx.enter_context(nc.sbuf_tensor("m_zg", [128, 128], F32))
            for j in range(4):
                p.dma("gpsimd", "agb", out=zg[:, 32 * j:32 * (j + 1)],
                      in_=agz_out[128 * j:128 * (j + 1), :], reads=["agz_out_d"],
                      writes=[f"m_zg:{j}"])
            u = [sx.enter_context(nc.sbuf_tensor(f"m_uu{j}", [128, 16], F32))
                 for j in range(4)]
            zin = sx.enter_context(nc.sbuf_tensor("m_zin", [128, 16], F32))
            p.op("gpsimd", lambda E: E.memset(u[0][:], 0.0), writes=["m_uu0"])
            p.op("vector", lambda E: E.tensor_copy(out=u[1][:], in_=zg[:, 0:16]),
                 reads=["m_zg:0"], writes=["m_uu1"])
            for j in (2, 3):
                p.op("vector",
                     lambda E, j=j: E.tensor_mul(
                         out=u[j][:], in0=zg[:, 32 * (j - 1) + 16:32 * (j - 1) + 32],
                         in1=u[j - 1][:]),
                     reads=[f"m_zg:{j - 1}", f"m_uu{j - 1}"], writes=[f"m_uu{j}"])
                p.op("vector",
                     lambda E, j=j: E.tensor_add(
                         out=u[j][:], in0=u[j][:], in1=zg[:, 32 * (j - 1):32 * (j - 1) + 16]),
                     reads=[f"m_uu{j}", f"m_zg:{j - 1}"], writes=[f"m_uu{j}"])
            p.op("gpsimd", lambda E: E.memset(zin[:], 0.0), writes=["m_zin"])
            for j in range(4):
                p.op("vector",
                     lambda E, j=j: E.scalar_tensor_tensor(
                         out=zin[:], in0=u[j][:], scalar=zselc[:, j:j + 1], in1=zin[:],
                         op0=A.mult, op1=A.add),
                     reads=[f"m_uu{j}", "zselc", "m_zin"], writes=["m_zin"])
            for f in range(FT):
                p.op("vector",
                     lambda E, f=f: E.scalar_tensor_tensor(
                         out=xln[:, f * CHUNK:(f + 1) * CHUNK],
                         in0=vbuf[:, f * CHUNK:(f + 1) * CHUNK],
                         scalar=zin[:, f:f + 1],
                         in1=ubuf[:, f * CHUNK:(f + 1) * CHUNK],
                         op0=A.mult, op1=A.add),
                     reads=[f"m_v:{f}", "m_zin", f"m_u:{f}"], writes=[f"xln:{f}"])
        with scope() as sx:
            nunits, qunit, qfin = qscale_make(sx)

            def filler(e, done=[0]):
                while done[0] * 16 < (e + 1) * nunits:
                    qunit(done[0])
                    done[0] += 1

            proj_residual(sx, wout_st, 16, FT, "xln", xln, x_in, None,
                          xhbm[0], "xh0", bias_off=48, filler=filler)
            qfin()

    def phase_ffn(xold_src, xold_key, xnew_dst, xnew_key, wu, wg, wdn, n_et,
                  mu_, mg_, md_, final=False):
        with scope() as sx:
            layernorm(xold_src, xold_key, sx)
        with scope() as sx:
            wfu = [sx.enter_context(nc.sbuf_tensor(f"f{prcnt[0]}_wfu{b}", [128, 2048], F32)) for b in range(3)]
            wfg = [sx.enter_context(nc.sbuf_tensor(f"f{prcnt[0]}_wfg{b}", [128, 2048], F32)) for b in range(3)]
            qu = [sx.enter_context(nc.sbuf_tensor(f"f{prcnt[0]}_qu{b}", [128, 2048], BF16)) for b in range(3)]
            qg = [sx.enter_context(nc.sbuf_tensor(f"f{prcnt[0]}_qg{b}", [128, 2048], BF16)) for b in range(3)]
            tsil = [sx.enter_context(nc.sbuf_tensor(f"f{prcnt[0]}_ts{b}", [128, CHUNK], F32)) for b in range(2)]
            hb = [sx.enter_context(nc.sbuf_tensor(f"f{prcnt[0]}_hb{b}", [128, CHUNK], BF16)) for b in range(2)]
            qm = [sx.enter_context(nc.sbuf_tensor(f"f{prcnt[0]}_qm{b}", [128, 2048], BF16)) for b in range(4)]
            dwf = [sx.enter_context(nc.sbuf_tensor(f"f{prcnt[0]}_dwf{b}", [128, 2048], F32)) for b in range(1)]
            dq = [sx.enter_context(nc.sbuf_tensor(f"f{prcnt[0]}_dq{b}", [128, 2048], BF16)) for b in range(2)]
            dqm = [sx.enter_context(nc.sbuf_tensor(f"f{prcnt[0]}_dqm{b}", [128, 2048], BF16)) for b in range(1)]
            cc = n_et * 128
            nq = (cc + 2047) // 2048
            dj_total = 16 * nq
            djq = 0
            with scope() as px:
                psu = [px.enter_context(nc.psum_tensor(f"f{prcnt[0]}_pu{b}", [128, CHUNK], F32)) for b in range(2)]
                psg = [px.enter_context(nc.psum_tensor(f"f{prcnt[0]}_pg{b}", [128, CHUNK], F32)) for b in range(2)]

                def down_prep_unit():
                    nonlocal djq
                    if djq >= dj_total:
                        return
                    j, qq = djq // nq, djq % nq
                    c0 = qq * 2048
                    c1 = min(cc, c0 + 2048)
                    bb = 0
                    p.dma("sync", f"dpw{bb}", out=dwf[bb][:, 0:c1 - c0],
                          in_=wdn[j][:, c0:c1], writes=[f"f_dwf{bb}"])
                    quant_chunk(dq[bb], f"f_dq{bb}", dwf[bb], [f"f_dwf{bb}"],
                                c1 - c0, md_, dqm[0], "f_dqm0")
                    p.dma("scalar", f"dpo{bb}", out=qdh[j][:, c0:c1],
                          in_=dq[bb][:, 0:c1 - c0], reads=[f"f_dq{bb}"],
                          writes=[f"qdh:{j}"])
                    djq += 1

                def fprep(e):
                    b3 = e % 3
                    b = e % 2
                    p.dma("sync", f"fu{b3}", out=wfu[b3][:], in_=wu[e], writes=[f"f_wfu{b3}"])
                    p.dma("sync", f"fg{b3}", out=wfg[b3][:], in_=wg[e], writes=[f"f_wfg{b3}"])
                    quant_chunk(qu[b3], f"f_qu{b3}", wfu[b3], [f"f_wfu{b3}"], 2048, mu_,
                                qm[b], f"f_qm{b}")
                    quant_chunk(qg[b3], f"f_qg{b3}", wfg[b3], [f"f_wfg{b3}"], 2048, mg_,
                                qm[2 + b], f"f_qm{2 + b}")

                fprep(0)
                for e in range(n_et):
                    b3 = e % 3
                    b = e % 2
                    if e + 1 < n_et:
                        fprep(e + 1)
                    while djq * n_et < (e + 1) * dj_total:
                        down_prep_unit()
                    def uq(E, b=b, b3=b3):
                        last = None
                        for d in range(FT):
                            for h in range(2):
                                last = E.matmul(
                                    psu[b][:, h * 512:(h + 1) * 512],
                                    qu[b3][:, d * 128:(d + 1) * 128],
                                    xln[:, d * CHUNK + h * 512: d * CHUNK + (h + 1) * 512],
                                    start=(d == 0), stop=(d == FT - 1))
                        return last
                    p.op("tensor", uq,
                         reads=[f"f_qu{b3}"] + [f"xln:{d}" for d in range(FT)],
                         writes=[f"f_pu{b}"])
                    def gq(E, b=b, b3=b3):
                        last = None
                        for d in range(FT):
                            for h in range(2):
                                last = E.matmul(
                                    psg[b][:, h * 512:(h + 1) * 512],
                                    qg[b3][:, d * 128:(d + 1) * 128],
                                    xln[:, d * CHUNK + h * 512: d * CHUNK + (h + 1) * 512],
                                    start=(d == 0), stop=(d == FT - 1))
                        return last
                    p.op("tensor", gq,
                         reads=[f"f_qg{b3}"] + [f"xln:{d}" for d in range(FT)],
                         writes=[f"f_pg{b}"])
                    p.op("scalar",
                         lambda E, b=b, m=mu_: E.activation(
                             out=tsil[b][:], in_=psu[b][:], func=AF.Silu,
                             scale=scol[:, m:m + 1]),
                         reads=[f"f_pu{b}", "scol"], writes=[f"f_ts{b}"])
                    p.op("vector",
                         lambda E, b=b, m=mg_: E.scalar_tensor_tensor(
                             out=hb[b][:], in0=psg[b][:], scalar=scol[:, m:m + 1],
                             in1=tsil[b][:], op0=A.mult, op1=A.mult),
                         reads=[f"f_pg{b}", "scol", f"f_ts{b}"], writes=[f"f_hb{b}"])
                    p.dma("scalar", f"fh{b}", out=hhbm[e * 128:(e + 1) * 128, :],
                          in_=hb[b][:], reads=[f"f_hb{b}"], writes=[f"hh:{e}"])
                while djq < dj_total:
                    down_prep_unit()
        i = prcnt[0]
        prcnt[0] += 1
        cc = n_et * 128
        nq = (cc + 2047) // 2048
        with scope() as sx:
            hsb = sx.enter_context(nc.sbuf_tensor(f"fd{i}_h", [128, n_et * 512], BF16))
            qd = [sx.enter_context(nc.sbuf_tensor(f"fd{i}_qd{b}", [128, cc], BF16)) for b in range(3)]
            xo = [sx.enter_context(nc.sbuf_tensor(f"fd{i}_xo{b}", [128, 512], F32)) for b in range(2)]
            xn = [sx.enter_context(nc.sbuf_tensor(f"fd{i}_xn{b}", [128, 512], F32)) for b in range(2)]
            with scope() as px:
                psd = [px.enter_context(nc.psum_tensor(f"fd{i}_ps{b}", [128, 512], F32))
                       for b in range(2)]
                for th in range(2):
                    for e in range(n_et):
                        p.dma("scalar", f"fhr{e % 3}",
                              out=hsb[:, e * 512:(e + 1) * 512],
                              in_=hhbm[e * 128:(e + 1) * 128, th * 512:(th + 1) * 512],
                              reads=[f"hh:{e}"], writes=[f"fd_h:{e}"])
                    def dprep(e, th=th):
                        b3 = e % 3
                        b = e % 2
                        p.dma("sync", f"fdw{b3}", out=qd[b3][:, 0:cc],
                              in_=qdh[e][:, 0:cc], reads=[f"qdh:{e}"],
                              writes=[f"fd_qd{b3}"])
                        p.dma("scalar", f"fdx{b}", out=xo[b][:],
                              in_=xold_src[e * 128:(e + 1) * 128, th * 512:(th + 1) * 512],
                              reads=([f"{xold_key}:{e}"] if xold_key else []),
                              writes=[f"fd_xo{b}"])

                    dprep(0)
                    for e in range(16):
                        b3 = e % 3
                        b = e % 2
                        if e + 1 < 16:
                            dprep(e + 1)
                        def dmg(E, b=b, b3=b3):
                            last = None
                            for d in range(n_et):
                                last = E.matmul(
                                    psd[b][:, :], qd[b3][:, d * 128:(d + 1) * 128],
                                    hsb[:, d * 512:(d + 1) * 512],
                                    start=(d == 0), stop=(d == n_et - 1))
                            return last
                        p.op("tensor", dmg,
                             reads=[f"fd_qd{b3}"] + [f"fd_h:{d}" for d in range(n_et)],
                             writes=[f"fd_ps{b}"])
                        p.op("vector",
                             lambda E, b=b, m=md_: E.scalar_tensor_tensor(
                                 out=xn[b][:], in0=psd[b][:], scalar=scol[:, m:m + 1],
                                 in1=xo[b][:], op0=A.mult, op1=A.add),
                             reads=[f"fd_ps{b}", "scol", f"fd_xo{b}"], writes=[f"fd_xn{b}"])
                        p.dma("scalar", f"fdo{b}",
                              out=xnew_dst[e * 128:(e + 1) * 128, th * 512:(th + 1) * 512],
                              in_=xn[b][:], reads=[f"fd_xn{b}"],
                              writes=[f"{xnew_key}:{e}"])

    def phase_attn(xold_src, xold_key, xnew_dst, xnew_key):
        with scope() as sx:
            layernorm(xold_src, xold_key, sx)
        with scope() as sx:
            wf = [sx.enter_context(nc.sbuf_tensor(f"a_wf{b}", [128, 2048], F32)) for b in range(2)]
            wb = [sx.enter_context(nc.sbuf_tensor(f"a_wb{b}", [128, 2048], BF16)) for b in range(2)]
            qt = [sx.enter_context(nc.sbuf_tensor(f"a_qt{b}", [128, CHUNK], BF16)) for b in range(2)]
            with scope() as px:
                psq = [px.enter_context(nc.psum_tensor(f"a_pq{b}", [128, CHUNK], F32)) for b in range(2)]
                for (w_st, n_e, dst, row0, scale) in (
                        (wk_st, 4, agkv_in, 0, 1.0),
                        (wv_st, 4, agkv_in, 512, 1.0),
                        (wq_st, 16, agq_in, 0, 1.0 / float(np.sqrt(128.0)))):
                    if w_st is wq_st:
                        p.collective("AllGather", A.bypass, BGRP, [agkv_in], [agkv_out],
                                     reads=["agkv_in_d"], writes=["agkv_out_d"])
                    def aprep(e, w_st=w_st):
                        b = e % 2
                        p.dma("sync", f"aw{b}", out=wf[b][:], in_=w_st[e], writes=[f"a_wf{b}"])
                        cast_chunk(wb[b], f"a_wb{b}", wf[b], [f"a_wf{b}"], 2048)

                    aprep(0)
                    for e in range(n_e):
                        b = e % 2
                        if e + 1 < n_e:
                            aprep(e + 1)
                        def qmg(E, b=b):
                            last = None
                            for d in range(FT):
                                for h in range(2):
                                    last = E.matmul(
                                        psq[b][:, h * 512:(h + 1) * 512],
                                        wb[b][:, d * 128:(d + 1) * 128],
                                        xln[:, d * CHUNK + h * 512: d * CHUNK + (h + 1) * 512],
                                        start=(d == 0), stop=(d == FT - 1))
                            return last
                        p.op("tensor", qmg,
                             reads=[f"a_wb{b}"] + [f"xln:{d}" for d in range(FT)],
                             writes=[f"a_pq{b}"])
                        p.op("scalar",
                             lambda E, b=b, scale=scale: E.activation(
                                 out=qt[b][:], in_=psq[b][:], func=AF.Copy, scale=scale),
                             reads=[f"a_pq{b}"], writes=[f"a_qt{b}"])
                        p.dma("scalar", f"aq{b}",
                              out=dst[row0 + e * 128: row0 + (e + 1) * 128, :],
                              in_=qt[b][:], reads=[f"a_qt{b}"],
                              writes=["agkv_in_d" if dst is agkv_in else "agq_in_d"])
        p.collective("AllGather", A.bypass, BGRP, [agq_in], [agq_out],
                     reads=["agq_in_d"], writes=["agq_out_d"])
        with scope() as sx:
            qg = sx.enter_context(nc.sbuf_tensor("at_q", [128, 4 * S], BF16))
            kg = sx.enter_context(nc.sbuf_tensor("at_k", [128, S], BF16))
            vf = sx.enter_context(nc.sbuf_tensor("at_vf", [128, S], BF16))
            vt = sx.enter_context(nc.sbuf_tensor("at_vt", [128, S], BF16))
            idn = sx.enter_context(nc.sbuf_tensor("at_idn", [128, 128], BF16))
            mstr = sx.enter_context(nc.sbuf_tensor("at_m", [128, 32 * 512], BF16))
            dtil = [sx.enter_context(nc.sbuf_tensor(f"at_d{b}", [128, 512], F32)) for b in range(2)]
            stt_ = [sx.enter_context(nc.sbuf_tensor(f"at_s{b}", [128, 1024], F32)) for b in range(3)]
            ptt = [sx.enter_context(nc.sbuf_tensor(f"at_p{b}", [128, 1024], BF16)) for b in range(4)]
            ob = [sx.enter_context(nc.sbuf_tensor(f"at_ob{b}", [128, 512], BF16)) for b in range(2)]
            for j in range(4):
                p.idma("gq", out=kg[:, j * CHUNK:(j + 1) * CHUNK], in_=agkv_out[:, :],
                       idx_ap=gidxs[:, 16 + j:17 + j],
                       reads=["agkv_out_d", "gidxs"], writes=[f"at_k:{j}"])
                p.idma("gq", out=vf[:, j * CHUNK:(j + 1) * CHUNK], in_=agkv_out[:, :],
                       idx_ap=gidxs[:, 20 + j:21 + j],
                       reads=["agkv_out_d", "gidxs"], writes=[f"at_vf:{j}"])
            for r in range(4):
                for j in range(4):
                    p.idma("gq", out=qg[:, (r * 4 + j) * CHUNK:(r * 4 + j + 1) * CHUNK],
                           in_=agq_out[:, :], idx_ap=gidxs[:, r * 4 + j:r * 4 + j + 1],
                           reads=["agq_out_d", "gidxs"], writes=[f"at_q:{r}:{j}"])
            p.op("gpsimd", lambda E: E.memset(idn[:], 1.0), writes=["at_idn"])
            p.op("gpsimd",
                 lambda E: E.affine_select(out=idn[:], in_=idn[:], pattern=[[-1, 128]],
                                           compare_op=A.is_equal, fill=0.0, base=0,
                                           channel_multiplier=1),
                 reads=["at_idn"], writes=["at_idn"])
            with scope() as px:
                pst = [px.enter_context(nc.psum_tensor(f"at_pt{b}", [128, 128], BF16))
                       for b in range(2)]
                for kt in range(32):
                    b = kt % 2
                    p.op("tensor",
                         lambda E, kt=kt, b=b: E.transpose(
                             pst[b][:, :], vf[:, kt * 128:(kt + 1) * 128], idn[:]),
                         reads=[f"at_vf:{kt // 8}", "at_idn"], writes=[f"at_pt{b}"])
                    p.op("scalar",
                         lambda E, kt=kt, b=b: E.copy(
                             out=vt[:, kt * 128:(kt + 1) * 128], in_=pst[b][:, :]),
                         reads=[f"at_pt{b}"], writes=[f"at_vt:{kt}"])
            with scope() as px:
                pssc = px.enter_context(nc.psum_tensor("at_sc", [128, 3072], F32))
                pso = px.enter_context(nc.psum_tensor("at_o", [128, 512], F32))
                psd = px.enter_context(nc.psum_tensor("at_dn", [128, 512], F32))
                dnrow = sx.enter_context(nc.sbuf_tensor("at_dnr", [1, 512], F32))
                for qs in range(8):
                    nkt = 4 * qs + 4
                    for kt in range(nkt):
                        b = kt % 2
                        p.op("gpsimd",
                             lambda E, qs=qs, kt=kt, b=b: E.iota(
                                 dtil[b][:], pattern=[[1, 512]], base=512 * qs - 128 * kt,
                                 channel_multiplier=-1,
                                 allow_small_or_imprecise_dtypes=True),
                             writes=[f"at_d{b}"])
                        p.op("scalar",
                             lambda E, kt=kt, b=b: E.activation(
                                 out=mstr[:, kt * 512:(kt + 1) * 512], in_=dtil[b][:],
                                 func=AF.Sqrt),
                             reads=[f"at_d{b}"], writes=[f"at_m:{kt}"])
                    for r in range(4):
                        def qkpair(kp, r=r, qs=qs):
                            k0 = 2 * kp
                            s0 = (kp % 3) * 1024
                            bp = kp % 4
                            bs = kp % 3
                            def qkg(E, k0=k0, s0=s0, r=r, qs=qs):
                                E.matmul(pssc[:, s0:s0 + 512],
                                         kg[:, k0 * 128:(k0 + 1) * 128],
                                         qg[:, r * S + qs * 512: r * S + (qs + 1) * 512],
                                         start=True, stop=True)
                                return E.matmul(pssc[:, s0 + 512:s0 + 1024],
                                                kg[:, (k0 + 1) * 128:(k0 + 2) * 128],
                                                qg[:, r * S + qs * 512: r * S + (qs + 1) * 512],
                                                start=True, stop=True)
                            p.op("tensor", qkg,
                                 reads=[f"at_k:{k0 // 8}", f"at_k:{(k0 + 1) // 8}",
                                        f"at_q:{r}:{qs // 2}"],
                                 writes=[f"at_scp{bs}"])
                            p.op("vector",
                                 lambda E, k0=k0, s0=s0, r=r, bs=bs: E.scalar_tensor_tensor(
                                     out=stt_[bs][:], in0=mstr[:, k0 * 512:(k0 + 2) * 512],
                                     scalar=slopc[:, r:r + 1], in1=pssc[:, s0:s0 + 1024],
                                     op0=A.mult, op1=A.add),
                                 reads=[f"at_m:{k0}", f"at_m:{k0 + 1}", "slopc", f"at_scp{bs}"],
                                 writes=[f"at_sp{bs}"])
                            p.op("scalar",
                                 lambda E, bp=bp, bs=bs: E.activation(
                                     out=ptt[bp][:], in_=stt_[bs][:], func=AF.Exp,
                                     bias=shiftc[:, 0:1]),
                                 reads=[f"at_sp{bs}", "shiftc"], writes=[f"at_pp{bp}"])
                            for kk in (k0, k0 + 1):
                                if kk >= 4 * qs:
                                    p.op("gpsimd",
                                         lambda E, qs=qs, kk=kk, bp=bp, k0=k0: E.affine_select(
                                             out=ptt[bp][:, (kk - k0) * 512:(kk - k0 + 1) * 512],
                                             in_=ptt[bp][:, (kk - k0) * 512:(kk - k0 + 1) * 512],
                                             pattern=[[1, 512]], compare_op=A.is_ge,
                                             fill=0.0, base=512 * qs - 128 * kk,
                                             channel_multiplier=-1),
                                         reads=[f"at_pp{bp}"], writes=[f"at_pp{bp}"])

                        def avdenpair(kp, nkt=nkt):
                            k0 = 2 * kp
                            bp = kp % 4
                            def avg_(E, k0=k0, bp=bp, nkt=nkt):
                                last = None
                                for kk in (k0, k0 + 1):
                                    E.matmul(pso[:, :], vt[:, kk * 128:(kk + 1) * 128],
                                             ptt[bp][:, (kk - k0) * 512:(kk - k0 + 1) * 512],
                                             start=(kk == 0), stop=(kk == nkt - 1))
                                    last = E.matmul(psd[0:1, :], onescol[:, 0:1],
                                                    ptt[bp][:, (kk - k0) * 512:(kk - k0 + 1) * 512],
                                                    start=(kk == 0), stop=(kk == nkt - 1))
                                return last
                            p.op("tensor", avg_,
                                 reads=[f"at_vt:{k0}", f"at_vt:{k0 + 1}", f"at_pp{bp}",
                                        "onescol"],
                                 writes=["at_pso", "at_psd"])

                        npair = nkt // 2
                        for kp in range(npair + 3):
                            if kp < npair:
                                qkpair(kp)
                            if kp >= 3:
                                avdenpair(kp - 3)
                        bo = r % 2
                        p.op("vector",
                             lambda E: E.tensor_copy(out=dnrow[:], in_=psd[0:1, :]),
                             reads=["at_psd"], writes=["at_dnr"])
                        p.dma("scalar", "aod",
                              out=agd_in[r:r + 1, qs * 512:(qs + 1) * 512],
                              in_=dnrow[:], reads=["at_dnr"], writes=["agd_in_d"])
                        p.op("scalar",
                             lambda E, bo=bo: E.copy(out=ob[bo][:], in_=pso[:, :]),
                             reads=["at_pso"], writes=[f"at_ob{bo}"])
                        p.dma("scalar", f"ao{bo}",
                              out=ago_in[r * 128:(r + 1) * 128, qs * 512:(qs + 1) * 512],
                              in_=ob[bo][:], reads=[f"at_ob{bo}"], writes=["ago_in_d"])
        p.collective("AllGather", A.bypass, BGRP, [ago_in], [ago_out],
                     reads=["ago_in_d"], writes=["ago_out_d"])
        p.collective("AllGather", A.bypass, BGRP, [agd_in], [agd_out],
                     reads=["agd_in_d"], writes=["agd_out_d"])
        with scope() as sx:
            mo = sx.enter_context(nc.sbuf_tensor("ao_m", [128, FT * CHUNK], BF16))
            dng = [sx.enter_context(nc.sbuf_tensor(f"ao_dn{b}", [128, CHUNK], F32)) for b in range(2)]
            ago_v = ago_out.rearrange("a (b c) -> (a b) c", c=CHUNK)
            agd_v = agd_out.rearrange("a (b c) -> (a b) c", c=CHUNK)
            for f in range(FT):
                b = f % 2
                p.idma("go", out=mo[:, f * CHUNK:(f + 1) * CHUNK], in_=ago_v,
                       idx_ap=oidxs[:, f:f + 1],
                       reads=["ago_out_d", "oidxs"], writes=[f"ao_m:{f}"])
                p.idma("go", out=dng[b][:], in_=agd_v, idx_ap=oidxs[:, 16 + f:17 + f],
                       reads=["agd_out_d", "oidxs"], writes=[f"ao_dn{b}"])
                p.op("vector",
                     lambda E, b=b: E.reciprocal(out=dng[b][:], in_=dng[b][:]),
                     reads=[f"ao_dn{b}"], writes=[f"ao_dn{b}"])
                p.op("vector",
                     lambda E, f=f, b=b: E.tensor_mul(
                         out=mo[:, f * CHUNK:(f + 1) * CHUNK],
                         in0=mo[:, f * CHUNK:(f + 1) * CHUNK], in1=dng[b][:]),
                     reads=[f"ao_m:{f}", f"ao_dn{b}"], writes=[f"ao_m:{f}"])
            proj_residual(sx, wo_st, 16, FT, "ao_m", mo, xold_src, xold_key,
                          xnew_dst, xnew_key)

    if upto >= 1:
        phase_l0_mixer()
    cur, cur_key = (xhbm[0], "xh0") if upto >= 1 else (x_in, None)
    if upto >= 2:
        phase_ffn(cur, cur_key, xhbm[1], "xh1", wu_st, wg0_st, wd_st, ET0, 0, 1, 2)
        cur, cur_key = xhbm[1], "xh1"
    if upto >= 3:
        phase_attn(cur, cur_key, xhbm[0], "xh0")
        cur, cur_key = xhbm[0], "xh0"
    if upto >= 4:
        phase_ffn(cur, cur_key, x_out, "xout", w1_st, w3_st, w2_st, ET1, 3, 4, 5)
    else:
        with scope() as sx:
            t = [sx.enter_context(nc.sbuf_tensor(f"cp{b}", [128, CHUNK], F32)) for b in range(2)]
            for f in range(FT):
                b = f % 2
                p.dma("scalar", f"cpi{b}", out=t[b][:], in_=cur[f * 128:(f + 1) * 128, :],
                      reads=([f"{cur_key}:{f}"] if cur_key else []), writes=[f"cpt{b}"])
                p.dma("scalar", f"cpo{b}", out=x_out[f * 128:(f + 1) * 128, :], in_=t[b][:],
                      reads=[f"cpt{b}"], writes=[f"xout:{f}"])
    p.fence_all()
    p.finalize()
    return nc, p


def _wst(w, n_d, n_e):
    return np.ascontiguousarray(
        w.reshape(n_d, 128, n_e, 128).transpose(2, 1, 0, 3).reshape(n_e, 128, n_d * 128),
        dtype=np.float32)


def _col16(v):
    return np.ascontiguousarray(np.asarray(v, np.float32).reshape(16, 128).T)


def _host_inputs(inputs):
    f32 = lambda k: np.asarray(inputs[k], np.float32)
    x = f32("L0_attn_x")
    win, wdec, wgat, wout = (f32("L0_attn_W_in"), f32("L0_attn_W_decay"),
                             f32("L0_attn_W_gate"), f32("L0_attn_W_out"))
    wu, wg0, wd = f32("L0_ffn_W_up"), f32("L0_ffn_W_gate"), f32("L0_ffn_W_down")
    wq, wk, wv, wo = (f32("L1_attn_Wq"), f32("L1_attn_Wk"), f32("L1_attn_Wv"),
                      f32("L1_attn_Wo"))
    w1, w3, w2 = f32("L1_ffn_w1"), f32("L1_ffn_w3"), f32("L1_ffn_w2")

    win_s, wdec_s, wgat_s = _wst(win, 16, 16), _wst(wdec, 16, 16), _wst(wgat, 16, 16)
    wcat = np.empty((48, 128, 2048), np.float32)
    for f in range(16):
        wcat[3 * f + 0] = win_s[f]
        wcat[3 * f + 1] = wdec_s[f]
        wcat[3 * f + 2] = wgat_s[f]
    bmix = np.concatenate([_col16(inputs["L0_attn_b_in"]),
                           _col16(inputs["L0_attn_b_decay"]),
                           _col16(inputs["L0_attn_b_gate"]),
                           _col16(inputs["L0_attn_b_out"])], axis=1)
    w1p = np.zeros((2048, H1P), np.float32); w1p[:, :H1] = w1.T
    w3p = np.zeros((2048, H1P), np.float32); w3p[:, :H1] = w3.T
    w2p = np.zeros((H1P, 2048), np.float32); w2p[:H1, :] = w2.T

    shared = dict(
        wcat_st=wcat, bmix=bmix, dbias=_col16(inputs["L0_attn_decay_bias"]),
        wout_st=_wst(wout, 16, 16),
        wu_st=_wst(wu.T, 16, 64), wg0_st=_wst(wg0.T, 16, 64), wd_st=_wst(wd.T, 64, 16),
        wq_st=_wst(wq.T, 16, 16), wk_st=_wst(wk.T, 16, 4), wv_st=_wst(wv.T, 16, 4),
        wo_st=_wst(wo.T, 16, 16),
        w1_st=_wst(w1p, 16, 43), w3_st=_wst(w3p, 16, 43), w2_st=_wst(w2p, 43, 16))

    qflats = [wu.ravel(), wg0.ravel(), wd.ravel(), w1.ravel(), w3.ravel(), w2.ravel()]
    qrcp = np.zeros((1, 16), np.float32)
    qrcp[0, :6] = [np.float32(1.0) / np.float32(q.size) for q in qflats]
    qeps = np.zeros((1, 16), np.float32)
    qeps[0, :3] = 1e-8
    shared["qrcp"] = qrcp
    shared["qeps"] = qeps

    slopes = (2.0 ** (-0.5 * np.arange(1, 17))).astype(np.float32)
    prng = np.arange(128, dtype=np.int32)
    in_maps = []
    for k in range(NCORE):
        b, c = k // 4, k % 4
        g = c
        xs = np.ascontiguousarray(x[b, c * CHUNK:(c + 1) * CHUNK, :].T, dtype=np.float32)
        ns = np.repeat(-slopes[4 * g:4 * g + 4][None, :], 128, axis=0)
        zs = np.zeros((128, 4), np.float32)
        zs[:, c] = 1.0
        gi = np.zeros((128, 24), np.int32)
        for r in range(4):
            for j in range(4):
                gi[:, r * 4 + j] = j * 2048 + 512 * g + 128 * r + prng
        for j in range(4):
            gi[:, 16 + j] = j * 1024 + 128 * g + prng
            gi[:, 20 + j] = j * 1024 + 512 + 128 * g + prng
        oi = np.zeros((128, 32), np.int32)
        for f in range(FT):
            oi[:, f] = (128 * f + prng) * 4 + c
            oi[:, 16 + f] = f * 4 + c
        qsh = np.concatenate(
            [q[k * (q.size // 8):(k + 1) * (q.size // 8)].reshape(128, -1) for q in qflats],
            axis=1).astype(np.float32)
        m = dict(shared)
        m.update(x_in=xs, negslopes=np.ascontiguousarray(ns), zsel=zs,
                 gidx=np.ascontiguousarray(gi), oidx=np.ascontiguousarray(oi),
                 qshard=np.ascontiguousarray(qsh))
        in_maps.append(m)
    return in_maps


def kernel(**inputs):
    in_maps = _host_inputs(inputs)
    nc, _ = build_program(upto=4)
    res = run_bass_kernel_spmd(nc, in_maps, core_ids=list(range(NCORE)))
    out = np.empty((B, S, D), np.float32)
    for k in range(NCORE):
        b, c = k // 4, k % 4
        out[b, c * CHUNK:(c + 1) * CHUNK, :] = res.results[k]["x_out"].T
    return out


# version 10
# speedup vs baseline: 1.0380x; 1.0250x over previous
import numpy as np
from contextlib import contextmanager
import concourse.bass as bass
import concourse.mybir as mybir
from concourse.bass_utils import run_bass_kernel_spmd
from contextlib import ExitStack

F32 = mybir.dt.float32
BF16 = mybir.dt.bfloat16
I32 = mybir.dt.int32
A = mybir.AluOpType
AF = mybir.ActivationFunctionType
AX = mybir.AxisListType

B, S, D = 2, 4096, 2048
NCORE = 8
CHUNK = 1024
FT = D // 128
H0, ET0 = 8192, 64
H1, H1P = 5461, 5504
ET1 = H1P // 128
EPS_LN = 1e-5
SHIFT = 16.0

ENGINES = ("tensor", "vector", "scalar", "gpsimd", "sync")


class _Sem:
    __slots__ = ("handle", "count", "snapshots", "name")

    def __init__(self, handle, name):
        self.handle = handle
        self.count = 0
        self.name = name
        self.snapshots = {}


class Dep:
    __slots__ = ("sem", "value")

    def __init__(self, sem, value):
        self.sem = sem
        self.value = value


class Prog:

    def __init__(self, nc, ctx):
        self.nc = nc
        self.ctx = ctx
        self.ops = {e: [] for e in ENGINES}
        self.esem = {e: _Sem(ctx.enter_context(nc.semaphore(f"es_{e}")), f"es_{e}")
                     for e in ENGINES}
        self.streams = {}
        self.cc_sem = _Sem(ctx.enter_context(nc.semaphore("cc")), "cc")
        self.clock = {e: {} for e in ENGINES}
        self.last_writer = {}
        self.readers = {}
        self.n_waits = 0
        self.n_bumped = 0

    def _stream(self, name):
        if name not in self.streams:
            h = self.ctx.enter_context(self.nc.semaphore(f"ds_{name}"))
            self.streams[name] = _Sem(h, f"ds_{name}")
        return self.streams[name]

    def _wait(self, eng, dep):
        sem, val = dep.sem, dep.value
        if self.clock[eng].get(sem.name, 0) >= val:
            return
        if sem.name.startswith("ds_") and val < sem.count:
            val = sem.count
            self.n_bumped += 1
        snap = sem.snapshots.get(val)
        c = self.clock[eng]
        c[sem.name] = max(c.get(sem.name, 0), val)
        if snap:
            for k, v in snap.items():
                if c.get(k, 0) < v:
                    c[k] = v
        h = sem.handle
        self.ops[eng].append(lambda E, h=h, val=val: E.wait_ge(h, val))
        self.n_waits += 1

    def _resolve(self, eng, reads, writes):
        for dl in getattr(self, "barrier_deps", ()):
            for d in dl:
                self._wait(eng, d)
        skip_own = (eng == "tensor")
        own = f"es_{eng}"
        for r in reads:
            lw = self.last_writer.get(r)
            if lw is not None and not (skip_own and lw.sem.name == own):
                self._wait(eng, lw)
        for w in writes:
            lw = self.last_writer.get(w)
            if lw is not None and not (skip_own and lw.sem.name == own):
                self._wait(eng, lw)
            for rd in self.readers.get(w, ()):
                if not (skip_own and rd.sem.name == own):
                    self._wait(eng, rd)

    def _commit(self, dep, reads, writes):
        for w in writes:
            self.last_writer[w] = dep
            self.readers[w] = []
        for r in reads:
            self.readers.setdefault(r, []).append(dep)

    def op(self, eng, fn, reads=(), writes=()):
        reads, writes = list(reads), list(writes)
        self._resolve(eng, reads, writes)
        sem = self.esem[eng]
        sem.count += 1
        val, h = sem.count, sem.handle

        def run(E, fn=fn, h=h):
            inst = fn(E)
            assert inst is not None, "op fn must return its last instruction"
            inst.then_inc(h, 1)

        self.ops[eng].append(run)
        dep = Dep(sem, val)
        snap = dict(self.clock[eng])
        snap[sem.name] = val
        sem.snapshots[val] = snap
        self._commit(dep, reads, writes)
        return dep

    def dma(self, eng, stream, out, in_, reads=(), writes=()):
        reads, writes = list(reads), list(writes)
        self._resolve(eng, reads, writes)
        s = self._stream(stream)
        s.count += 16
        val, h = s.count, s.handle

        def run(E, out=out, in_=in_, h=h):
            E.dma_start(out=out, in_=in_).then_inc(h, 16)

        self.ops[eng].append(run)
        dep = Dep(s, val)
        s.snapshots[val] = dict(self.clock[eng])
        self._commit(dep, reads, writes)
        return dep

    def idma(self, stream, out, in_, idx_ap, reads=(), writes=()):
        reads, writes = list(reads), list(writes)
        self._resolve("gpsimd", reads, writes)
        s = self._stream(stream)
        s.count += 16
        val, h = s.count, s.handle

        def run(E, out=out, in_=in_, idx_ap=idx_ap, h=h):
            E.indirect_dma_start(
                out=out, out_offset=None, in_=in_,
                in_offset=bass.IndirectOffsetOnAxis(ap=idx_ap, axis=0),
            ).then_inc(h, 16)

        self.ops["gpsimd"].append(run)
        dep = Dep(s, val)
        s.snapshots[val] = dict(self.clock["gpsimd"])
        self._commit(dep, reads, writes)
        return dep

    def collective(self, kind, op_, groups, ins, outs, reads=(), writes=()):
        reads, writes = list(reads), list(writes)
        self._resolve("gpsimd", reads, writes)
        s = self.cc_sem
        s.count += 1
        val, h = s.count, s.handle

        def run(E, kind=kind, op_=op_, rg=groups, ins=ins, outs=outs, h=h):
            E.collective_compute(kind, op_, replica_groups=rg, ins=ins,
                                 outs=outs).then_inc(h, 1)

        self.ops["gpsimd"].append(run)
        dep = Dep(s, val)
        s.snapshots[val] = dict(self.clock["gpsimd"])
        self._commit(dep, reads, writes)
        return dep

    def fence_all(self):
        deps = list(self.last_writer.values())
        for e in ENGINES:
            for d in deps:
                self._wait(e, d)

    def barrier(self):
        deps = list(self.last_writer.values())
        for keydeps in self.readers.values():
            deps.extend(keydeps)
        self.barrier_deps = [deps]

    def finalize(self):
        block = self.ctx.enter_context(self.nc.Block())
        for e in ENGINES:
            ops = self.ops[e]
            if not ops:
                continue

            def body(E, ops=ops):
                for f in ops:
                    f(E)

            getattr(block, e)(body)
        return block


def build_program(upto=4):
    nc = bass.Bass("TRN2", target_bir_lowering=False)
    ctx = ExitStack()
    p = Prog(nc, ctx)
    sb = ctx.enter_context

    def din(name, shape, dt=F32):
        return nc.dram_tensor(name, list(shape), dt, kind="ExternalInput").ap()

    def dint(name, shape, dt=F32):
        return nc.dram_tensor(name, list(shape), dt, kind="Internal").ap()

    x_in = din("x_in", (D, CHUNK))
    negslopes = din("negslopes", (128, 4))
    zsel = din("zsel", (128, 4))
    qshard = din("qshard", (128, 81918))
    qrcp = din("qrcp", (1, 16))
    qeps = din("qeps", (1, 16))
    gidx = din("gidx", (128, 24), I32)
    oidx = din("oidx", (128, 32), I32)
    wcat_st = din("wcat_st", (48, 128, 2048))
    bmix = din("bmix", (128, 64))
    dbias_in = din("dbias", (128, 16))
    wout_st = din("wout_st", (16, 128, 2048))
    wu_st = din("wu_st", (64, 128, 2048))
    wg0_st = din("wg0_st", (64, 128, 2048))
    wd_st = din("wd_st", (16, 128, 8192))
    wq_st = din("wq_st", (16, 128, 2048))
    wk_st = din("wk_st", (4, 128, 2048))
    wv_st = din("wv_st", (4, 128, 2048))
    wo_st = din("wo_st", (16, 128, 2048))
    w1_st = din("w1_st", (43, 128, 2048))
    w3_st = din("w3_st", (43, 128, 2048))
    w2_st = din("w2_st", (16, 128, 5504))
    x_out = nc.dram_tensor("x_out", [D, CHUNK], F32, kind="ExternalOutput").ap()

    xhbm = [dint("xhbm0", (D, CHUNK)), dint("xhbm1", (D, CHUNK))]
    hhbm = dint("hhbm", (H0, CHUNK), BF16)
    qdh = dint("qdh", (16, 128, 8192), BF16)
    ags_in = dint("ags_in", (1, 16))
    ags_out = dint("ags_out", (8, 16))
    agz_in = dint("agz_in", (128, 32))
    agz_out = dint("agz_out", (512, 32))
    agkv_in = dint("agkv_in", (1024, CHUNK), BF16)
    agkv_out = dint("agkv_out", (4096, CHUNK), BF16)
    agq_in = dint("agq_in", (2048, CHUNK), BF16)
    agq_out = dint("agq_out", (2, 4096, CHUNK), BF16)
    ago_in = [dint("ago_inA", (512, S // 2), BF16), dint("ago_inB", (512, S // 2), BF16)]
    ago_out = dint("ago_out", (2, 2048, S // 2), BF16)
    agd_in = [dint("agd_inA", (4, S // 2)), dint("agd_inB", (4, S // 2))]
    agd_out = dint("agd_out", (2, 16, S // 2))

    ALL8 = [list(range(8))]
    BGRP = [[0, 1, 2, 3], [4, 5, 6, 7]]

    @contextmanager
    def scope():
        sx = ExitStack()
        try:
            yield sx
        finally:
            sx.close()
            p.barrier()

    xln = sb(nc.sbuf_tensor("xln", [128, FT * CHUNK], BF16))
    biasc = sb(nc.sbuf_tensor("biasc", [128, 64], F32))
    bdec = sb(nc.sbuf_tensor("bdec", [128, 16], F32))
    dbc = sb(nc.sbuf_tensor("dbc", [128, 16], F32))
    slopc = sb(nc.sbuf_tensor("slopc", [128, 4], F32))
    zselc = sb(nc.sbuf_tensor("zselc", [128, 4], F32))
    onescol = sb(nc.sbuf_tensor("onescol", [128, 1], BF16))
    shiftc = sb(nc.sbuf_tensor("shiftc", [128, 1], F32))

    scol = sb(nc.sbuf_tensor("scol", [128, 16], F32))
    rscol = sb(nc.sbuf_tensor("rscol", [128, 16], F32))
    statrow = sb(nc.sbuf_tensor("statrow", [1, 4 * CHUNK], F32))
    rrep = sb(nc.sbuf_tensor("rrep", [128, CHUNK], F32))
    murep = sb(nc.sbuf_tensor("murep", [128, CHUNK], F32))
    gidxs = sb(nc.sbuf_tensor("gidxs", [128, 24], I32))
    lnx = sb(nc.sbuf_tensor("lnx", [128, 4 * CHUNK], F32))
    lnsq = [sb(nc.sbuf_tensor(f"lnsq{b}", [128, CHUNK], BF16)) for b in range(2)]
    lntmp = sb(nc.sbuf_tensor("lntmp", [128, CHUNK], F32))
    oidxs = sb(nc.sbuf_tensor("oidxs", [128, 32], I32))

    ones1 = sb(nc.sbuf_tensor("ones1", [1, 128], F32))
    p.op("gpsimd", lambda E: E.memset(ones1[:], 1.0), writes=["ones1"])
    p.op("gpsimd", lambda E: E.memset(onescol[:], 1.0), writes=["onescol"])
    p.op("gpsimd", lambda E: E.memset(shiftc[:], -SHIFT), writes=["shiftc"])

    p.dma("sync", "cst", out=biasc[:], in_=bmix, writes=["biasc"])
    p.dma("sync", "cst", out=slopc[:], in_=negslopes, writes=["slopc"])
    p.dma("sync", "cst", out=zselc[:], in_=zsel, writes=["zselc"])
    p.dma("sync", "cst", out=dbc[:], in_=dbias_in, writes=["dbc"])
    p.dma("sync", "cst", out=gidxs[:], in_=gidx, writes=["gidxs"])
    p.dma("sync", "cst", out=oidxs[:], in_=oidx, writes=["oidxs"])
    p.op("vector", lambda E: E.tensor_add(out=bdec[:], in0=biasc[:, 16:32], in1=dbc[:]),
         reads=["biasc", "dbc"], writes=["bdec"])

    QN = [16384, 16384, 16384, 10922, 10922, 10922]
    QCNT = [float(H0 * D)] * 3 + [float(H1 * D)] * 3

    def qscale_make(sx):
        if True:
            qs = sx.enter_context(nc.sbuf_tensor("q_s", [128, 16384], F32))
            part = sx.enter_context(nc.sbuf_tensor("q_part", [128, 16], F32))
            part6 = sx.enter_context(nc.sbuf_tensor("q_p6", [128, 8], F32))
            ssum = sx.enter_context(nc.sbuf_tensor("q_ssum", [1, 16], F32))
            ag8 = sx.enter_context(nc.sbuf_tensor("q_ag8", [8, 16], F32))
            o8 = sx.enter_context(nc.sbuf_tensor("q_o8", [128, 1], F32))
            qr = sx.enter_context(nc.sbuf_tensor("q_qr", [1, 16], F32))
            qe = sx.enter_context(nc.sbuf_tensor("q_qe", [1, 16], F32))
            srow = sx.enter_context(nc.sbuf_tensor("q_srow", [1, 16], F32))
            rs16 = sx.enter_context(nc.sbuf_tensor("q_rs16", [1, 16], F32))
            pp = sx.enter_context(nc.psum_tensor("q_pp", [128, 512], F32))
            p.op("gpsimd", lambda E: E.memset(ssum[:], 0.0), writes=["q_ssum"])
            p.op("gpsimd", lambda E: E.memset(o8[:], 1.0), writes=["q_o8"])
            p.dma("sync", "cst", out=qr[:], in_=qrcp, writes=["q_qr"])
            p.dma("sync", "cst", out=qe[:], in_=qeps, writes=["q_qe"])
            units = []
            off = 0
            j = 0
            ranges = []
            for m, ncols in enumerate(QN):
                a0 = j
                done = 0
                while done < ncols:
                    n = min(8192, ncols - done)
                    units.append((j, off, n))
                    off += n
                    done += n
                    j += 1
                ranges.append((a0, j))

            def unit(uix):
                j, off, n = units[uix]
                lo = (j % 2) * 8192
                p.dma("sync", f"q{j % 2}", out=qs[:, lo:lo + n],
                      in_=qshard[:, off:off + n], writes=[f"q_s:{j % 2}"])
                p.op("vector",
                     lambda E, n=n, j=j, lo=lo: E.reduce_sum(
                         out=part[:, j:j + 1], in_=qs[:, lo:lo + n], axis=AX.X,
                         apply_absolute_value=True),
                     reads=[f"q_s:{j % 2}"], writes=[f"q_part:{j}"])

        def finalize():
            for m, (a0, a1) in enumerate(ranges):
                p.op("vector",
                     lambda E, m=m, a0=a0, a1=a1: E.reduce_sum(
                         out=part6[:, m:m + 1], in_=part[:, a0:a1], axis=AX.X),
                     reads=[f"q_part:{jj}" for jj in range(a0, a1)], writes=["q_p6"])
            p.op("tensor",
                 lambda E: E.matmul(pp[0:1, 0:6], o8[:, 0:1], part6[:, 0:6],
                                    start=True, stop=True),
                 reads=["q_p6", "q_o8"], writes=["q_pp"])
            p.op("vector", lambda E: E.tensor_copy(out=ssum[0:1, 0:6], in_=pp[0:1, 0:6]),
                 reads=["q_pp"], writes=["q_ssum"])
            p.dma("gpsimd", "agb", out=ags_in, in_=ssum[:], reads=["q_ssum"],
                  writes=["ags_in_d"])
            p.collective("AllGather", A.bypass, ALL8, [ags_in], [ags_out],
                         reads=["ags_in_d"], writes=["ags_out_d"])
            p.dma("gpsimd", "agb", out=ag8[:], in_=ags_out, reads=["ags_out_d"],
                  writes=["q_ag8"])
            p.op("tensor",
                 lambda E: E.matmul(pp[0:1, 16:32], o8[0:8, 0:1], ag8[:, :],
                                    start=True, stop=True),
                 reads=["q_ag8", "q_o8"], writes=["q_pp"])
            p.op("vector", lambda E: E.tensor_mul(out=srow[:], in0=pp[0:1, 16:32], in1=qr[:]),
                 reads=["q_pp", "q_qr"], writes=["q_srow"])
            p.op("vector", lambda E: E.tensor_add(out=srow[:], in0=srow[:], in1=qe[:]),
                 reads=["q_srow", "q_qe"], writes=["q_srow"])
            p.op("vector", lambda E: E.reciprocal(out=rs16[:], in_=srow[:]),
                 reads=["q_srow"], writes=["q_rs16"])
            p.op("tensor",
                 lambda E: E.matmul(pp[:, 32:48], ones1[0:1, :], srow[0:1, :],
                                    start=True, stop=True),
                 reads=["q_srow", "ones1"], writes=["q_pp"])
            p.op("scalar", lambda E: E.copy(out=scol[:, :], in_=pp[:, 32:48]),
                 reads=["q_pp"], writes=["scol"])
            p.op("tensor",
                 lambda E: E.matmul(pp[:, 48:64], ones1[0:1, :], rs16[0:1, :],
                                    start=True, stop=True),
                 reads=["q_rs16", "ones1"], writes=["q_pp"])
            p.op("scalar", lambda E: E.copy(out=rscol[:, :], in_=pp[:, 48:64]),
                 reads=["q_pp"], writes=["rscol"])
        return len(units), unit, finalize

    def replicate_row(ps_rep, ps_key, row_ap, row_key, dst, dst_key, n):
        for h0 in range(0, n, 512):
            w = min(512, n - h0)
            p.op("tensor",
                 lambda E, h0=h0, w=w: E.matmul(
                     ps_rep[:, 0:w], ones1[0:1, :], row_ap[0:1, h0:h0 + w],
                     start=True, stop=True),
                 reads=[row_key, "ones1"], writes=[ps_key])
            p.op("scalar",
                 lambda E, h0=h0, w=w: E.copy(out=dst[:, h0:h0 + w], in_=ps_rep[:, 0:w]),
                 reads=[ps_key], writes=[dst_key])

    def cast_chunk(dst, dst_key, src, src_keys, cols):
        p.op("scalar",
             lambda E: E.activation(out=dst[:, 0:cols], in_=src[:, 0:cols], func=AF.Copy),
             reads=src_keys, writes=[dst_key])

    def quant_chunk(dst, dst_key, src, src_keys, cols, mq, tmp, tmp_key):
        p.op("vector",
             lambda E: E.tensor_scalar(out=dst[:, 0:cols], in0=src[:, 0:cols],
                                       scalar1=rscol[:, mq:mq + 1], scalar2=0.5,
                                       op0=A.mult, op1=A.is_gt),
             reads=list(src_keys) + ["rscol"], writes=[dst_key])
        p.op("vector",
             lambda E: E.tensor_scalar(out=tmp[:, 0:cols], in0=src[:, 0:cols],
                                       scalar1=rscol[:, mq:mq + 1], scalar2=-0.5,
                                       op0=A.mult, op1=A.is_lt),
             reads=list(src_keys) + ["rscol"], writes=[tmp_key])
        p.op("vector",
             lambda E: E.tensor_sub(out=dst[:, 0:cols], in0=dst[:, 0:cols],
                                    in1=tmp[:, 0:cols]),
             reads=[dst_key, tmp_key], writes=[dst_key])

    lncnt = [0]

    def layernorm(x_src, xsrc_key, sx):
        i = lncnt[0]
        lncnt[0] += 1
        sq, tmp = lnsq, lntmp
        with scope() as px:
            psx = [px.enter_context(nc.psum_tensor(f"ln{i}_px{h}", [128, 512], F32))
                   for h in range(2)]
            psq = [px.enter_context(nc.psum_tensor(f"ln{i}_pq{h}", [128, 512], F32))
                   for h in range(2)]
            psr = px.enter_context(nc.psum_tensor(f"ln{i}_pr", [128, 512], F32))
            for f in range(FT):
                s4 = f % 4
                p.dma("scalar", f"lnx{f % 3}", out=lnx[:, s4 * CHUNK:(s4 + 1) * CHUNK],
                      in_=x_src[f * 128:(f + 1) * 128, :],
                      reads=([f"{xsrc_key}:{f}"] if xsrc_key else []),
                      writes=[f"lnx:{s4}"])
                p.op("vector",
                     lambda E, f=f, s4=s4: E.tensor_copy(out=xln[:, f * CHUNK:(f + 1) * CHUNK],
                                                         in_=lnx[:, s4 * CHUNK:(s4 + 1) * CHUNK]),
                     reads=[f"lnx:{s4}"], writes=[f"xln:{f}"])
                b = f % 2
                p.op("scalar",
                     lambda E, f=f, b=b: E.activation(
                         out=sq[b][:], in_=xln[:, f * CHUNK:(f + 1) * CHUNK],
                         func=AF.Square),
                     reads=[f"xln:{f}"], writes=[f"lnsq:{b}"])
                def statg(E, f=f, b=b):
                    last = None
                    for h in range(2):
                        E.matmul(psx[h][0:1, :], onescol[:, 0:1],
                                 xln[:, f * CHUNK + h * 512: f * CHUNK + (h + 1) * 512],
                                 start=(f == 0), stop=(f == FT - 1))
                        last = E.matmul(psq[h][0:1, :], onescol[:, 0:1],
                                        sq[b][:, h * 512:(h + 1) * 512],
                                        start=(f == 0), stop=(f == FT - 1))
                    return last
                p.op("tensor", statg,
                     reads=[f"xln:{f}", f"lnsq:{b}", "onescol"],
                     writes=["ln_px0", "ln_px1", "ln_pq0", "ln_pq1"])
            mu = statrow[:, 0:CHUNK]
            e2 = statrow[:, CHUNK:2 * CHUNK]
            veps = statrow[:, 2 * CHUNK:3 * CHUNK]
            sd = statrow[:, 3 * CHUNK:4 * CHUNK]
            rr = statrow[:, 2 * CHUNK:3 * CHUNK]
            mr = statrow[:, 3 * CHUNK:4 * CHUNK]
            for h in range(2):
                p.op("vector",
                     lambda E, h=h: E.tensor_scalar(
                         out=mu[:, h * 512:(h + 1) * 512], in0=psx[h][0:1, :],
                         scalar1=1.0 / D, scalar2=None, op0=A.mult),
                     reads=[f"ln_px{h}"], writes=["ln_mu"])
                p.op("vector",
                     lambda E, h=h: E.tensor_scalar(
                         out=e2[:, h * 512:(h + 1) * 512], in0=psq[h][0:1, :],
                         scalar1=1.0 / D, scalar2=EPS_LN, op0=A.mult, op1=A.add),
                     reads=[f"ln_pq{h}"], writes=["ln_e2"])
            p.op("vector", lambda E: E.tensor_mul(out=veps[:], in0=mu[:], in1=mu[:]),
                 reads=["ln_mu"], writes=["ln_veps"])
            p.op("vector", lambda E: E.tensor_sub(out=veps[:], in0=e2[:], in1=veps[:]),
                 reads=["ln_e2", "ln_veps"], writes=["ln_veps"])
            p.op("scalar", lambda E: E.activation(out=sd[:], in_=veps[:], func=AF.Sqrt),
                 reads=["ln_veps"], writes=["ln_sd"])
            p.op("vector", lambda E: E.reciprocal(out=rr[:], in_=sd[:]),
                 reads=["ln_sd"], writes=["ln_rr", "ln_veps"])
            p.op("vector", lambda E: E.tensor_mul(out=mr[:], in0=mu[:], in1=rr[:]),
                 reads=["ln_mu", "ln_rr"], writes=["ln_mr", "ln_sd"])
            replicate_row(psr, "ln_psr", rr, "ln_rr", rrep, "rrep", CHUNK)
            replicate_row(psr, "ln_psr", mr, "ln_mr", murep, "murep", CHUNK)
            for f in range(FT):
                s4 = f % 4
                p.dma("scalar", f"lnx{f % 3}", out=lnx[:, s4 * CHUNK:(s4 + 1) * CHUNK],
                      in_=x_src[f * 128:(f + 1) * 128, :],
                      reads=([f"{xsrc_key}:{f}"] if xsrc_key else []),
                      writes=[f"lnx:{s4}"])
                p.op("vector",
                     lambda E, f=f, s4=s4: E.tensor_mul(
                         out=tmp[:], in0=lnx[:, s4 * CHUNK:(s4 + 1) * CHUNK], in1=rrep[:]),
                     reads=[f"lnx:{s4}", "rrep"], writes=["ln_tmp"])
                p.op("vector",
                     lambda E, f=f: E.tensor_sub(
                         out=xln[:, f * CHUNK:(f + 1) * CHUNK], in0=tmp[:], in1=murep[:]),
                     reads=["ln_tmp", "murep"], writes=[f"xln:{f}"])

    prcnt = [0]

    def proj_residual(sx, w_st, n_e, n_d, mov_key, mov_buf, xold_src, xold_key,
                      xnew_dst, xnew_key, bias_off=None, quant_m=None, filler=None):
        i = prcnt[0]
        prcnt[0] += 1
        cc = n_d * 128
        wf = [sx.enter_context(nc.sbuf_tensor(f"pr{i}_wf{b}", [128, cc], F32)) for b in range(2)]
        wb = [sx.enter_context(nc.sbuf_tensor(f"pr{i}_wb{b}", [128, cc], BF16)) for b in range(2)]
        xo = [sx.enter_context(nc.sbuf_tensor(f"pr{i}_xo{b}", [128, CHUNK], F32)) for b in range(2)]
        xn = [sx.enter_context(nc.sbuf_tensor(f"pr{i}_xn{b}", [128, CHUNK], F32)) for b in range(2)]
        with scope() as px:
            pse = [px.enter_context(nc.psum_tensor(f"pr{i}_ps{b}", [128, CHUNK], F32))
                   for b in range(2)]
            assert quant_m is None

            def prep(e):
                b = e % 2
                p.dma("sync", f"prw{b}", out=wf[b][:], in_=w_st[e], writes=[f"pr_wf{b}"])
                cast_chunk(wb[b], f"pr_wb{b}", wf[b], [f"pr_wf{b}"], cc)
                p.dma("scalar", f"prx{b}", out=xo[b][:],
                      in_=xold_src[e * 128:(e + 1) * 128, :],
                      reads=([f"{xold_key}:{e}"] if xold_key else []),
                      writes=[f"pr_xo{b}"])

            prep(0)
            for e in range(n_e):
                b = e % 2
                if e + 1 < n_e:
                    prep(e + 1)
                if filler is not None:
                    filler(e)
                def mmgrp(E, b=b):
                    last = None
                    for d in range(n_d):
                        for h in range(2):
                            last = E.matmul(
                                pse[b][:, h * 512:(h + 1) * 512],
                                wb[b][:, d * 128:(d + 1) * 128],
                                mov_buf[:, d * CHUNK + h * 512: d * CHUNK + (h + 1) * 512],
                                start=(d == 0), stop=(d == n_d - 1))
                    return last
                p.op("tensor", mmgrp,
                     reads=[f"pr_wb{b}"] + [f"{mov_key}:{d}" for d in range(n_d)],
                     writes=[f"pr_ps{b}"])
                if quant_m is not None:
                    p.op("vector",
                         lambda E, b=b, m=quant_m: E.scalar_tensor_tensor(
                             out=xn[b][:], in0=pse[b][:], scalar=scol[:, m:m + 1],
                             in1=xo[b][:], op0=A.mult, op1=A.add),
                         reads=[f"pr_ps{b}", "scol", f"pr_xo{b}"], writes=[f"pr_xn{b}"])
                elif bias_off is not None:
                    p.op("vector",
                         lambda E, b=b, e=e: E.scalar_tensor_tensor(
                             out=xn[b][:], in0=pse[b][:],
                             scalar=biasc[:, bias_off + e:bias_off + e + 1],
                             in1=xo[b][:], op0=A.add, op1=A.add),
                         reads=[f"pr_ps{b}", "biasc", f"pr_xo{b}"], writes=[f"pr_xn{b}"])
                else:
                    p.op("vector",
                         lambda E, b=b: E.tensor_add(out=xn[b][:], in0=pse[b][:], in1=xo[b][:]),
                         reads=[f"pr_ps{b}", f"pr_xo{b}"], writes=[f"pr_xn{b}"])
                p.dma("scalar", f"pro{b}", out=xnew_dst[e * 128:(e + 1) * 128, :],
                      in_=xn[b][:], reads=[f"pr_xn{b}"], writes=[f"{xnew_key}:{e}"])

    def phase_l0_mixer():
        with scope() as sx:
            layernorm(x_in, None, sx)
        with scope() as sx:
            ubuf = sx.enter_context(nc.sbuf_tensor("m_u", [128, FT * CHUNK], BF16))
            vbuf = sx.enter_context(nc.sbuf_tensor("m_v", [128, FT * CHUNK], BF16))
            wf = [sx.enter_context(nc.sbuf_tensor(f"m_wf{j}", [128, 2048], F32)) for j in range(3)]
            wb = [sx.enter_context(nc.sbuf_tensor(f"m_wb{j}", [128, 2048], BF16)) for j in range(6)]
            td = [sx.enter_context(nc.sbuf_tensor(f"m_td{b}", [128, CHUNK], F32)) for b in range(1)]
            ti = sx.enter_context(nc.sbuf_tensor("m_ti", [128, CHUNK], F32))
            tb = sx.enter_context(nc.sbuf_tensor("m_tb", [128, CHUNK], F32))
            zacc = sx.enter_context(nc.sbuf_tensor("m_za", [128, 32], F32))

            with scope() as px:
                ps = {n: px.enter_context(nc.psum_tensor(f"m_ps_{n}", [128, CHUNK], F32))
                      for n in ("inp", "dec", "gate")}
                def mprep(f):
                    par = (f % 2) * 3
                    for j in range(3):
                        p.dma("sync", f"mw{j}", out=wf[j][:], in_=wcat_st[3 * f + j],
                              writes=[f"m_wf{j}"])
                        cast_chunk(wb[par + j], f"m_wb{par + j}", wf[j], [f"m_wf{j}"], 2048)

                mprep(0)
                for f in range(FT):
                    par = (f % 2) * 3
                    if f + 1 < FT:
                        mprep(f + 1)
                    for j, pr in enumerate(("inp", "dec", "gate")):
                        def mmg(E, j=j, pr=pr, par=par):
                            last = None
                            for d in range(FT):
                                for h in range(2):
                                    last = E.matmul(
                                        ps[pr][:, h * 512:(h + 1) * 512],
                                        wb[par + j][:, d * 128:(d + 1) * 128],
                                        xln[:, d * CHUNK + h * 512: d * CHUNK + (h + 1) * 512],
                                        start=(d == 0), stop=(d == FT - 1))
                            return last
                        p.op("tensor", mmg,
                             reads=[f"m_wb{par + j}"] + [f"xln:{d}" for d in range(FT)],
                             writes=[f"m_ps_{pr}"])
                    b = 0
                    p.op("scalar",
                         lambda E, f=f, b=b: E.activation(
                             out=td[b][:], in_=ps["dec"][:], func=AF.Sigmoid,
                             bias=bdec[:, f:f + 1]),
                         reads=["m_ps_dec", "bdec"], writes=[f"m_td{b}"])
                    p.op("scalar",
                         lambda E, f=f: E.activation(
                             out=ti[:], in_=ps["inp"][:], func=AF.Silu,
                             bias=biasc[:, f:f + 1]),
                         reads=["m_ps_inp", "biasc"], writes=["m_ti"])
                    p.op("scalar",
                         lambda E, f=f: E.activation(
                             out=vbuf[:, f * CHUNK:(f + 1) * CHUNK], in_=ps["gate"][:],
                             func=AF.Sigmoid, bias=biasc[:, 32 + f:33 + f]),
                         reads=["m_ps_gate", "biasc"], writes=[f"m_v:{f}"])
                    p.op("vector",
                         lambda E, b=b: E.tensor_scalar(out=tb[:], in0=td[b][:],
                                                        scalar1=-1.0, scalar2=1.0,
                                                        op0=A.mult, op1=A.add),
                         reads=[f"m_td{b}"], writes=["m_tb"])
                    p.op("vector", lambda E: E.tensor_mul(out=tb[:], in0=tb[:], in1=ti[:]),
                         reads=["m_tb", "m_ti"], writes=["m_tb"])
                    p.op("vector",
                         lambda E, b=b: E.tensor_tensor_scan(
                             out=ti[:], data0=td[b][:], data1=tb[:], initial=0.0,
                             op0=A.mult, op1=A.add),
                         reads=[f"m_td{b}", "m_tb"], writes=["m_ti"])
                    p.op("vector",
                         lambda E, b=b: E.tensor_tensor_scan(
                             out=tb[:], data0=td[b][:], data1=td[b][:], initial=1.0,
                             op0=A.mult, op1=A.bypass),
                         reads=[f"m_td{b}"], writes=["m_tb"])
                    p.op("vector",
                         lambda E, f=f: E.tensor_mul(
                             out=ubuf[:, f * CHUNK:(f + 1) * CHUNK],
                             in0=vbuf[:, f * CHUNK:(f + 1) * CHUNK], in1=ti[:]),
                         reads=[f"m_v:{f}", "m_ti"], writes=[f"m_u:{f}"])
                    p.op("vector",
                         lambda E, f=f: E.tensor_mul(
                             out=vbuf[:, f * CHUNK:(f + 1) * CHUNK],
                             in0=vbuf[:, f * CHUNK:(f + 1) * CHUNK], in1=tb[:]),
                         reads=[f"m_v:{f}", "m_tb"], writes=[f"m_v:{f}"])
                    p.op("vector",
                         lambda E, f=f: E.tensor_copy(out=zacc[:, f:f + 1],
                                                      in_=ti[:, CHUNK - 1:CHUNK]),
                         reads=["m_ti"], writes=[f"m_za:{f}"])
                    p.op("vector",
                         lambda E, f=f: E.tensor_copy(out=zacc[:, 16 + f:17 + f],
                                                      in_=tb[:, CHUNK - 1:CHUNK]),
                         reads=["m_tb"], writes=[f"m_za:{16 + f}"])
            p.dma("gpsimd", "agb", out=agz_in, in_=zacc[:],
                  reads=[f"m_za:{j}" for j in range(32)], writes=["agz_in_d"])
            p.collective("AllGather", A.bypass, BGRP, [agz_in], [agz_out],
                         reads=["agz_in_d"], writes=["agz_out_d"])
            zg = sx.enter_context(nc.sbuf_tensor("m_zg", [128, 128], F32))
            for j in range(4):
                p.dma("gpsimd", "agb", out=zg[:, 32 * j:32 * (j + 1)],
                      in_=agz_out[128 * j:128 * (j + 1), :], reads=["agz_out_d"],
                      writes=[f"m_zg:{j}"])
            u = [sx.enter_context(nc.sbuf_tensor(f"m_uu{j}", [128, 16], F32))
                 for j in range(4)]
            zin = sx.enter_context(nc.sbuf_tensor("m_zin", [128, 16], F32))
            p.op("gpsimd", lambda E: E.memset(u[0][:], 0.0), writes=["m_uu0"])
            p.op("vector", lambda E: E.tensor_copy(out=u[1][:], in_=zg[:, 0:16]),
                 reads=["m_zg:0"], writes=["m_uu1"])
            for j in (2, 3):
                p.op("vector",
                     lambda E, j=j: E.tensor_mul(
                         out=u[j][:], in0=zg[:, 32 * (j - 1) + 16:32 * (j - 1) + 32],
                         in1=u[j - 1][:]),
                     reads=[f"m_zg:{j - 1}", f"m_uu{j - 1}"], writes=[f"m_uu{j}"])
                p.op("vector",
                     lambda E, j=j: E.tensor_add(
                         out=u[j][:], in0=u[j][:], in1=zg[:, 32 * (j - 1):32 * (j - 1) + 16]),
                     reads=[f"m_uu{j}", f"m_zg:{j - 1}"], writes=[f"m_uu{j}"])
            p.op("gpsimd", lambda E: E.memset(zin[:], 0.0), writes=["m_zin"])
            for j in range(4):
                p.op("vector",
                     lambda E, j=j: E.scalar_tensor_tensor(
                         out=zin[:], in0=u[j][:], scalar=zselc[:, j:j + 1], in1=zin[:],
                         op0=A.mult, op1=A.add),
                     reads=[f"m_uu{j}", "zselc", "m_zin"], writes=["m_zin"])
            for f in range(FT):
                p.op("vector",
                     lambda E, f=f: E.scalar_tensor_tensor(
                         out=xln[:, f * CHUNK:(f + 1) * CHUNK],
                         in0=vbuf[:, f * CHUNK:(f + 1) * CHUNK],
                         scalar=zin[:, f:f + 1],
                         in1=ubuf[:, f * CHUNK:(f + 1) * CHUNK],
                         op0=A.mult, op1=A.add),
                     reads=[f"m_v:{f}", "m_zin", f"m_u:{f}"], writes=[f"xln:{f}"])
        with scope() as sx:
            nunits, qunit, qfin = qscale_make(sx)

            def filler(e, done=[0]):
                while done[0] * 16 < (e + 1) * nunits:
                    qunit(done[0])
                    done[0] += 1

            proj_residual(sx, wout_st, 16, FT, "xln", xln, x_in, None,
                          xhbm[0], "xh0", bias_off=48, filler=filler)
            qfin()

    def phase_ffn(xold_src, xold_key, xnew_dst, xnew_key, wu, wg, wdn, n_et,
                  mu_, mg_, md_, final=False):
        with scope() as sx:
            layernorm(xold_src, xold_key, sx)
        with scope() as sx:
            wfu = [sx.enter_context(nc.sbuf_tensor(f"f{prcnt[0]}_wfu{b}", [128, 2048], F32)) for b in range(3)]
            wfg = [sx.enter_context(nc.sbuf_tensor(f"f{prcnt[0]}_wfg{b}", [128, 2048], F32)) for b in range(3)]
            qu = [sx.enter_context(nc.sbuf_tensor(f"f{prcnt[0]}_qu{b}", [128, 2048], BF16)) for b in range(3)]
            qg = [sx.enter_context(nc.sbuf_tensor(f"f{prcnt[0]}_qg{b}", [128, 2048], BF16)) for b in range(3)]
            tsil = [sx.enter_context(nc.sbuf_tensor(f"f{prcnt[0]}_ts{b}", [128, CHUNK], F32)) for b in range(2)]
            hb = [sx.enter_context(nc.sbuf_tensor(f"f{prcnt[0]}_hb{b}", [128, CHUNK], BF16)) for b in range(2)]
            qm = [sx.enter_context(nc.sbuf_tensor(f"f{prcnt[0]}_qm{b}", [128, 2048], BF16)) for b in range(4)]
            dwf = [sx.enter_context(nc.sbuf_tensor(f"f{prcnt[0]}_dwf{b}", [128, 2048], F32)) for b in range(1)]
            dq = [sx.enter_context(nc.sbuf_tensor(f"f{prcnt[0]}_dq{b}", [128, 2048], BF16)) for b in range(2)]
            dqm = [sx.enter_context(nc.sbuf_tensor(f"f{prcnt[0]}_dqm{b}", [128, 2048], BF16)) for b in range(1)]
            cc = n_et * 128
            nq = (cc + 2047) // 2048
            dj_total = 16 * nq
            djq = 0
            with scope() as px:
                psu = [px.enter_context(nc.psum_tensor(f"f{prcnt[0]}_pu{b}", [128, CHUNK], F32)) for b in range(2)]
                psg = [px.enter_context(nc.psum_tensor(f"f{prcnt[0]}_pg{b}", [128, CHUNK], F32)) for b in range(2)]

                def down_prep_unit():
                    nonlocal djq
                    if djq >= dj_total:
                        return
                    j, qq = djq // nq, djq % nq
                    c0 = qq * 2048
                    c1 = min(cc, c0 + 2048)
                    bb = 0
                    p.dma("sync", f"dpw{bb}", out=dwf[bb][:, 0:c1 - c0],
                          in_=wdn[j][:, c0:c1], writes=[f"f_dwf{bb}"])
                    quant_chunk(dq[bb], f"f_dq{bb}", dwf[bb], [f"f_dwf{bb}"],
                                c1 - c0, md_, dqm[0], "f_dqm0")
                    p.dma("scalar", f"dpo{bb}", out=qdh[j][:, c0:c1],
                          in_=dq[bb][:, 0:c1 - c0], reads=[f"f_dq{bb}"],
                          writes=[f"qdh:{j}"])
                    djq += 1

                def fprep(e):
                    b3 = e % 3
                    b = e % 2
                    p.dma("sync", f"fu{b3}", out=wfu[b3][:], in_=wu[e], writes=[f"f_wfu{b3}"])
                    p.dma("sync", f"fg{b3}", out=wfg[b3][:], in_=wg[e], writes=[f"f_wfg{b3}"])
                    quant_chunk(qu[b3], f"f_qu{b3}", wfu[b3], [f"f_wfu{b3}"], 2048, mu_,
                                qm[b], f"f_qm{b}")
                    quant_chunk(qg[b3], f"f_qg{b3}", wfg[b3], [f"f_wfg{b3}"], 2048, mg_,
                                qm[2 + b], f"f_qm{2 + b}")

                fprep(0)
                for e in range(n_et):
                    b3 = e % 3
                    b = e % 2
                    if e + 1 < n_et:
                        fprep(e + 1)
                    while djq * n_et < (e + 1) * dj_total:
                        down_prep_unit()
                    def uq(E, b=b, b3=b3):
                        last = None
                        for d in range(FT):
                            for h in range(2):
                                last = E.matmul(
                                    psu[b][:, h * 512:(h + 1) * 512],
                                    qu[b3][:, d * 128:(d + 1) * 128],
                                    xln[:, d * CHUNK + h * 512: d * CHUNK + (h + 1) * 512],
                                    start=(d == 0), stop=(d == FT - 1))
                        return last
                    p.op("tensor", uq,
                         reads=[f"f_qu{b3}"] + [f"xln:{d}" for d in range(FT)],
                         writes=[f"f_pu{b}"])
                    def gq(E, b=b, b3=b3):
                        last = None
                        for d in range(FT):
                            for h in range(2):
                                last = E.matmul(
                                    psg[b][:, h * 512:(h + 1) * 512],
                                    qg[b3][:, d * 128:(d + 1) * 128],
                                    xln[:, d * CHUNK + h * 512: d * CHUNK + (h + 1) * 512],
                                    start=(d == 0), stop=(d == FT - 1))
                        return last
                    p.op("tensor", gq,
                         reads=[f"f_qg{b3}"] + [f"xln:{d}" for d in range(FT)],
                         writes=[f"f_pg{b}"])
                    p.op("scalar",
                         lambda E, b=b, m=mu_: E.activation(
                             out=tsil[b][:], in_=psu[b][:], func=AF.Silu,
                             scale=scol[:, m:m + 1]),
                         reads=[f"f_pu{b}", "scol"], writes=[f"f_ts{b}"])
                    p.op("vector",
                         lambda E, b=b, m=mg_: E.scalar_tensor_tensor(
                             out=hb[b][:], in0=psg[b][:], scalar=scol[:, m:m + 1],
                             in1=tsil[b][:], op0=A.mult, op1=A.mult),
                         reads=[f"f_pg{b}", "scol", f"f_ts{b}"], writes=[f"f_hb{b}"])
                    p.dma("scalar", f"fh{b}", out=hhbm[e * 128:(e + 1) * 128, :],
                          in_=hb[b][:], reads=[f"f_hb{b}"], writes=[f"hh:{e}"])
                while djq < dj_total:
                    down_prep_unit()
        i = prcnt[0]
        prcnt[0] += 1
        cc = n_et * 128
        nq = (cc + 2047) // 2048
        with scope() as sx:
            hsb = sx.enter_context(nc.sbuf_tensor(f"fd{i}_h", [128, n_et * 512], BF16))
            qd = [sx.enter_context(nc.sbuf_tensor(f"fd{i}_qd{b}", [128, cc], BF16)) for b in range(3)]
            xo = [sx.enter_context(nc.sbuf_tensor(f"fd{i}_xo{b}", [128, 512], F32)) for b in range(2)]
            xn = [sx.enter_context(nc.sbuf_tensor(f"fd{i}_xn{b}", [128, 512], F32)) for b in range(2)]
            with scope() as px:
                psd = [px.enter_context(nc.psum_tensor(f"fd{i}_ps{b}", [128, 512], F32))
                       for b in range(2)]
                for th in range(2):
                    for e in range(n_et):
                        p.dma("scalar", f"fhr{e % 3}",
                              out=hsb[:, e * 512:(e + 1) * 512],
                              in_=hhbm[e * 128:(e + 1) * 128, th * 512:(th + 1) * 512],
                              reads=[f"hh:{e}"], writes=[f"fd_h:{e}"])
                    def dprep(e, th=th):
                        b3 = e % 3
                        b = e % 2
                        p.dma("sync", f"fdw{b3}", out=qd[b3][:, 0:cc],
                              in_=qdh[e][:, 0:cc], reads=[f"qdh:{e}"],
                              writes=[f"fd_qd{b3}"])
                        p.dma("scalar", f"fdx{b}", out=xo[b][:],
                              in_=xold_src[e * 128:(e + 1) * 128, th * 512:(th + 1) * 512],
                              reads=([f"{xold_key}:{e}"] if xold_key else []),
                              writes=[f"fd_xo{b}"])

                    dprep(0)
                    for e in range(16):
                        b3 = e % 3
                        b = e % 2
                        if e + 1 < 16:
                            dprep(e + 1)
                        def dmg(E, b=b, b3=b3):
                            last = None
                            for d in range(n_et):
                                last = E.matmul(
                                    psd[b][:, :], qd[b3][:, d * 128:(d + 1) * 128],
                                    hsb[:, d * 512:(d + 1) * 512],
                                    start=(d == 0), stop=(d == n_et - 1))
                            return last
                        p.op("tensor", dmg,
                             reads=[f"fd_qd{b3}"] + [f"fd_h:{d}" for d in range(n_et)],
                             writes=[f"fd_ps{b}"])
                        p.op("vector",
                             lambda E, b=b, m=md_: E.scalar_tensor_tensor(
                                 out=xn[b][:], in0=psd[b][:], scalar=scol[:, m:m + 1],
                                 in1=xo[b][:], op0=A.mult, op1=A.add),
                             reads=[f"fd_ps{b}", "scol", f"fd_xo{b}"], writes=[f"fd_xn{b}"])
                        p.dma("scalar", f"fdo{b}",
                              out=xnew_dst[e * 128:(e + 1) * 128, th * 512:(th + 1) * 512],
                              in_=xn[b][:], reads=[f"fd_xn{b}"],
                              writes=[f"{xnew_key}:{e}"])

    def phase_attn(xold_src, xold_key, xnew_dst, xnew_key):
        with scope() as sx:
            layernorm(xold_src, xold_key, sx)
        with scope() as sx:
            wf = [sx.enter_context(nc.sbuf_tensor(f"a_wf{b}", [128, 2048], F32)) for b in range(2)]
            wb = [sx.enter_context(nc.sbuf_tensor(f"a_wb{b}", [128, 2048], BF16)) for b in range(2)]
            qt = [sx.enter_context(nc.sbuf_tensor(f"a_qt{b}", [128, CHUNK], BF16)) for b in range(2)]
            with scope() as px:
                psq = [px.enter_context(nc.psum_tensor(f"a_pq{b}", [128, CHUNK], F32)) for b in range(2)]
                for (w_st, n_e, dst, row0, scale) in (
                        (wk_st, 4, agkv_in, 0, 1.0),
                        (wv_st, 4, agkv_in, 512, 1.0),
                        (wq_st, 16, agq_in, 0, 1.0 / float(np.sqrt(128.0)))):
                    if w_st is wq_st:
                        p.collective("AllGather", A.bypass, BGRP, [agkv_in], [agkv_out],
                                     reads=["agkv_in_d"], writes=["agkv_out_d"])
                    def aprep(e, w_st=w_st):
                        b = e % 2
                        p.dma("sync", f"aw{b}", out=wf[b][:], in_=w_st[e], writes=[f"a_wf{b}"])
                        cast_chunk(wb[b], f"a_wb{b}", wf[b], [f"a_wf{b}"], 2048)

                    aprep(0)
                    for e in range(n_e):
                        if w_st is wq_st and e == 8:
                            p.collective("AllGather", A.bypass, BGRP,
                                         [agq_in[0:1024, :]], [agq_out[0]],
                                         reads=["agq_in_d0"], writes=["agq_out_d"])
                        b = e % 2
                        if e + 1 < n_e:
                            aprep(e + 1)
                        def qmg(E, b=b):
                            last = None
                            for d in range(FT):
                                for h in range(2):
                                    last = E.matmul(
                                        psq[b][:, h * 512:(h + 1) * 512],
                                        wb[b][:, d * 128:(d + 1) * 128],
                                        xln[:, d * CHUNK + h * 512: d * CHUNK + (h + 1) * 512],
                                        start=(d == 0), stop=(d == FT - 1))
                            return last
                        p.op("tensor", qmg,
                             reads=[f"a_wb{b}"] + [f"xln:{d}" for d in range(FT)],
                             writes=[f"a_pq{b}"])
                        p.op("scalar",
                             lambda E, b=b, scale=scale: E.activation(
                                 out=qt[b][:], in_=psq[b][:], func=AF.Copy, scale=scale),
                             reads=[f"a_pq{b}"], writes=[f"a_qt{b}"])
                        p.dma("scalar", f"aq{b}",
                              out=dst[row0 + e * 128: row0 + (e + 1) * 128, :],
                              in_=qt[b][:], reads=[f"a_qt{b}"],
                              writes=["agkv_in_d" if dst is agkv_in
                                      else f"agq_in_d{e // 8}"])
        p.collective("AllGather", A.bypass, BGRP, [agq_in[1024:2048, :]], [agq_out[1]],
                     reads=["agq_in_d1"], writes=["agq_out_d"])
        with scope() as sx:
            qg = sx.enter_context(nc.sbuf_tensor("at_q", [128, 4 * S], BF16))
            kg = sx.enter_context(nc.sbuf_tensor("at_k", [128, S], BF16))
            vf = sx.enter_context(nc.sbuf_tensor("at_vf", [128, S], BF16))
            vt = sx.enter_context(nc.sbuf_tensor("at_vt", [128, S], BF16))
            idn = sx.enter_context(nc.sbuf_tensor("at_idn", [128, 128], BF16))
            mstr = sx.enter_context(nc.sbuf_tensor("at_m", [128, 32 * 512], BF16))
            dtil = [sx.enter_context(nc.sbuf_tensor(f"at_d{b}", [128, 512], F32)) for b in range(2)]
            stt_ = [sx.enter_context(nc.sbuf_tensor(f"at_s{b}", [128, 1024], F32)) for b in range(3)]
            ptt = [sx.enter_context(nc.sbuf_tensor(f"at_p{b}", [128, 1024], BF16)) for b in range(4)]
            ob = [sx.enter_context(nc.sbuf_tensor(f"at_ob{b}", [128, 512], BF16)) for b in range(2)]
            for j in range(4):
                p.idma("gq", out=kg[:, j * CHUNK:(j + 1) * CHUNK], in_=agkv_out[:, :],
                       idx_ap=gidxs[:, 16 + j:17 + j],
                       reads=["agkv_out_d", "gidxs"], writes=[f"at_k:{j}"])
                p.idma("gq", out=vf[:, j * CHUNK:(j + 1) * CHUNK], in_=agkv_out[:, :],
                       idx_ap=gidxs[:, 20 + j:21 + j],
                       reads=["agkv_out_d", "gidxs"], writes=[f"at_vf:{j}"])
            agq_v = agq_out.rearrange("a b c -> (a b) c")
            for r in range(4):
                for j in range(4):
                    p.idma("gq", out=qg[:, (r * 4 + j) * CHUNK:(r * 4 + j + 1) * CHUNK],
                           in_=agq_v, idx_ap=gidxs[:, r * 4 + j:r * 4 + j + 1],
                           reads=["agq_out_d", "gidxs"], writes=[f"at_q:{r}:{j}"])
            p.op("gpsimd", lambda E: E.memset(idn[:], 1.0), writes=["at_idn"])
            p.op("gpsimd",
                 lambda E: E.affine_select(out=idn[:], in_=idn[:], pattern=[[-1, 128]],
                                           compare_op=A.is_equal, fill=0.0, base=0,
                                           channel_multiplier=1),
                 reads=["at_idn"], writes=["at_idn"])
            with scope() as px:
                pst = [px.enter_context(nc.psum_tensor(f"at_pt{b}", [128, 128], BF16))
                       for b in range(2)]
                for kt in range(32):
                    b = kt % 2
                    p.op("tensor",
                         lambda E, kt=kt, b=b: E.transpose(
                             pst[b][:, :], vf[:, kt * 128:(kt + 1) * 128], idn[:]),
                         reads=[f"at_vf:{kt // 8}", "at_idn"], writes=[f"at_pt{b}"])
                    p.op("scalar",
                         lambda E, kt=kt, b=b: E.copy(
                             out=vt[:, kt * 128:(kt + 1) * 128], in_=pst[b][:, :]),
                         reads=[f"at_pt{b}"], writes=[f"at_vt:{kt}"])
            with scope() as px:
                pssc = px.enter_context(nc.psum_tensor("at_sc", [128, 3072], F32))
                pso = px.enter_context(nc.psum_tensor("at_o", [128, 512], F32))
                psd = px.enter_context(nc.psum_tensor("at_dn", [128, 512], F32))
                dnrow = sx.enter_context(nc.sbuf_tensor("at_dnr", [1, 512], F32))
                for qs in range(8):
                    nkt = 4 * qs + 4
                    for kt in range(nkt):
                        b = kt % 2
                        p.op("gpsimd",
                             lambda E, qs=qs, kt=kt, b=b: E.iota(
                                 dtil[b][:], pattern=[[1, 512]], base=512 * qs - 128 * kt,
                                 channel_multiplier=-1,
                                 allow_small_or_imprecise_dtypes=True),
                             writes=[f"at_d{b}"])
                        p.op("scalar",
                             lambda E, kt=kt, b=b: E.activation(
                                 out=mstr[:, kt * 512:(kt + 1) * 512], in_=dtil[b][:],
                                 func=AF.Sqrt),
                             reads=[f"at_d{b}"], writes=[f"at_m:{kt}"])
                    for r in range(4):
                        def qkpair(kp, r=r, qs=qs):
                            k0 = 2 * kp
                            s0 = (kp % 3) * 1024
                            bp = kp % 4
                            bs = kp % 3
                            def qkg(E, k0=k0, s0=s0, r=r, qs=qs):
                                E.matmul(pssc[:, s0:s0 + 512],
                                         kg[:, k0 * 128:(k0 + 1) * 128],
                                         qg[:, r * S + qs * 512: r * S + (qs + 1) * 512],
                                         start=True, stop=True)
                                return E.matmul(pssc[:, s0 + 512:s0 + 1024],
                                                kg[:, (k0 + 1) * 128:(k0 + 2) * 128],
                                                qg[:, r * S + qs * 512: r * S + (qs + 1) * 512],
                                                start=True, stop=True)
                            p.op("tensor", qkg,
                                 reads=[f"at_k:{k0 // 8}", f"at_k:{(k0 + 1) // 8}",
                                        f"at_q:{r}:{qs // 2}"],
                                 writes=[f"at_scp{bs}"])
                            p.op("vector",
                                 lambda E, k0=k0, s0=s0, r=r, bs=bs: E.scalar_tensor_tensor(
                                     out=stt_[bs][:], in0=mstr[:, k0 * 512:(k0 + 2) * 512],
                                     scalar=slopc[:, r:r + 1], in1=pssc[:, s0:s0 + 1024],
                                     op0=A.mult, op1=A.add),
                                 reads=[f"at_m:{k0}", f"at_m:{k0 + 1}", "slopc", f"at_scp{bs}"],
                                 writes=[f"at_sp{bs}"])
                            p.op("scalar",
                                 lambda E, bp=bp, bs=bs: E.activation(
                                     out=ptt[bp][:], in_=stt_[bs][:], func=AF.Exp,
                                     bias=shiftc[:, 0:1]),
                                 reads=[f"at_sp{bs}", "shiftc"], writes=[f"at_pp{bp}"])
                            for kk in (k0, k0 + 1):
                                if kk >= 4 * qs:
                                    p.op("gpsimd",
                                         lambda E, qs=qs, kk=kk, bp=bp, k0=k0: E.affine_select(
                                             out=ptt[bp][:, (kk - k0) * 512:(kk - k0 + 1) * 512],
                                             in_=ptt[bp][:, (kk - k0) * 512:(kk - k0 + 1) * 512],
                                             pattern=[[1, 512]], compare_op=A.is_ge,
                                             fill=0.0, base=512 * qs - 128 * kk,
                                             channel_multiplier=-1),
                                         reads=[f"at_pp{bp}"], writes=[f"at_pp{bp}"])

                        def avdenpair(kp, nkt=nkt):
                            k0 = 2 * kp
                            bp = kp % 4
                            def avg_(E, k0=k0, bp=bp, nkt=nkt):
                                last = None
                                for kk in (k0, k0 + 1):
                                    E.matmul(pso[:, :], vt[:, kk * 128:(kk + 1) * 128],
                                             ptt[bp][:, (kk - k0) * 512:(kk - k0 + 1) * 512],
                                             start=(kk == 0), stop=(kk == nkt - 1))
                                    last = E.matmul(psd[0:1, :], onescol[:, 0:1],
                                                    ptt[bp][:, (kk - k0) * 512:(kk - k0 + 1) * 512],
                                                    start=(kk == 0), stop=(kk == nkt - 1))
                                return last
                            p.op("tensor", avg_,
                                 reads=[f"at_vt:{k0}", f"at_vt:{k0 + 1}", f"at_pp{bp}",
                                        "onescol"],
                                 writes=["at_pso", "at_psd"])

                        npair = nkt // 2
                        for kp in range(npair + 3):
                            if kp < npair:
                                qkpair(kp)
                            if kp >= 3:
                                avdenpair(kp - 3)
                        bo = r % 2
                        hh = qs // 4
                        c0 = (qs % 4) * 512
                        p.op("vector",
                             lambda E: E.tensor_copy(out=dnrow[:], in_=psd[0:1, :]),
                             reads=["at_psd"], writes=["at_dnr"])
                        p.dma("scalar", "aod",
                              out=agd_in[hh][r:r + 1, c0:c0 + 512],
                              in_=dnrow[:], reads=["at_dnr"], writes=[f"agd_in_d{hh}"])
                        p.op("scalar",
                             lambda E, bo=bo: E.copy(out=ob[bo][:], in_=pso[:, :]),
                             reads=["at_pso"], writes=[f"at_ob{bo}"])
                        p.dma("scalar", f"ao{bo}",
                              out=ago_in[hh][r * 128:(r + 1) * 128, c0:c0 + 512],
                              in_=ob[bo][:], reads=[f"at_ob{bo}"], writes=[f"ago_in_d{hh}"])
                    if qs == 3:
                        p.collective("AllGather", A.bypass, BGRP, [ago_in[0]], [ago_out[0]],
                                     reads=["ago_in_d0"], writes=["ago_out_d"])
                        p.collective("AllGather", A.bypass, BGRP, [agd_in[0]], [agd_out[0]],
                                     reads=["agd_in_d0"], writes=["agd_out_d"])
        p.collective("AllGather", A.bypass, BGRP, [ago_in[1]], [ago_out[1]],
                     reads=["ago_in_d1"], writes=["ago_out_d"])
        p.collective("AllGather", A.bypass, BGRP, [agd_in[1]], [agd_out[1]],
                     reads=["agd_in_d1"], writes=["agd_out_d"])
        with scope() as sx:
            mo = sx.enter_context(nc.sbuf_tensor("ao_m", [128, FT * CHUNK], BF16))
            dng = [sx.enter_context(nc.sbuf_tensor(f"ao_dn{b}", [128, CHUNK], F32)) for b in range(2)]
            ago_v = ago_out.rearrange("a b (c1 c2) -> (a b c1) c2", c2=CHUNK)
            agd_v = agd_out.rearrange("a b (c1 c2) -> (a b c1) c2", c2=CHUNK)
            for f in range(FT):
                b = f % 2
                p.idma("go", out=mo[:, f * CHUNK:(f + 1) * CHUNK], in_=ago_v,
                       idx_ap=oidxs[:, f:f + 1],
                       reads=["ago_out_d", "oidxs"], writes=[f"ao_m:{f}"])
                p.idma("go", out=dng[b][:], in_=agd_v, idx_ap=oidxs[:, 16 + f:17 + f],
                       reads=["agd_out_d", "oidxs"], writes=[f"ao_dn{b}"])
                p.op("vector",
                     lambda E, b=b: E.reciprocal(out=dng[b][:], in_=dng[b][:]),
                     reads=[f"ao_dn{b}"], writes=[f"ao_dn{b}"])
                p.op("vector",
                     lambda E, f=f, b=b: E.tensor_mul(
                         out=mo[:, f * CHUNK:(f + 1) * CHUNK],
                         in0=mo[:, f * CHUNK:(f + 1) * CHUNK], in1=dng[b][:]),
                     reads=[f"ao_m:{f}", f"ao_dn{b}"], writes=[f"ao_m:{f}"])
            proj_residual(sx, wo_st, 16, FT, "ao_m", mo, xold_src, xold_key,
                          xnew_dst, xnew_key)

    if upto >= 1:
        phase_l0_mixer()
    cur, cur_key = (xhbm[0], "xh0") if upto >= 1 else (x_in, None)
    if upto >= 2:
        phase_ffn(cur, cur_key, xhbm[1], "xh1", wu_st, wg0_st, wd_st, ET0, 0, 1, 2)
        cur, cur_key = xhbm[1], "xh1"
    if upto >= 3:
        phase_attn(cur, cur_key, xhbm[0], "xh0")
        cur, cur_key = xhbm[0], "xh0"
    if upto >= 4:
        phase_ffn(cur, cur_key, x_out, "xout", w1_st, w3_st, w2_st, ET1, 3, 4, 5)
    else:
        with scope() as sx:
            t = [sx.enter_context(nc.sbuf_tensor(f"cp{b}", [128, CHUNK], F32)) for b in range(2)]
            for f in range(FT):
                b = f % 2
                p.dma("scalar", f"cpi{b}", out=t[b][:], in_=cur[f * 128:(f + 1) * 128, :],
                      reads=([f"{cur_key}:{f}"] if cur_key else []), writes=[f"cpt{b}"])
                p.dma("scalar", f"cpo{b}", out=x_out[f * 128:(f + 1) * 128, :], in_=t[b][:],
                      reads=[f"cpt{b}"], writes=[f"xout:{f}"])
    p.fence_all()
    p.finalize()
    return nc, p


def _wst(w, n_d, n_e):
    return np.ascontiguousarray(
        w.reshape(n_d, 128, n_e, 128).transpose(2, 1, 0, 3).reshape(n_e, 128, n_d * 128),
        dtype=np.float32)


def _col16(v):
    return np.ascontiguousarray(np.asarray(v, np.float32).reshape(16, 128).T)


def _host_inputs(inputs):
    f32 = lambda k: np.asarray(inputs[k], np.float32)
    x = f32("L0_attn_x")
    win, wdec, wgat, wout = (f32("L0_attn_W_in"), f32("L0_attn_W_decay"),
                             f32("L0_attn_W_gate"), f32("L0_attn_W_out"))
    wu, wg0, wd = f32("L0_ffn_W_up"), f32("L0_ffn_W_gate"), f32("L0_ffn_W_down")
    wq, wk, wv, wo = (f32("L1_attn_Wq"), f32("L1_attn_Wk"), f32("L1_attn_Wv"),
                      f32("L1_attn_Wo"))
    w1, w3, w2 = f32("L1_ffn_w1"), f32("L1_ffn_w3"), f32("L1_ffn_w2")

    win_s, wdec_s, wgat_s = _wst(win, 16, 16), _wst(wdec, 16, 16), _wst(wgat, 16, 16)
    wcat = np.empty((48, 128, 2048), np.float32)
    for f in range(16):
        wcat[3 * f + 0] = win_s[f]
        wcat[3 * f + 1] = wdec_s[f]
        wcat[3 * f + 2] = wgat_s[f]
    bmix = np.concatenate([_col16(inputs["L0_attn_b_in"]),
                           _col16(inputs["L0_attn_b_decay"]),
                           _col16(inputs["L0_attn_b_gate"]),
                           _col16(inputs["L0_attn_b_out"])], axis=1)
    w1p = np.zeros((2048, H1P), np.float32); w1p[:, :H1] = w1.T
    w3p = np.zeros((2048, H1P), np.float32); w3p[:, :H1] = w3.T
    w2p = np.zeros((H1P, 2048), np.float32); w2p[:H1, :] = w2.T

    shared = dict(
        wcat_st=wcat, bmix=bmix, dbias=_col16(inputs["L0_attn_decay_bias"]),
        wout_st=_wst(wout, 16, 16),
        wu_st=_wst(wu.T, 16, 64), wg0_st=_wst(wg0.T, 16, 64), wd_st=_wst(wd.T, 64, 16),
        wq_st=_wst(wq.T, 16, 16), wk_st=_wst(wk.T, 16, 4), wv_st=_wst(wv.T, 16, 4),
        wo_st=_wst(wo.T, 16, 16),
        w1_st=_wst(w1p, 16, 43), w3_st=_wst(w3p, 16, 43), w2_st=_wst(w2p, 43, 16))

    qflats = [wu.ravel(), wg0.ravel(), wd.ravel(), w1.ravel(), w3.ravel(), w2.ravel()]
    qrcp = np.zeros((1, 16), np.float32)
    qrcp[0, :6] = [np.float32(1.0) / np.float32(q.size) for q in qflats]
    qeps = np.zeros((1, 16), np.float32)
    qeps[0, :3] = 1e-8
    shared["qrcp"] = qrcp
    shared["qeps"] = qeps

    slopes = (2.0 ** (-0.5 * np.arange(1, 17))).astype(np.float32)
    prng = np.arange(128, dtype=np.int32)
    in_maps = []
    for k in range(NCORE):
        b, c = k // 4, k % 4
        g = c
        xs = np.ascontiguousarray(x[b, c * CHUNK:(c + 1) * CHUNK, :].T, dtype=np.float32)
        ns = np.repeat(-slopes[4 * g:4 * g + 4][None, :], 128, axis=0)
        zs = np.zeros((128, 4), np.float32)
        zs[:, c] = 1.0
        gi = np.zeros((128, 24), np.int32)
        for r in range(4):
            for j in range(4):
                gi[:, r * 4 + j] = (g // 2) * 4096 + j * 1024 + (g % 2) * 512 + 128 * r + prng
        for j in range(4):
            gi[:, 16 + j] = j * 1024 + 128 * g + prng
            gi[:, 20 + j] = j * 1024 + 512 + 128 * g + prng
        oi = np.zeros((128, 32), np.int32)
        for f in range(FT):
            oi[:, f] = (c // 2) * 4096 + (128 * f + prng) * 2 + (c % 2)
            oi[:, 16 + f] = (c // 2) * 32 + f * 2 + (c % 2)
        qsh = np.concatenate(
            [q[k * (q.size // 8):(k + 1) * (q.size // 8)].reshape(128, -1) for q in qflats],
            axis=1).astype(np.float32)
        m = dict(shared)
        m.update(x_in=xs, negslopes=np.ascontiguousarray(ns), zsel=zs,
                 gidx=np.ascontiguousarray(gi), oidx=np.ascontiguousarray(oi),
                 qshard=np.ascontiguousarray(qsh))
        in_maps.append(m)
    return in_maps


def kernel(**inputs):
    in_maps = _host_inputs(inputs)
    nc, _ = build_program(upto=4)
    res = run_bass_kernel_spmd(nc, in_maps, core_ids=list(range(NCORE)))
    out = np.empty((B, S, D), np.float32)
    for k in range(NCORE):
        b, c = k // 4, k % 4
        out[b, c * CHUNK:(c + 1) * CHUNK, :] = res.results[k]["x_out"].T
    return out


# version 11 (final)
# speedup vs baseline: 1.0547x; 1.0161x over previous
import numpy as np
from contextlib import contextmanager
import concourse.bass as bass
import concourse.mybir as mybir
from concourse.bass_utils import run_bass_kernel_spmd
from contextlib import ExitStack

F32 = mybir.dt.float32
BF16 = mybir.dt.bfloat16
I32 = mybir.dt.int32
A = mybir.AluOpType
AF = mybir.ActivationFunctionType
AX = mybir.AxisListType

B, S, D = 2, 4096, 2048
NCORE = 8
CHUNK = 1024
FT = D // 128
H0, ET0 = 8192, 64
H1, H1P = 5461, 5504
ET1 = H1P // 128
EPS_LN = 1e-5
SHIFT = 16.0

ENGINES = ("tensor", "vector", "scalar", "gpsimd", "sync")


class _Sem:
    __slots__ = ("handle", "count", "snapshots", "name")

    def __init__(self, handle, name):
        self.handle = handle
        self.count = 0
        self.name = name
        self.snapshots = {}


class Dep:
    __slots__ = ("sem", "value")

    def __init__(self, sem, value):
        self.sem = sem
        self.value = value


class Prog:

    def __init__(self, nc, ctx):
        self.nc = nc
        self.ctx = ctx
        self.ops = {e: [] for e in ENGINES}
        self.esem = {e: _Sem(ctx.enter_context(nc.semaphore(f"es_{e}")), f"es_{e}")
                     for e in ENGINES}
        self.streams = {}
        self.cc_sem = _Sem(ctx.enter_context(nc.semaphore("cc")), "cc")
        self.clock = {e: {} for e in ENGINES}
        self.last_writer = {}
        self.readers = {}
        self.n_waits = 0
        self.n_bumped = 0

    def _stream(self, name):
        if name not in self.streams:
            h = self.ctx.enter_context(self.nc.semaphore(f"ds_{name}"))
            self.streams[name] = _Sem(h, f"ds_{name}")
        return self.streams[name]

    def _wait(self, eng, dep):
        sem, val = dep.sem, dep.value
        if self.clock[eng].get(sem.name, 0) >= val:
            return
        if sem.name.startswith("ds_") and val < sem.count:
            val = sem.count
            self.n_bumped += 1
        snap = sem.snapshots.get(val)
        c = self.clock[eng]
        c[sem.name] = max(c.get(sem.name, 0), val)
        if snap:
            for k, v in snap.items():
                if c.get(k, 0) < v:
                    c[k] = v
        h = sem.handle
        self.ops[eng].append(lambda E, h=h, val=val: E.wait_ge(h, val))
        self.n_waits += 1

    def _resolve(self, eng, reads, writes):
        for dl in getattr(self, "barrier_deps", ()):
            for d in dl:
                self._wait(eng, d)
        skip_own = (eng == "tensor")
        own = f"es_{eng}"
        for r in reads:
            lw = self.last_writer.get(r)
            if lw is not None and not (skip_own and lw.sem.name == own):
                self._wait(eng, lw)
        for w in writes:
            lw = self.last_writer.get(w)
            if lw is not None and not (skip_own and lw.sem.name == own):
                self._wait(eng, lw)
            for rd in self.readers.get(w, ()):
                if not (skip_own and rd.sem.name == own):
                    self._wait(eng, rd)

    def _commit(self, dep, reads, writes):
        for w in writes:
            self.last_writer[w] = dep
            self.readers[w] = []
        for r in reads:
            self.readers.setdefault(r, []).append(dep)

    def op(self, eng, fn, reads=(), writes=()):
        reads, writes = list(reads), list(writes)
        self._resolve(eng, reads, writes)
        sem = self.esem[eng]
        sem.count += 1
        val, h = sem.count, sem.handle

        def run(E, fn=fn, h=h):
            inst = fn(E)
            assert inst is not None, "op fn must return its last instruction"
            inst.then_inc(h, 1)

        self.ops[eng].append(run)
        dep = Dep(sem, val)
        snap = dict(self.clock[eng])
        snap[sem.name] = val
        sem.snapshots[val] = snap
        self._commit(dep, reads, writes)
        return dep

    def dma(self, eng, stream, out, in_, reads=(), writes=()):
        reads, writes = list(reads), list(writes)
        self._resolve(eng, reads, writes)
        s = self._stream(stream)
        s.count += 16
        val, h = s.count, s.handle

        def run(E, out=out, in_=in_, h=h):
            E.dma_start(out=out, in_=in_).then_inc(h, 16)

        self.ops[eng].append(run)
        dep = Dep(s, val)
        s.snapshots[val] = dict(self.clock[eng])
        self._commit(dep, reads, writes)
        return dep

    def idma(self, stream, out, in_, idx_ap, reads=(), writes=()):
        reads, writes = list(reads), list(writes)
        self._resolve("gpsimd", reads, writes)
        s = self._stream(stream)
        s.count += 16
        val, h = s.count, s.handle

        def run(E, out=out, in_=in_, idx_ap=idx_ap, h=h):
            E.indirect_dma_start(
                out=out, out_offset=None, in_=in_,
                in_offset=bass.IndirectOffsetOnAxis(ap=idx_ap, axis=0),
            ).then_inc(h, 16)

        self.ops["gpsimd"].append(run)
        dep = Dep(s, val)
        s.snapshots[val] = dict(self.clock["gpsimd"])
        self._commit(dep, reads, writes)
        return dep

    def collective(self, kind, op_, groups, ins, outs, reads=(), writes=()):
        reads, writes = list(reads), list(writes)
        self._resolve("gpsimd", reads, writes)
        s = self.cc_sem
        s.count += 1
        val, h = s.count, s.handle

        def run(E, kind=kind, op_=op_, rg=groups, ins=ins, outs=outs, h=h):
            E.collective_compute(kind, op_, replica_groups=rg, ins=ins,
                                 outs=outs).then_inc(h, 1)

        self.ops["gpsimd"].append(run)
        dep = Dep(s, val)
        s.snapshots[val] = dict(self.clock["gpsimd"])
        self._commit(dep, reads, writes)
        return dep

    def fence_all(self):
        deps = list(self.last_writer.values())
        for e in ENGINES:
            for d in deps:
                self._wait(e, d)

    def barrier(self):
        deps = list(self.last_writer.values())
        for keydeps in self.readers.values():
            deps.extend(keydeps)
        self.barrier_deps = [deps]

    def finalize(self):
        block = self.ctx.enter_context(self.nc.Block())
        for e in ENGINES:
            ops = self.ops[e]
            if not ops:
                continue

            def body(E, ops=ops):
                for f in ops:
                    f(E)

            getattr(block, e)(body)
        return block


def build_program(upto=4):
    nc = bass.Bass("TRN2", target_bir_lowering=False)
    ctx = ExitStack()
    p = Prog(nc, ctx)
    sb = ctx.enter_context

    def din(name, shape, dt=F32):
        return nc.dram_tensor(name, list(shape), dt, kind="ExternalInput").ap()

    def dint(name, shape, dt=F32):
        return nc.dram_tensor(name, list(shape), dt, kind="Internal").ap()

    x_in = din("x_in", (D, CHUNK))
    negslopes = din("negslopes", (128, 4))
    zsel = din("zsel", (128, 4))
    qshard = din("qshard", (128, 81918))
    qrcp = din("qrcp", (1, 16))
    qeps = din("qeps", (1, 16))
    gidx = din("gidx", (128, 24), I32)
    oidx = din("oidx", (128, 32), I32)
    wcat_st = din("wcat_st", (48, 128, 2048))
    bmix = din("bmix", (128, 64))
    dbias_in = din("dbias", (128, 16))
    wout_st = din("wout_st", (16, 128, 2048))
    wu_st = din("wu_st", (64, 128, 2048))
    wg0_st = din("wg0_st", (64, 128, 2048))
    wd_st = din("wd_st", (16, 128, 8192))
    wq_st = din("wq_st", (16, 128, 2048))
    wk_st = din("wk_st", (4, 128, 2048))
    wv_st = din("wv_st", (4, 128, 2048))
    wo_st = din("wo_st", (16, 128, 2048))
    w1_st = din("w1_st", (43, 128, 2048))
    w3_st = din("w3_st", (43, 128, 2048))
    w2_st = din("w2_st", (16, 128, 5504))
    x_out = nc.dram_tensor("x_out", [D, CHUNK], F32, kind="ExternalOutput").ap()

    xhbm = [dint("xhbm0", (D, CHUNK)), dint("xhbm1", (D, CHUNK))]
    hhbm = dint("hhbm", (H0, CHUNK), BF16)
    qdh = dint("qdh", (16, 128, 8192), BF16)
    ags_in = dint("ags_in", (1, 16))
    ags_out = dint("ags_out", (8, 16))
    agz_in = dint("agz_in", (128, 32))
    agz_out = dint("agz_out", (512, 32))
    agkv_in = dint("agkv_in", (1024, CHUNK), BF16)
    agkv_out = dint("agkv_out", (4096, CHUNK), BF16)
    agq_in = dint("agq_in", (2048, CHUNK), BF16)
    agq_out = dint("agq_out", (2, 4096, CHUNK), BF16)
    ago_in = [dint(f"ago_in{q}", (512, 1024), BF16) for q in range(4)]
    ago_out = dint("ago_out", (4, 2048, 1024), BF16)
    agd_in = [dint(f"agd_in{q}", (4, 1024)) for q in range(4)]
    agd_out = dint("agd_out", (4, 16, 1024))

    ALL8 = [list(range(8))]
    BGRP = [[0, 1, 2, 3], [4, 5, 6, 7]]

    @contextmanager
    def scope():
        sx = ExitStack()
        try:
            yield sx
        finally:
            sx.close()
            p.barrier()

    xln = sb(nc.sbuf_tensor("xln", [128, FT * CHUNK], BF16))
    biasc = sb(nc.sbuf_tensor("biasc", [128, 64], F32))
    bdec = sb(nc.sbuf_tensor("bdec", [128, 16], F32))
    dbc = sb(nc.sbuf_tensor("dbc", [128, 16], F32))
    slopc = sb(nc.sbuf_tensor("slopc", [128, 4], F32))
    zselc = sb(nc.sbuf_tensor("zselc", [128, 4], F32))
    onescol = sb(nc.sbuf_tensor("onescol", [128, 1], BF16))
    shiftc = sb(nc.sbuf_tensor("shiftc", [128, 1], F32))

    scol = sb(nc.sbuf_tensor("scol", [128, 16], F32))
    rscol = sb(nc.sbuf_tensor("rscol", [128, 16], F32))
    statrow = sb(nc.sbuf_tensor("statrow", [1, 4 * CHUNK], F32))
    rrep = sb(nc.sbuf_tensor("rrep", [128, CHUNK], F32))
    murep = sb(nc.sbuf_tensor("murep", [128, CHUNK], F32))
    gidxs = sb(nc.sbuf_tensor("gidxs", [128, 24], I32))
    lnx = sb(nc.sbuf_tensor("lnx", [128, 4 * CHUNK], F32))
    lnsq = [sb(nc.sbuf_tensor(f"lnsq{b}", [128, CHUNK], BF16)) for b in range(2)]
    lntmp = sb(nc.sbuf_tensor("lntmp", [128, CHUNK], F32))
    oidxs = sb(nc.sbuf_tensor("oidxs", [128, 32], I32))

    ones1 = sb(nc.sbuf_tensor("ones1", [1, 128], F32))
    p.op("gpsimd", lambda E: E.memset(ones1[:], 1.0), writes=["ones1"])
    p.op("gpsimd", lambda E: E.memset(onescol[:], 1.0), writes=["onescol"])
    p.op("gpsimd", lambda E: E.memset(shiftc[:], -SHIFT), writes=["shiftc"])

    p.dma("sync", "cst", out=biasc[:], in_=bmix, writes=["biasc"])
    p.dma("sync", "cst", out=slopc[:], in_=negslopes, writes=["slopc"])
    p.dma("sync", "cst", out=zselc[:], in_=zsel, writes=["zselc"])
    p.dma("sync", "cst", out=dbc[:], in_=dbias_in, writes=["dbc"])
    p.dma("sync", "cst", out=gidxs[:], in_=gidx, writes=["gidxs"])
    p.dma("sync", "cst", out=oidxs[:], in_=oidx, writes=["oidxs"])
    p.op("vector", lambda E: E.tensor_add(out=bdec[:], in0=biasc[:, 16:32], in1=dbc[:]),
         reads=["biasc", "dbc"], writes=["bdec"])

    QN = [16384, 16384, 16384, 10922, 10922, 10922]
    QCNT = [float(H0 * D)] * 3 + [float(H1 * D)] * 3

    def qscale_make(sx):
        if True:
            qs = sx.enter_context(nc.sbuf_tensor("q_s", [128, 16384], F32))
            part = sx.enter_context(nc.sbuf_tensor("q_part", [128, 16], F32))
            part6 = sx.enter_context(nc.sbuf_tensor("q_p6", [128, 8], F32))
            ssum = sx.enter_context(nc.sbuf_tensor("q_ssum", [1, 16], F32))
            ag8 = sx.enter_context(nc.sbuf_tensor("q_ag8", [8, 16], F32))
            o8 = sx.enter_context(nc.sbuf_tensor("q_o8", [128, 1], F32))
            qr = sx.enter_context(nc.sbuf_tensor("q_qr", [1, 16], F32))
            qe = sx.enter_context(nc.sbuf_tensor("q_qe", [1, 16], F32))
            srow = sx.enter_context(nc.sbuf_tensor("q_srow", [1, 16], F32))
            rs16 = sx.enter_context(nc.sbuf_tensor("q_rs16", [1, 16], F32))
            pp = sx.enter_context(nc.psum_tensor("q_pp", [128, 512], F32))
            p.op("gpsimd", lambda E: E.memset(ssum[:], 0.0), writes=["q_ssum"])
            p.op("gpsimd", lambda E: E.memset(o8[:], 1.0), writes=["q_o8"])
            p.dma("sync", "cst", out=qr[:], in_=qrcp, writes=["q_qr"])
            p.dma("sync", "cst", out=qe[:], in_=qeps, writes=["q_qe"])
            units = []
            off = 0
            j = 0
            ranges = []
            for m, ncols in enumerate(QN):
                a0 = j
                done = 0
                while done < ncols:
                    n = min(8192, ncols - done)
                    units.append((j, off, n))
                    off += n
                    done += n
                    j += 1
                ranges.append((a0, j))

            def unit(uix):
                j, off, n = units[uix]
                lo = (j % 2) * 8192
                p.dma("sync", f"q{j % 2}", out=qs[:, lo:lo + n],
                      in_=qshard[:, off:off + n], writes=[f"q_s:{j % 2}"])
                p.op("vector",
                     lambda E, n=n, j=j, lo=lo: E.reduce_sum(
                         out=part[:, j:j + 1], in_=qs[:, lo:lo + n], axis=AX.X,
                         apply_absolute_value=True),
                     reads=[f"q_s:{j % 2}"], writes=[f"q_part:{j}"])

        def finalize():
            for m, (a0, a1) in enumerate(ranges):
                p.op("vector",
                     lambda E, m=m, a0=a0, a1=a1: E.reduce_sum(
                         out=part6[:, m:m + 1], in_=part[:, a0:a1], axis=AX.X),
                     reads=[f"q_part:{jj}" for jj in range(a0, a1)], writes=["q_p6"])
            p.op("tensor",
                 lambda E: E.matmul(pp[0:1, 0:6], o8[:, 0:1], part6[:, 0:6],
                                    start=True, stop=True),
                 reads=["q_p6", "q_o8"], writes=["q_pp"])
            p.op("vector", lambda E: E.tensor_copy(out=ssum[0:1, 0:6], in_=pp[0:1, 0:6]),
                 reads=["q_pp"], writes=["q_ssum"])
            p.dma("gpsimd", "agb", out=ags_in, in_=ssum[:], reads=["q_ssum"],
                  writes=["ags_in_d"])
            p.collective("AllGather", A.bypass, ALL8, [ags_in], [ags_out],
                         reads=["ags_in_d"], writes=["ags_out_d"])
            p.dma("gpsimd", "agb", out=ag8[:], in_=ags_out, reads=["ags_out_d"],
                  writes=["q_ag8"])
            p.op("tensor",
                 lambda E: E.matmul(pp[0:1, 16:32], o8[0:8, 0:1], ag8[:, :],
                                    start=True, stop=True),
                 reads=["q_ag8", "q_o8"], writes=["q_pp"])
            p.op("vector", lambda E: E.tensor_mul(out=srow[:], in0=pp[0:1, 16:32], in1=qr[:]),
                 reads=["q_pp", "q_qr"], writes=["q_srow"])
            p.op("vector", lambda E: E.tensor_add(out=srow[:], in0=srow[:], in1=qe[:]),
                 reads=["q_srow", "q_qe"], writes=["q_srow"])
            p.op("vector", lambda E: E.reciprocal(out=rs16[:], in_=srow[:]),
                 reads=["q_srow"], writes=["q_rs16"])
            p.op("tensor",
                 lambda E: E.matmul(pp[:, 32:48], ones1[0:1, :], srow[0:1, :],
                                    start=True, stop=True),
                 reads=["q_srow", "ones1"], writes=["q_pp"])
            p.op("scalar", lambda E: E.copy(out=scol[:, :], in_=pp[:, 32:48]),
                 reads=["q_pp"], writes=["scol"])
            p.op("tensor",
                 lambda E: E.matmul(pp[:, 48:64], ones1[0:1, :], rs16[0:1, :],
                                    start=True, stop=True),
                 reads=["q_rs16", "ones1"], writes=["q_pp"])
            p.op("scalar", lambda E: E.copy(out=rscol[:, :], in_=pp[:, 48:64]),
                 reads=["q_pp"], writes=["rscol"])
        return len(units), unit, finalize

    def replicate_row(ps_rep, ps_key, row_ap, row_key, dst, dst_key, n):
        for h0 in range(0, n, 512):
            w = min(512, n - h0)
            p.op("tensor",
                 lambda E, h0=h0, w=w: E.matmul(
                     ps_rep[:, 0:w], ones1[0:1, :], row_ap[0:1, h0:h0 + w],
                     start=True, stop=True),
                 reads=[row_key, "ones1"], writes=[ps_key])
            p.op("scalar",
                 lambda E, h0=h0, w=w: E.copy(out=dst[:, h0:h0 + w], in_=ps_rep[:, 0:w]),
                 reads=[ps_key], writes=[dst_key])

    def cast_chunk(dst, dst_key, src, src_keys, cols):
        p.op("scalar",
             lambda E: E.activation(out=dst[:, 0:cols], in_=src[:, 0:cols], func=AF.Copy),
             reads=src_keys, writes=[dst_key])

    def quant_chunk(dst, dst_key, src, src_keys, cols, mq, tmp, tmp_key):
        p.op("vector",
             lambda E: E.tensor_scalar(out=dst[:, 0:cols], in0=src[:, 0:cols],
                                       scalar1=rscol[:, mq:mq + 1], scalar2=0.5,
                                       op0=A.mult, op1=A.is_gt),
             reads=list(src_keys) + ["rscol"], writes=[dst_key])
        p.op("vector",
             lambda E: E.tensor_scalar(out=tmp[:, 0:cols], in0=src[:, 0:cols],
                                       scalar1=rscol[:, mq:mq + 1], scalar2=-0.5,
                                       op0=A.mult, op1=A.is_lt),
             reads=list(src_keys) + ["rscol"], writes=[tmp_key])
        p.op("vector",
             lambda E: E.tensor_sub(out=dst[:, 0:cols], in0=dst[:, 0:cols],
                                    in1=tmp[:, 0:cols]),
             reads=[dst_key, tmp_key], writes=[dst_key])

    lncnt = [0]

    def layernorm(x_src, xsrc_key, sx):
        i = lncnt[0]
        lncnt[0] += 1
        sq, tmp = lnsq, lntmp
        with scope() as px:
            psx = [px.enter_context(nc.psum_tensor(f"ln{i}_px{h}", [128, 512], F32))
                   for h in range(2)]
            psq = [px.enter_context(nc.psum_tensor(f"ln{i}_pq{h}", [128, 512], F32))
                   for h in range(2)]
            psr = px.enter_context(nc.psum_tensor(f"ln{i}_pr", [128, 512], F32))
            for f in range(FT):
                s4 = f % 4
                p.dma("scalar", f"lnx{f % 3}", out=lnx[:, s4 * CHUNK:(s4 + 1) * CHUNK],
                      in_=x_src[f * 128:(f + 1) * 128, :],
                      reads=([f"{xsrc_key}:{f}"] if xsrc_key else []),
                      writes=[f"lnx:{s4}"])
                p.op("vector",
                     lambda E, f=f, s4=s4: E.tensor_copy(out=xln[:, f * CHUNK:(f + 1) * CHUNK],
                                                         in_=lnx[:, s4 * CHUNK:(s4 + 1) * CHUNK]),
                     reads=[f"lnx:{s4}"], writes=[f"xln:{f}"])
                b = f % 2
                p.op("scalar",
                     lambda E, f=f, b=b: E.activation(
                         out=sq[b][:], in_=xln[:, f * CHUNK:(f + 1) * CHUNK],
                         func=AF.Square),
                     reads=[f"xln:{f}"], writes=[f"lnsq:{b}"])
                def statg(E, f=f, b=b):
                    last = None
                    for h in range(2):
                        E.matmul(psx[h][0:1, :], onescol[:, 0:1],
                                 xln[:, f * CHUNK + h * 512: f * CHUNK + (h + 1) * 512],
                                 start=(f == 0), stop=(f == FT - 1))
                        last = E.matmul(psq[h][0:1, :], onescol[:, 0:1],
                                        sq[b][:, h * 512:(h + 1) * 512],
                                        start=(f == 0), stop=(f == FT - 1))
                    return last
                p.op("tensor", statg,
                     reads=[f"xln:{f}", f"lnsq:{b}", "onescol"],
                     writes=["ln_px0", "ln_px1", "ln_pq0", "ln_pq1"])
            mu = statrow[:, 0:CHUNK]
            e2 = statrow[:, CHUNK:2 * CHUNK]
            veps = statrow[:, 2 * CHUNK:3 * CHUNK]
            sd = statrow[:, 3 * CHUNK:4 * CHUNK]
            rr = statrow[:, 2 * CHUNK:3 * CHUNK]
            mr = statrow[:, 3 * CHUNK:4 * CHUNK]
            for h in range(2):
                p.op("vector",
                     lambda E, h=h: E.tensor_scalar(
                         out=mu[:, h * 512:(h + 1) * 512], in0=psx[h][0:1, :],
                         scalar1=1.0 / D, scalar2=None, op0=A.mult),
                     reads=[f"ln_px{h}"], writes=["ln_mu"])
                p.op("vector",
                     lambda E, h=h: E.tensor_scalar(
                         out=e2[:, h * 512:(h + 1) * 512], in0=psq[h][0:1, :],
                         scalar1=1.0 / D, scalar2=EPS_LN, op0=A.mult, op1=A.add),
                     reads=[f"ln_pq{h}"], writes=["ln_e2"])
            p.op("vector", lambda E: E.tensor_mul(out=veps[:], in0=mu[:], in1=mu[:]),
                 reads=["ln_mu"], writes=["ln_veps"])
            p.op("vector", lambda E: E.tensor_sub(out=veps[:], in0=e2[:], in1=veps[:]),
                 reads=["ln_e2", "ln_veps"], writes=["ln_veps"])
            p.op("scalar", lambda E: E.activation(out=sd[:], in_=veps[:], func=AF.Sqrt),
                 reads=["ln_veps"], writes=["ln_sd"])
            p.op("vector", lambda E: E.reciprocal(out=rr[:], in_=sd[:]),
                 reads=["ln_sd"], writes=["ln_rr", "ln_veps"])
            p.op("vector", lambda E: E.tensor_mul(out=mr[:], in0=mu[:], in1=rr[:]),
                 reads=["ln_mu", "ln_rr"], writes=["ln_mr", "ln_sd"])
            replicate_row(psr, "ln_psr", rr, "ln_rr", rrep, "rrep", CHUNK)
            replicate_row(psr, "ln_psr", mr, "ln_mr", murep, "murep", CHUNK)
            for f in range(FT):
                s4 = f % 4
                p.dma("scalar", f"lnx{f % 3}", out=lnx[:, s4 * CHUNK:(s4 + 1) * CHUNK],
                      in_=x_src[f * 128:(f + 1) * 128, :],
                      reads=([f"{xsrc_key}:{f}"] if xsrc_key else []),
                      writes=[f"lnx:{s4}"])
                p.op("vector",
                     lambda E, f=f, s4=s4: E.tensor_mul(
                         out=tmp[:], in0=lnx[:, s4 * CHUNK:(s4 + 1) * CHUNK], in1=rrep[:]),
                     reads=[f"lnx:{s4}", "rrep"], writes=["ln_tmp"])
                p.op("vector",
                     lambda E, f=f: E.tensor_sub(
                         out=xln[:, f * CHUNK:(f + 1) * CHUNK], in0=tmp[:], in1=murep[:]),
                     reads=["ln_tmp", "murep"], writes=[f"xln:{f}"])

    prcnt = [0]

    def proj_residual(sx, w_st, n_e, n_d, mov_key, mov_buf, xold_src, xold_key,
                      xnew_dst, xnew_key, bias_off=None, quant_m=None, filler=None):
        i = prcnt[0]
        prcnt[0] += 1
        cc = n_d * 128
        wf = [sx.enter_context(nc.sbuf_tensor(f"pr{i}_wf{b}", [128, cc], F32)) for b in range(2)]
        wb = [sx.enter_context(nc.sbuf_tensor(f"pr{i}_wb{b}", [128, cc], BF16)) for b in range(2)]
        xo = [sx.enter_context(nc.sbuf_tensor(f"pr{i}_xo{b}", [128, CHUNK], F32)) for b in range(2)]
        xn = [sx.enter_context(nc.sbuf_tensor(f"pr{i}_xn{b}", [128, CHUNK], F32)) for b in range(2)]
        with scope() as px:
            pse = [px.enter_context(nc.psum_tensor(f"pr{i}_ps{b}", [128, CHUNK], F32))
                   for b in range(2)]
            assert quant_m is None

            def prep(e):
                b = e % 2
                p.dma("sync", f"prw{b}", out=wf[b][:], in_=w_st[e], writes=[f"pr_wf{b}"])
                cast_chunk(wb[b], f"pr_wb{b}", wf[b], [f"pr_wf{b}"], cc)
                p.dma("scalar", f"prx{b}", out=xo[b][:],
                      in_=xold_src[e * 128:(e + 1) * 128, :],
                      reads=([f"{xold_key}:{e}"] if xold_key else []),
                      writes=[f"pr_xo{b}"])

            prep(0)
            for e in range(n_e):
                b = e % 2
                if e + 1 < n_e:
                    prep(e + 1)
                if filler is not None:
                    filler(e)
                def mmgrp(E, b=b):
                    last = None
                    for d in range(n_d):
                        for h in range(2):
                            last = E.matmul(
                                pse[b][:, h * 512:(h + 1) * 512],
                                wb[b][:, d * 128:(d + 1) * 128],
                                mov_buf[:, d * CHUNK + h * 512: d * CHUNK + (h + 1) * 512],
                                start=(d == 0), stop=(d == n_d - 1))
                    return last
                p.op("tensor", mmgrp,
                     reads=[f"pr_wb{b}"] + [f"{mov_key}:{d}" for d in range(n_d)],
                     writes=[f"pr_ps{b}"])
                if quant_m is not None:
                    p.op("vector",
                         lambda E, b=b, m=quant_m: E.scalar_tensor_tensor(
                             out=xn[b][:], in0=pse[b][:], scalar=scol[:, m:m + 1],
                             in1=xo[b][:], op0=A.mult, op1=A.add),
                         reads=[f"pr_ps{b}", "scol", f"pr_xo{b}"], writes=[f"pr_xn{b}"])
                elif bias_off is not None:
                    p.op("vector",
                         lambda E, b=b, e=e: E.scalar_tensor_tensor(
                             out=xn[b][:], in0=pse[b][:],
                             scalar=biasc[:, bias_off + e:bias_off + e + 1],
                             in1=xo[b][:], op0=A.add, op1=A.add),
                         reads=[f"pr_ps{b}", "biasc", f"pr_xo{b}"], writes=[f"pr_xn{b}"])
                else:
                    p.op("vector",
                         lambda E, b=b: E.tensor_add(out=xn[b][:], in0=pse[b][:], in1=xo[b][:]),
                         reads=[f"pr_ps{b}", f"pr_xo{b}"], writes=[f"pr_xn{b}"])
                p.dma("scalar", f"pro{b}", out=xnew_dst[e * 128:(e + 1) * 128, :],
                      in_=xn[b][:], reads=[f"pr_xn{b}"], writes=[f"{xnew_key}:{e}"])

    def phase_l0_mixer():
        with scope() as sx:
            layernorm(x_in, None, sx)
        with scope() as sx:
            ubuf = sx.enter_context(nc.sbuf_tensor("m_u", [128, FT * CHUNK], BF16))
            vbuf = sx.enter_context(nc.sbuf_tensor("m_v", [128, FT * CHUNK], BF16))
            wf = [sx.enter_context(nc.sbuf_tensor(f"m_wf{j}", [128, 2048], F32)) for j in range(3)]
            wb = [sx.enter_context(nc.sbuf_tensor(f"m_wb{j}", [128, 2048], BF16)) for j in range(6)]
            td = [sx.enter_context(nc.sbuf_tensor(f"m_td{b}", [128, CHUNK], F32)) for b in range(1)]
            ti = sx.enter_context(nc.sbuf_tensor("m_ti", [128, CHUNK], F32))
            tb = sx.enter_context(nc.sbuf_tensor("m_tb", [128, CHUNK], F32))
            zacc = sx.enter_context(nc.sbuf_tensor("m_za", [128, 32], F32))

            with scope() as px:
                ps = {n: [px.enter_context(nc.psum_tensor(f"m_ps_{n}{h}", [128, 512], F32))
                          for h in range(2)]
                      for n in ("inp", "dec", "gate")}
                def mprep(f):
                    par = (f % 2) * 3
                    for j in range(3):
                        p.dma("sync", f"mw{j}", out=wf[j][:], in_=wcat_st[3 * f + j],
                              writes=[f"m_wf{j}"])
                        cast_chunk(wb[par + j], f"m_wb{par + j}", wf[j], [f"m_wf{j}"], 2048)

                mprep(0)
                for f in range(FT):
                    par = (f % 2) * 3
                    if f + 1 < FT:
                        mprep(f + 1)
                    for j, pr in enumerate(("inp", "dec", "gate")):
                        for h in range(2):
                            def mmg(E, j=j, pr=pr, par=par, h=h):
                                last = None
                                for d in range(FT):
                                    last = E.matmul(
                                        ps[pr][h][:, :],
                                        wb[par + j][:, d * 128:(d + 1) * 128],
                                        xln[:, d * CHUNK + h * 512: d * CHUNK + (h + 1) * 512],
                                        start=(d == 0), stop=(d == FT - 1))
                                return last
                            p.op("tensor", mmg,
                                 reads=[f"m_wb{par + j}"] + [f"xln:{d}" for d in range(FT)],
                                 writes=[f"m_ps_{pr}{h}"])
                    b = 0
                    for h in range(2):
                        p.op("scalar",
                             lambda E, f=f, b=b, h=h: E.activation(
                                 out=td[b][:, h * 512:(h + 1) * 512], in_=ps["dec"][h][:],
                                 func=AF.Sigmoid, bias=bdec[:, f:f + 1]),
                             reads=[f"m_ps_dec{h}", "bdec"], writes=[f"m_td{b}"])
                        p.op("scalar",
                             lambda E, f=f, h=h: E.activation(
                                 out=ti[:, h * 512:(h + 1) * 512], in_=ps["inp"][h][:],
                                 func=AF.Silu, bias=biasc[:, f:f + 1]),
                             reads=[f"m_ps_inp{h}", "biasc"], writes=["m_ti"])
                        p.op("scalar",
                             lambda E, f=f, h=h: E.activation(
                                 out=vbuf[:, f * CHUNK + h * 512:f * CHUNK + (h + 1) * 512],
                                 in_=ps["gate"][h][:],
                                 func=AF.Sigmoid, bias=biasc[:, 32 + f:33 + f]),
                             reads=[f"m_ps_gate{h}", "biasc"], writes=[f"m_v:{f}"])
                    p.op("vector",
                         lambda E, b=b: E.tensor_scalar(out=tb[:], in0=td[b][:],
                                                        scalar1=-1.0, scalar2=1.0,
                                                        op0=A.mult, op1=A.add),
                         reads=[f"m_td{b}"], writes=["m_tb"])
                    p.op("vector", lambda E: E.tensor_mul(out=tb[:], in0=tb[:], in1=ti[:]),
                         reads=["m_tb", "m_ti"], writes=["m_tb"])
                    p.op("vector",
                         lambda E, b=b: E.tensor_tensor_scan(
                             out=ti[:], data0=td[b][:], data1=tb[:], initial=0.0,
                             op0=A.mult, op1=A.add),
                         reads=[f"m_td{b}", "m_tb"], writes=["m_ti"])
                    p.op("vector",
                         lambda E, b=b: E.tensor_tensor_scan(
                             out=tb[:], data0=td[b][:], data1=td[b][:], initial=1.0,
                             op0=A.mult, op1=A.bypass),
                         reads=[f"m_td{b}"], writes=["m_tb"])
                    p.op("vector",
                         lambda E, f=f: E.tensor_mul(
                             out=ubuf[:, f * CHUNK:(f + 1) * CHUNK],
                             in0=vbuf[:, f * CHUNK:(f + 1) * CHUNK], in1=ti[:]),
                         reads=[f"m_v:{f}", "m_ti"], writes=[f"m_u:{f}"])
                    p.op("vector",
                         lambda E, f=f: E.tensor_mul(
                             out=vbuf[:, f * CHUNK:(f + 1) * CHUNK],
                             in0=vbuf[:, f * CHUNK:(f + 1) * CHUNK], in1=tb[:]),
                         reads=[f"m_v:{f}", "m_tb"], writes=[f"m_v:{f}"])
                    p.op("vector",
                         lambda E, f=f: E.tensor_copy(out=zacc[:, f:f + 1],
                                                      in_=ti[:, CHUNK - 1:CHUNK]),
                         reads=["m_ti"], writes=[f"m_za:{f}"])
                    p.op("vector",
                         lambda E, f=f: E.tensor_copy(out=zacc[:, 16 + f:17 + f],
                                                      in_=tb[:, CHUNK - 1:CHUNK]),
                         reads=["m_tb"], writes=[f"m_za:{16 + f}"])
            p.dma("gpsimd", "agb", out=agz_in, in_=zacc[:],
                  reads=[f"m_za:{j}" for j in range(32)], writes=["agz_in_d"])
            p.collective("AllGather", A.bypass, BGRP, [agz_in], [agz_out],
                         reads=["agz_in_d"], writes=["agz_out_d"])
            zg = sx.enter_context(nc.sbuf_tensor("m_zg", [128, 128], F32))
            for j in range(4):
                p.dma("gpsimd", "agb", out=zg[:, 32 * j:32 * (j + 1)],
                      in_=agz_out[128 * j:128 * (j + 1), :], reads=["agz_out_d"],
                      writes=[f"m_zg:{j}"])
            u = [sx.enter_context(nc.sbuf_tensor(f"m_uu{j}", [128, 16], F32))
                 for j in range(4)]
            zin = sx.enter_context(nc.sbuf_tensor("m_zin", [128, 16], F32))
            p.op("gpsimd", lambda E: E.memset(u[0][:], 0.0), writes=["m_uu0"])
            p.op("vector", lambda E: E.tensor_copy(out=u[1][:], in_=zg[:, 0:16]),
                 reads=["m_zg:0"], writes=["m_uu1"])
            for j in (2, 3):
                p.op("vector",
                     lambda E, j=j: E.tensor_mul(
                         out=u[j][:], in0=zg[:, 32 * (j - 1) + 16:32 * (j - 1) + 32],
                         in1=u[j - 1][:]),
                     reads=[f"m_zg:{j - 1}", f"m_uu{j - 1}"], writes=[f"m_uu{j}"])
                p.op("vector",
                     lambda E, j=j: E.tensor_add(
                         out=u[j][:], in0=u[j][:], in1=zg[:, 32 * (j - 1):32 * (j - 1) + 16]),
                     reads=[f"m_uu{j}", f"m_zg:{j - 1}"], writes=[f"m_uu{j}"])
            p.op("gpsimd", lambda E: E.memset(zin[:], 0.0), writes=["m_zin"])
            for j in range(4):
                p.op("vector",
                     lambda E, j=j: E.scalar_tensor_tensor(
                         out=zin[:], in0=u[j][:], scalar=zselc[:, j:j + 1], in1=zin[:],
                         op0=A.mult, op1=A.add),
                     reads=[f"m_uu{j}", "zselc", "m_zin"], writes=["m_zin"])
            for f in range(FT):
                p.op("vector",
                     lambda E, f=f: E.scalar_tensor_tensor(
                         out=xln[:, f * CHUNK:(f + 1) * CHUNK],
                         in0=vbuf[:, f * CHUNK:(f + 1) * CHUNK],
                         scalar=zin[:, f:f + 1],
                         in1=ubuf[:, f * CHUNK:(f + 1) * CHUNK],
                         op0=A.mult, op1=A.add),
                     reads=[f"m_v:{f}", "m_zin", f"m_u:{f}"], writes=[f"xln:{f}"])
        with scope() as sx:
            nunits, qunit, qfin = qscale_make(sx)

            def filler(e, done=[0]):
                while done[0] * 16 < (e + 1) * nunits:
                    qunit(done[0])
                    done[0] += 1

            proj_residual(sx, wout_st, 16, FT, "xln", xln, x_in, None,
                          xhbm[0], "xh0", bias_off=48, filler=filler)
            qfin()

    def phase_ffn(xold_src, xold_key, xnew_dst, xnew_key, wu, wg, wdn, n_et,
                  mu_, mg_, md_, final=False):
        with scope() as sx:
            layernorm(xold_src, xold_key, sx)
        with scope() as sx:
            wfu = [sx.enter_context(nc.sbuf_tensor(f"f{prcnt[0]}_wfu{b}", [128, 2048], F32)) for b in range(3)]
            wfg = [sx.enter_context(nc.sbuf_tensor(f"f{prcnt[0]}_wfg{b}", [128, 2048], F32)) for b in range(3)]
            qu = [sx.enter_context(nc.sbuf_tensor(f"f{prcnt[0]}_qu{b}", [128, 2048], BF16)) for b in range(3)]
            qg = [sx.enter_context(nc.sbuf_tensor(f"f{prcnt[0]}_qg{b}", [128, 2048], BF16)) for b in range(3)]
            tsil = [sx.enter_context(nc.sbuf_tensor(f"f{prcnt[0]}_ts{b}", [128, CHUNK], F32)) for b in range(2)]
            hb = [sx.enter_context(nc.sbuf_tensor(f"f{prcnt[0]}_hb{b}", [128, CHUNK], BF16)) for b in range(2)]
            qm = [sx.enter_context(nc.sbuf_tensor(f"f{prcnt[0]}_qm{b}", [128, 2048], BF16)) for b in range(4)]
            dwf = [sx.enter_context(nc.sbuf_tensor(f"f{prcnt[0]}_dwf{b}", [128, 2048], F32)) for b in range(1)]
            dq = [sx.enter_context(nc.sbuf_tensor(f"f{prcnt[0]}_dq{b}", [128, 2048], BF16)) for b in range(2)]
            dqm = [sx.enter_context(nc.sbuf_tensor(f"f{prcnt[0]}_dqm{b}", [128, 2048], BF16)) for b in range(1)]
            cc = n_et * 128
            nq = (cc + 2047) // 2048
            dj_total = 16 * nq
            djq = 0
            with scope() as px:
                psu = [px.enter_context(nc.psum_tensor(f"f{prcnt[0]}_pu{b}", [128, CHUNK], F32)) for b in range(2)]
                psg = [px.enter_context(nc.psum_tensor(f"f{prcnt[0]}_pg{b}", [128, CHUNK], F32)) for b in range(2)]

                def down_prep_unit():
                    nonlocal djq
                    if djq >= dj_total:
                        return
                    j, qq = djq // nq, djq % nq
                    c0 = qq * 2048
                    c1 = min(cc, c0 + 2048)
                    bb = 0
                    p.dma("sync", f"dpw{bb}", out=dwf[bb][:, 0:c1 - c0],
                          in_=wdn[j][:, c0:c1], writes=[f"f_dwf{bb}"])
                    quant_chunk(dq[bb], f"f_dq{bb}", dwf[bb], [f"f_dwf{bb}"],
                                c1 - c0, md_, dqm[0], "f_dqm0")
                    p.dma("scalar", f"dpo{bb}", out=qdh[j][:, c0:c1],
                          in_=dq[bb][:, 0:c1 - c0], reads=[f"f_dq{bb}"],
                          writes=[f"qdh:{j}"])
                    djq += 1

                def fprep(e):
                    b3 = e % 3
                    b = e % 2
                    p.dma("sync", f"fu{b3}", out=wfu[b3][:], in_=wu[e], writes=[f"f_wfu{b3}"])
                    p.dma("sync", f"fg{b3}", out=wfg[b3][:], in_=wg[e], writes=[f"f_wfg{b3}"])
                    quant_chunk(qu[b3], f"f_qu{b3}", wfu[b3], [f"f_wfu{b3}"], 2048, mu_,
                                qm[b], f"f_qm{b}")
                    quant_chunk(qg[b3], f"f_qg{b3}", wfg[b3], [f"f_wfg{b3}"], 2048, mg_,
                                qm[2 + b], f"f_qm{2 + b}")

                fprep(0)
                for e in range(n_et):
                    b3 = e % 3
                    b = e % 2
                    if e + 1 < n_et:
                        fprep(e + 1)
                    while djq * n_et < (e + 1) * dj_total:
                        down_prep_unit()
                    def uq(E, b=b, b3=b3):
                        last = None
                        for d in range(FT):
                            for h in range(2):
                                last = E.matmul(
                                    psu[b][:, h * 512:(h + 1) * 512],
                                    qu[b3][:, d * 128:(d + 1) * 128],
                                    xln[:, d * CHUNK + h * 512: d * CHUNK + (h + 1) * 512],
                                    start=(d == 0), stop=(d == FT - 1))
                        return last
                    p.op("tensor", uq,
                         reads=[f"f_qu{b3}"] + [f"xln:{d}" for d in range(FT)],
                         writes=[f"f_pu{b}"])
                    def gq(E, b=b, b3=b3):
                        last = None
                        for d in range(FT):
                            for h in range(2):
                                last = E.matmul(
                                    psg[b][:, h * 512:(h + 1) * 512],
                                    qg[b3][:, d * 128:(d + 1) * 128],
                                    xln[:, d * CHUNK + h * 512: d * CHUNK + (h + 1) * 512],
                                    start=(d == 0), stop=(d == FT - 1))
                        return last
                    p.op("tensor", gq,
                         reads=[f"f_qg{b3}"] + [f"xln:{d}" for d in range(FT)],
                         writes=[f"f_pg{b}"])
                    p.op("scalar",
                         lambda E, b=b, m=mu_: E.activation(
                             out=tsil[b][:], in_=psu[b][:], func=AF.Silu,
                             scale=scol[:, m:m + 1]),
                         reads=[f"f_pu{b}", "scol"], writes=[f"f_ts{b}"])
                    p.op("vector",
                         lambda E, b=b, m=mg_: E.scalar_tensor_tensor(
                             out=hb[b][:], in0=psg[b][:], scalar=scol[:, m:m + 1],
                             in1=tsil[b][:], op0=A.mult, op1=A.mult),
                         reads=[f"f_pg{b}", "scol", f"f_ts{b}"], writes=[f"f_hb{b}"])
                    p.dma("scalar", f"fh{b}", out=hhbm[e * 128:(e + 1) * 128, :],
                          in_=hb[b][:], reads=[f"f_hb{b}"], writes=[f"hh:{e}"])
                while djq < dj_total:
                    down_prep_unit()
        i = prcnt[0]
        prcnt[0] += 1
        cc = n_et * 128
        nq = (cc + 2047) // 2048
        with scope() as sx:
            hsb = sx.enter_context(nc.sbuf_tensor(f"fd{i}_h", [128, n_et * 512], BF16))
            qd = [sx.enter_context(nc.sbuf_tensor(f"fd{i}_qd{b}", [128, cc], BF16)) for b in range(3)]
            xo = [sx.enter_context(nc.sbuf_tensor(f"fd{i}_xo{b}", [128, 512], F32)) for b in range(2)]
            xn = [sx.enter_context(nc.sbuf_tensor(f"fd{i}_xn{b}", [128, 512], F32)) for b in range(2)]
            with scope() as px:
                psd = [px.enter_context(nc.psum_tensor(f"fd{i}_ps{b}", [128, 512], F32))
                       for b in range(2)]
                for th in range(2):
                    for e in range(n_et):
                        p.dma("scalar", f"fhr{e % 3}",
                              out=hsb[:, e * 512:(e + 1) * 512],
                              in_=hhbm[e * 128:(e + 1) * 128, th * 512:(th + 1) * 512],
                              reads=[f"hh:{e}"], writes=[f"fd_h:{e}"])
                    def dprep(e, th=th):
                        b3 = e % 3
                        b = e % 2
                        p.dma("sync", f"fdw{b3}", out=qd[b3][:, 0:cc],
                              in_=qdh[e][:, 0:cc], reads=[f"qdh:{e}"],
                              writes=[f"fd_qd{b3}"])
                        p.dma("scalar", f"fdx{b}", out=xo[b][:],
                              in_=xold_src[e * 128:(e + 1) * 128, th * 512:(th + 1) * 512],
                              reads=([f"{xold_key}:{e}"] if xold_key else []),
                              writes=[f"fd_xo{b}"])

                    dprep(0)
                    for e in range(16):
                        b3 = e % 3
                        b = e % 2
                        if e + 1 < 16:
                            dprep(e + 1)
                        def dmg(E, b=b, b3=b3):
                            last = None
                            for d in range(n_et):
                                last = E.matmul(
                                    psd[b][:, :], qd[b3][:, d * 128:(d + 1) * 128],
                                    hsb[:, d * 512:(d + 1) * 512],
                                    start=(d == 0), stop=(d == n_et - 1))
                            return last
                        p.op("tensor", dmg,
                             reads=[f"fd_qd{b3}"] + [f"fd_h:{d}" for d in range(n_et)],
                             writes=[f"fd_ps{b}"])
                        p.op("vector",
                             lambda E, b=b, m=md_: E.scalar_tensor_tensor(
                                 out=xn[b][:], in0=psd[b][:], scalar=scol[:, m:m + 1],
                                 in1=xo[b][:], op0=A.mult, op1=A.add),
                             reads=[f"fd_ps{b}", "scol", f"fd_xo{b}"], writes=[f"fd_xn{b}"])
                        p.dma("scalar", f"fdo{b}",
                              out=xnew_dst[e * 128:(e + 1) * 128, th * 512:(th + 1) * 512],
                              in_=xn[b][:], reads=[f"fd_xn{b}"],
                              writes=[f"{xnew_key}:{e}"])

    def phase_attn(xold_src, xold_key, xnew_dst, xnew_key):
        with scope() as sx:
            layernorm(xold_src, xold_key, sx)
        with scope() as sx:
            wf = [sx.enter_context(nc.sbuf_tensor(f"a_wf{b}", [128, 2048], F32)) for b in range(2)]
            wb = [sx.enter_context(nc.sbuf_tensor(f"a_wb{b}", [128, 2048], BF16)) for b in range(2)]
            qt = [sx.enter_context(nc.sbuf_tensor(f"a_qt{b}", [128, CHUNK], BF16)) for b in range(2)]
            with scope() as px:
                psq = [px.enter_context(nc.psum_tensor(f"a_pq{b}", [128, CHUNK], F32)) for b in range(2)]
                for (w_st, n_e, dst, row0, scale) in (
                        (wk_st, 4, agkv_in, 0, 1.0),
                        (wv_st, 4, agkv_in, 512, 1.0),
                        (wq_st, 16, agq_in, 0, 1.0 / float(np.sqrt(128.0)))):
                    if w_st is wq_st:
                        p.collective("AllGather", A.bypass, BGRP, [agkv_in], [agkv_out],
                                     reads=["agkv_in_d"], writes=["agkv_out_d"])
                    def aprep(e, w_st=w_st):
                        b = e % 2
                        p.dma("sync", f"aw{b}", out=wf[b][:], in_=w_st[e], writes=[f"a_wf{b}"])
                        cast_chunk(wb[b], f"a_wb{b}", wf[b], [f"a_wf{b}"], 2048)

                    aprep(0)
                    for e in range(n_e):
                        if w_st is wq_st and e == 8:
                            p.collective("AllGather", A.bypass, BGRP,
                                         [agq_in[0:1024, :]], [agq_out[0]],
                                         reads=["agq_in_d0"], writes=["agq_out_d"])
                        b = e % 2
                        if e + 1 < n_e:
                            aprep(e + 1)
                        def qmg(E, b=b):
                            last = None
                            for d in range(FT):
                                for h in range(2):
                                    last = E.matmul(
                                        psq[b][:, h * 512:(h + 1) * 512],
                                        wb[b][:, d * 128:(d + 1) * 128],
                                        xln[:, d * CHUNK + h * 512: d * CHUNK + (h + 1) * 512],
                                        start=(d == 0), stop=(d == FT - 1))
                            return last
                        p.op("tensor", qmg,
                             reads=[f"a_wb{b}"] + [f"xln:{d}" for d in range(FT)],
                             writes=[f"a_pq{b}"])
                        p.op("scalar",
                             lambda E, b=b, scale=scale: E.activation(
                                 out=qt[b][:], in_=psq[b][:], func=AF.Copy, scale=scale),
                             reads=[f"a_pq{b}"], writes=[f"a_qt{b}"])
                        p.dma("scalar", f"aq{b}",
                              out=dst[row0 + e * 128: row0 + (e + 1) * 128, :],
                              in_=qt[b][:], reads=[f"a_qt{b}"],
                              writes=["agkv_in_d" if dst is agkv_in
                                      else f"agq_in_d{e // 8}"])
        p.collective("AllGather", A.bypass, BGRP, [agq_in[1024:2048, :]], [agq_out[1]],
                     reads=["agq_in_d1"], writes=["agq_out_d"])
        with scope() as sx:
            qg = sx.enter_context(nc.sbuf_tensor("at_q", [128, 4 * S], BF16))
            kg = sx.enter_context(nc.sbuf_tensor("at_k", [128, S], BF16))
            vf = sx.enter_context(nc.sbuf_tensor("at_vf", [128, S], BF16))
            vt = sx.enter_context(nc.sbuf_tensor("at_vt", [128, S], BF16))
            idn = sx.enter_context(nc.sbuf_tensor("at_idn", [128, 128], BF16))
            mstr = sx.enter_context(nc.sbuf_tensor("at_m", [128, 32 * 512], BF16))
            dtil = [sx.enter_context(nc.sbuf_tensor(f"at_d{b}", [128, 512], F32)) for b in range(2)]
            stt_ = [sx.enter_context(nc.sbuf_tensor(f"at_s{b}", [128, 1024], F32)) for b in range(3)]
            ptt = [sx.enter_context(nc.sbuf_tensor(f"at_p{b}", [128, 1024], BF16)) for b in range(4)]
            ob = [sx.enter_context(nc.sbuf_tensor(f"at_ob{b}", [128, 512], BF16)) for b in range(2)]
            for j in range(4):
                p.idma("gq", out=kg[:, j * CHUNK:(j + 1) * CHUNK], in_=agkv_out[:, :],
                       idx_ap=gidxs[:, 16 + j:17 + j],
                       reads=["agkv_out_d", "gidxs"], writes=[f"at_k:{j}"])
                p.idma("gq", out=vf[:, j * CHUNK:(j + 1) * CHUNK], in_=agkv_out[:, :],
                       idx_ap=gidxs[:, 20 + j:21 + j],
                       reads=["agkv_out_d", "gidxs"], writes=[f"at_vf:{j}"])
            agq_v = agq_out.rearrange("a b c -> (a b) c")
            for r in range(4):
                for j in range(4):
                    p.idma("gq", out=qg[:, (r * 4 + j) * CHUNK:(r * 4 + j + 1) * CHUNK],
                           in_=agq_v, idx_ap=gidxs[:, r * 4 + j:r * 4 + j + 1],
                           reads=["agq_out_d", "gidxs"], writes=[f"at_q:{r}:{j}"])
            p.op("gpsimd", lambda E: E.memset(idn[:], 1.0), writes=["at_idn"])
            p.op("gpsimd",
                 lambda E: E.affine_select(out=idn[:], in_=idn[:], pattern=[[-1, 128]],
                                           compare_op=A.is_equal, fill=0.0, base=0,
                                           channel_multiplier=1),
                 reads=["at_idn"], writes=["at_idn"])
            with scope() as px:
                pst = [px.enter_context(nc.psum_tensor(f"at_pt{b}", [128, 128], BF16))
                       for b in range(2)]
                for kt in range(32):
                    b = kt % 2
                    p.op("tensor",
                         lambda E, kt=kt, b=b: E.transpose(
                             pst[b][:, :], vf[:, kt * 128:(kt + 1) * 128], idn[:]),
                         reads=[f"at_vf:{kt // 8}", "at_idn"], writes=[f"at_pt{b}"])
                    p.op("scalar",
                         lambda E, kt=kt, b=b: E.copy(
                             out=vt[:, kt * 128:(kt + 1) * 128], in_=pst[b][:, :]),
                         reads=[f"at_pt{b}"], writes=[f"at_vt:{kt}"])
            with scope() as px:
                pssc = px.enter_context(nc.psum_tensor("at_sc", [128, 3072], F32))
                pso = px.enter_context(nc.psum_tensor("at_o", [128, 512], F32))
                psd = px.enter_context(nc.psum_tensor("at_dn", [128, 512], F32))
                dnrow = sx.enter_context(nc.sbuf_tensor("at_dnr", [1, 512], F32))
                for qs in range(8):
                    nkt = 4 * qs + 4
                    for kt in range(nkt):
                        b = kt % 2
                        p.op("gpsimd",
                             lambda E, qs=qs, kt=kt, b=b: E.iota(
                                 dtil[b][:], pattern=[[1, 512]], base=512 * qs - 128 * kt,
                                 channel_multiplier=-1,
                                 allow_small_or_imprecise_dtypes=True),
                             writes=[f"at_d{b}"])
                        p.op("scalar",
                             lambda E, kt=kt, b=b: E.activation(
                                 out=mstr[:, kt * 512:(kt + 1) * 512], in_=dtil[b][:],
                                 func=AF.Sqrt),
                             reads=[f"at_d{b}"], writes=[f"at_m:{kt}"])
                    for r in range(4):
                        def qkpair(kp, r=r, qs=qs):
                            k0 = 2 * kp
                            s0 = (kp % 3) * 1024
                            bp = kp % 4
                            bs = kp % 3
                            def qkg(E, k0=k0, s0=s0, r=r, qs=qs):
                                E.matmul(pssc[:, s0:s0 + 512],
                                         kg[:, k0 * 128:(k0 + 1) * 128],
                                         qg[:, r * S + qs * 512: r * S + (qs + 1) * 512],
                                         start=True, stop=True)
                                return E.matmul(pssc[:, s0 + 512:s0 + 1024],
                                                kg[:, (k0 + 1) * 128:(k0 + 2) * 128],
                                                qg[:, r * S + qs * 512: r * S + (qs + 1) * 512],
                                                start=True, stop=True)
                            p.op("tensor", qkg,
                                 reads=[f"at_k:{k0 // 8}", f"at_k:{(k0 + 1) // 8}",
                                        f"at_q:{r}:{qs // 2}"],
                                 writes=[f"at_scp{bs}"])
                            p.op("vector",
                                 lambda E, k0=k0, s0=s0, r=r, bs=bs: E.scalar_tensor_tensor(
                                     out=stt_[bs][:], in0=mstr[:, k0 * 512:(k0 + 2) * 512],
                                     scalar=slopc[:, r:r + 1], in1=pssc[:, s0:s0 + 1024],
                                     op0=A.mult, op1=A.add),
                                 reads=[f"at_m:{k0}", f"at_m:{k0 + 1}", "slopc", f"at_scp{bs}"],
                                 writes=[f"at_sp{bs}"])
                            p.op("scalar",
                                 lambda E, bp=bp, bs=bs: E.activation(
                                     out=ptt[bp][:], in_=stt_[bs][:], func=AF.Exp,
                                     bias=shiftc[:, 0:1]),
                                 reads=[f"at_sp{bs}", "shiftc"], writes=[f"at_pp{bp}"])
                            for kk in (k0, k0 + 1):
                                if kk >= 4 * qs:
                                    p.op("gpsimd",
                                         lambda E, qs=qs, kk=kk, bp=bp, k0=k0: E.affine_select(
                                             out=ptt[bp][:, (kk - k0) * 512:(kk - k0 + 1) * 512],
                                             in_=ptt[bp][:, (kk - k0) * 512:(kk - k0 + 1) * 512],
                                             pattern=[[1, 512]], compare_op=A.is_ge,
                                             fill=0.0, base=512 * qs - 128 * kk,
                                             channel_multiplier=-1),
                                         reads=[f"at_pp{bp}"], writes=[f"at_pp{bp}"])

                        def avdenpair(kp, nkt=nkt):
                            k0 = 2 * kp
                            bp = kp % 4
                            def avg_(E, k0=k0, bp=bp, nkt=nkt):
                                last = None
                                for kk in (k0, k0 + 1):
                                    E.matmul(pso[:, :], vt[:, kk * 128:(kk + 1) * 128],
                                             ptt[bp][:, (kk - k0) * 512:(kk - k0 + 1) * 512],
                                             start=(kk == 0), stop=(kk == nkt - 1))
                                    last = E.matmul(psd[0:1, :], onescol[:, 0:1],
                                                    ptt[bp][:, (kk - k0) * 512:(kk - k0 + 1) * 512],
                                                    start=(kk == 0), stop=(kk == nkt - 1))
                                return last
                            p.op("tensor", avg_,
                                 reads=[f"at_vt:{k0}", f"at_vt:{k0 + 1}", f"at_pp{bp}",
                                        "onescol"],
                                 writes=["at_pso", "at_psd"])

                        npair = nkt // 2
                        for kp in range(npair + 3):
                            if kp < npair:
                                qkpair(kp)
                            if kp >= 3:
                                avdenpair(kp - 3)
                        bo = r % 2
                        hh = qs // 2
                        c0 = (qs % 2) * 512
                        p.op("vector",
                             lambda E: E.tensor_copy(out=dnrow[:], in_=psd[0:1, :]),
                             reads=["at_psd"], writes=["at_dnr"])
                        p.dma("scalar", "aod",
                              out=agd_in[hh][r:r + 1, c0:c0 + 512],
                              in_=dnrow[:], reads=["at_dnr"], writes=[f"agd_in_d{hh}"])
                        p.op("scalar",
                             lambda E, bo=bo: E.copy(out=ob[bo][:], in_=pso[:, :]),
                             reads=["at_pso"], writes=[f"at_ob{bo}"])
                        p.dma("scalar", f"ao{bo}",
                              out=ago_in[hh][r * 128:(r + 1) * 128, c0:c0 + 512],
                              in_=ob[bo][:], reads=[f"at_ob{bo}"], writes=[f"ago_in_d{hh}"])
                    if qs % 2 == 1 and qs < 7:
                        q4 = qs // 2
                        p.collective("AllGather", A.bypass, BGRP, [ago_in[q4]], [ago_out[q4]],
                                     reads=[f"ago_in_d{q4}"], writes=["ago_out_d"])
                        p.collective("AllGather", A.bypass, BGRP, [agd_in[q4]], [agd_out[q4]],
                                     reads=[f"agd_in_d{q4}"], writes=["agd_out_d"])
        p.collective("AllGather", A.bypass, BGRP, [ago_in[3]], [ago_out[3]],
                     reads=["ago_in_d3"], writes=["ago_out_d"])
        p.collective("AllGather", A.bypass, BGRP, [agd_in[3]], [agd_out[3]],
                     reads=["agd_in_d3"], writes=["agd_out_d"])
        with scope() as sx:
            mo = sx.enter_context(nc.sbuf_tensor("ao_m", [128, FT * CHUNK], BF16))
            dng = [sx.enter_context(nc.sbuf_tensor(f"ao_dn{b}", [128, CHUNK], F32)) for b in range(2)]
            ago_v = ago_out.rearrange("a b c -> (a b) c")
            agd_v = agd_out.rearrange("a b c -> (a b) c")
            for f in range(FT):
                b = f % 2
                p.idma("go", out=mo[:, f * CHUNK:(f + 1) * CHUNK], in_=ago_v,
                       idx_ap=oidxs[:, f:f + 1],
                       reads=["ago_out_d", "oidxs"], writes=[f"ao_m:{f}"])
                p.idma("go", out=dng[b][:], in_=agd_v, idx_ap=oidxs[:, 16 + f:17 + f],
                       reads=["agd_out_d", "oidxs"], writes=[f"ao_dn{b}"])
                p.op("vector",
                     lambda E, b=b: E.reciprocal(out=dng[b][:], in_=dng[b][:]),
                     reads=[f"ao_dn{b}"], writes=[f"ao_dn{b}"])
                p.op("vector",
                     lambda E, f=f, b=b: E.tensor_mul(
                         out=mo[:, f * CHUNK:(f + 1) * CHUNK],
                         in0=mo[:, f * CHUNK:(f + 1) * CHUNK], in1=dng[b][:]),
                     reads=[f"ao_m:{f}", f"ao_dn{b}"], writes=[f"ao_m:{f}"])
            proj_residual(sx, wo_st, 16, FT, "ao_m", mo, xold_src, xold_key,
                          xnew_dst, xnew_key)

    if upto >= 1:
        phase_l0_mixer()
    cur, cur_key = (xhbm[0], "xh0") if upto >= 1 else (x_in, None)
    if upto >= 2:
        phase_ffn(cur, cur_key, xhbm[1], "xh1", wu_st, wg0_st, wd_st, ET0, 0, 1, 2)
        cur, cur_key = xhbm[1], "xh1"
    if upto >= 3:
        phase_attn(cur, cur_key, xhbm[0], "xh0")
        cur, cur_key = xhbm[0], "xh0"
    if upto >= 4:
        phase_ffn(cur, cur_key, x_out, "xout", w1_st, w3_st, w2_st, ET1, 3, 4, 5)
    else:
        with scope() as sx:
            t = [sx.enter_context(nc.sbuf_tensor(f"cp{b}", [128, CHUNK], F32)) for b in range(2)]
            for f in range(FT):
                b = f % 2
                p.dma("scalar", f"cpi{b}", out=t[b][:], in_=cur[f * 128:(f + 1) * 128, :],
                      reads=([f"{cur_key}:{f}"] if cur_key else []), writes=[f"cpt{b}"])
                p.dma("scalar", f"cpo{b}", out=x_out[f * 128:(f + 1) * 128, :], in_=t[b][:],
                      reads=[f"cpt{b}"], writes=[f"xout:{f}"])
    p.fence_all()
    p.finalize()
    return nc, p


def _wst(w, n_d, n_e):
    return np.ascontiguousarray(
        w.reshape(n_d, 128, n_e, 128).transpose(2, 1, 0, 3).reshape(n_e, 128, n_d * 128),
        dtype=np.float32)


def _col16(v):
    return np.ascontiguousarray(np.asarray(v, np.float32).reshape(16, 128).T)


def _host_inputs(inputs):
    f32 = lambda k: np.asarray(inputs[k], np.float32)
    x = f32("L0_attn_x")
    win, wdec, wgat, wout = (f32("L0_attn_W_in"), f32("L0_attn_W_decay"),
                             f32("L0_attn_W_gate"), f32("L0_attn_W_out"))
    wu, wg0, wd = f32("L0_ffn_W_up"), f32("L0_ffn_W_gate"), f32("L0_ffn_W_down")
    wq, wk, wv, wo = (f32("L1_attn_Wq"), f32("L1_attn_Wk"), f32("L1_attn_Wv"),
                      f32("L1_attn_Wo"))
    w1, w3, w2 = f32("L1_ffn_w1"), f32("L1_ffn_w3"), f32("L1_ffn_w2")

    win_s, wdec_s, wgat_s = _wst(win, 16, 16), _wst(wdec, 16, 16), _wst(wgat, 16, 16)
    wcat = np.empty((48, 128, 2048), np.float32)
    for f in range(16):
        wcat[3 * f + 0] = win_s[f]
        wcat[3 * f + 1] = wdec_s[f]
        wcat[3 * f + 2] = wgat_s[f]
    bmix = np.concatenate([_col16(inputs["L0_attn_b_in"]),
                           _col16(inputs["L0_attn_b_decay"]),
                           _col16(inputs["L0_attn_b_gate"]),
                           _col16(inputs["L0_attn_b_out"])], axis=1)
    w1p = np.zeros((2048, H1P), np.float32); w1p[:, :H1] = w1.T
    w3p = np.zeros((2048, H1P), np.float32); w3p[:, :H1] = w3.T
    w2p = np.zeros((H1P, 2048), np.float32); w2p[:H1, :] = w2.T

    shared = dict(
        wcat_st=wcat, bmix=bmix, dbias=_col16(inputs["L0_attn_decay_bias"]),
        wout_st=_wst(wout, 16, 16),
        wu_st=_wst(wu.T, 16, 64), wg0_st=_wst(wg0.T, 16, 64), wd_st=_wst(wd.T, 64, 16),
        wq_st=_wst(wq.T, 16, 16), wk_st=_wst(wk.T, 16, 4), wv_st=_wst(wv.T, 16, 4),
        wo_st=_wst(wo.T, 16, 16),
        w1_st=_wst(w1p, 16, 43), w3_st=_wst(w3p, 16, 43), w2_st=_wst(w2p, 43, 16))

    qflats = [wu.ravel(), wg0.ravel(), wd.ravel(), w1.ravel(), w3.ravel(), w2.ravel()]
    qrcp = np.zeros((1, 16), np.float32)
    qrcp[0, :6] = [np.float32(1.0) / np.float32(q.size) for q in qflats]
    qeps = np.zeros((1, 16), np.float32)
    qeps[0, :3] = 1e-8
    shared["qrcp"] = qrcp
    shared["qeps"] = qeps

    slopes = (2.0 ** (-0.5 * np.arange(1, 17))).astype(np.float32)
    prng = np.arange(128, dtype=np.int32)
    in_maps = []
    for k in range(NCORE):
        b, c = k // 4, k % 4
        g = c
        xs = np.ascontiguousarray(x[b, c * CHUNK:(c + 1) * CHUNK, :].T, dtype=np.float32)
        ns = np.repeat(-slopes[4 * g:4 * g + 4][None, :], 128, axis=0)
        zs = np.zeros((128, 4), np.float32)
        zs[:, c] = 1.0
        gi = np.zeros((128, 24), np.int32)
        for r in range(4):
            for j in range(4):
                gi[:, r * 4 + j] = (g // 2) * 4096 + j * 1024 + (g % 2) * 512 + 128 * r + prng
        for j in range(4):
            gi[:, 16 + j] = j * 1024 + 128 * g + prng
            gi[:, 20 + j] = j * 1024 + 512 + 128 * g + prng
        oi = np.zeros((128, 32), np.int32)
        for f in range(FT):
            oi[:, f] = c * 2048 + 128 * f + prng
            oi[:, 16 + f] = c * 16 + f
        qsh = np.concatenate(
            [q[k * (q.size // 8):(k + 1) * (q.size // 8)].reshape(128, -1) for q in qflats],
            axis=1).astype(np.float32)
        m = dict(shared)
        m.update(x_in=xs, negslopes=np.ascontiguousarray(ns), zsel=zs,
                 gidx=np.ascontiguousarray(gi), oidx=np.ascontiguousarray(oi),
                 qshard=np.ascontiguousarray(qsh))
        in_maps.append(m)
    return in_maps


def kernel(**inputs):
    in_maps = _host_inputs(inputs)
    nc, _ = build_program(upto=4)
    res = run_bass_kernel_spmd(nc, in_maps, core_ids=list(range(NCORE)))
    out = np.empty((B, S, D), np.float32)
    for k in range(NCORE):
        b, c = k // 4, k % 4
        out[b, c * CHUNK:(c + 1) * CHUNK, :] = res.results[k]["x_out"].T
    return out
